# version 03
# baseline (speedup 1.0000x reference)
import numpy as np
import ml_dtypes
from contextlib import ExitStack

import concourse.bass as bass
import concourse.mybir as mybir
from concourse.bass import IndirectOffsetOnAxis
from concourse.bass_utils import run_bass_kernel_spmd

F32 = mybir.dt.float32
BF16 = mybir.dt.bfloat16
I32 = mybir.dt.int32
U32 = mybir.dt.uint32
AF = mybir.ActivationFunctionType
ALU = mybir.AluOpType
AX = mybir.AxisListType

P = 128
D = 2048
DC = D // P
TSEQ = 2048
NSEQ = 2
T = TSEQ * NSEQ
NCORE = 8

GATE_OFFSET = 6.906768
ATT_SCALE = 1.0 / (128.0 ** 0.5)
BF = ml_dtypes.bfloat16


def fmv(ap):
    return ap.rearrange("(c p) t -> p c t", p=P)


class B:

    def __init__(self):
        self.nc = bass.Bass("TRN2", target_bir_lowering=False)
        self.ctx = ExitStack()
        self._semv = {}
        self.ones_row = None
        self.ones_col = None
        self.onesf_row = None

    def sb(self, name, shape, dt):
        return self.ctx.enter_context(self.nc.sbuf_tensor(name, shape, dt))

    def psum(self, name, shape, dt=F32):
        return self.ctx.enter_context(self.nc.psum_tensor(name, shape, dt))

    def sem(self, name):
        h = self.ctx.enter_context(self.nc.semaphore(name))
        self._semv[name] = 0
        return h

    def dram(self, name, shape, dt, kind=None):
        if kind:
            t = self.nc.dram_tensor(name, shape, dt, kind=kind)
        else:
            t = self.nc.dram_tensor(name, shape, dt)
        return t.ap()

    def block(self):
        return self.nc.Block(no_gpsimd_drain=True)

    def init_consts(self):
        self.ones_row = self.sb("c_ones_row", [1, P], BF16)
        self.ones_col = self.sb("c_ones_col", [P, 1], BF16)
        self.onesf_row = self.sb("c_onesf_row", [1, P], F32)
        self.onesf_col = self.sb("c_onesf_col", [P, 1], F32)
        with self.block() as block:
            @block.vector
            def _(v):
                v.memset(self.ones_row[:], 1.0)
                v.memset(self.ones_col[:], 1.0)
                v.memset(self.onesf_row[:], 1.0)
                v.memset(self.onesf_col[:], 1.0)

    def finish(self):
        self.ctx.close()
        return self.nc




def emit_mm_fm(b, name, wt_dram, x_sb, kdim, mdim, tdim, evac=None,
               tchunk=512, x_kc0=0, tok0=0, out_dram=None, bias_sb=None,
               act_fn=None, wt_f32=False, out_dt=BF16, out_tok0=0):
    nc = b.nc
    kc = kdim // P
    mc = (mdim + P - 1) // P
    ntc = (tdim + tchunk - 1) // tchunk
    with ExitStack() as ctx:
        wdt = F32 if wt_f32 else BF16
        wb = [ctx.enter_context(nc.sbuf_tensor(f"{name}_w{i}", [P, kc, P], wdt))
              for i in range(2)]
        pt = [ctx.enter_context(nc.psum_tensor(f"{name}_p{i}", [P, tchunk], F32))
              for i in range(2)]
        stage = None
        if out_dram is not None:
            stage = [ctx.enter_context(nc.sbuf_tensor(f"{name}_st{i}", [P, tchunk], out_dt))
                     for i in range(3)]
            odst = fmv(out_dram)
        s_w = b.sem(f"{name}_sw")
        s_f = b.sem(f"{name}_sf")
        s_m = b.sem(f"{name}_sm")
        s_e = b.sem(f"{name}_se")
        s_o = b.sem(f"{name}_so")
        wsrc = wt_dram.rearrange("(c p) m -> p c m", p=P)
        with b.block() as block:
            @block.gpsimd
            def _(g):
                for m in range(mc):
                    mw = min(P, mdim - m * P)
                    if m >= 2:
                        g.wait_ge(s_m, (m - 1) * ntc)
                    g.dma_start(wb[m % 2][:, :, 0:mw],
                                wsrc[:, :, m * P:m * P + mw]).then_inc(s_w, 16)

            @block.tensor
            def _(t):
                cnt = 0
                for m in range(mc):
                    mw = min(P, mdim - m * P)
                    t.wait_ge(s_w, 16 * (m + 1))
                    for tc in range(ntc):
                        t0 = tc * tchunk
                        tw = min(tchunk, tdim - t0)
                        cnt += 1
                        if cnt > 2:
                            t.wait_ge(s_e, cnt - 2)
                        for k in range(kc):
                            ins = t.matmul(pt[(cnt - 1) % 2][0:mw, 0:tw],
                                           wb[m % 2][:, k, 0:mw],
                                           x_sb[:, x_kc0 + k, tok0 + t0:tok0 + t0 + tw],
                                           start=(k == 0), stop=(k == kc - 1))
                        ins.then_inc(s_m, 1)

            @block.scalar
            def _(s):
                cnt = 0
                for m in range(mc):
                    mw = min(P, mdim - m * P)
                    for tc in range(ntc):
                        t0 = tc * tchunk
                        tw = min(tchunk, tdim - t0)
                        cnt += 1
                        s.wait_ge(s_m, cnt)
                        if out_dram is not None:
                            if cnt > 3:
                                s.wait_ge(s_o, 16 * (cnt - 3))
                            ins = s.activation(out=stage[(cnt - 1) % 3][0:mw, 0:tw],
                                               in_=pt[(cnt - 1) % 2][0:mw, 0:tw],
                                               func=(act_fn or AF.Identity),
                                               bias=(bias_sb[:, m:m + 1] if bias_sb is not None else 0.0))
                        else:
                            ins = evac(s, pt[(cnt - 1) % 2][0:mw, 0:tw], m, t0, tw)
                        ins.then_inc(s_e, 1)

            if out_dram is not None:
                @block.sync
                def _(g):
                    cnt = 0
                    for m in range(mc):
                        mw = min(P, mdim - m * P)
                        for tc in range(ntc):
                            t0 = tc * tchunk
                            tw = min(tchunk, tdim - t0)
                            cnt += 1
                            g.wait_ge(s_e, cnt)
                            g.dma_start(odst[0:mw, m, out_tok0 + t0:out_tok0 + t0 + tw],
                                        stage[(cnt - 1) % 3][0:mw, 0:tw]).then_inc(s_o, 16)
                    g.wait_ge(s_o, 16 * cnt)




def emit_ln(b, name, x_dram, xn_dram, tdim=T, eps=1e-5, x_store=None):
    nc = b.nc
    SW = 512
    nst = tdim // SW
    with ExitStack() as ctx:
        def csb(n, shape, dt):
            return ctx.enter_context(nc.sbuf_tensor(f"{name}_{n}", shape, dt))
        xs = csb("x", [P, DC, SW], F32)
        xadd = csb("xa", [P, DC, SW], F32) if x_store else None
        xbf = csb("bf", [P, DC, SW], BF16)
        xsq = csb("sq", [P, DC, SW], BF16)
        stat = csb("st", [1, 2, tdim], F32)
        rstd = csb("rs", [1, tdim], F32)
        mu = csb("mu", [1, tdim], F32)
        pbs = csb("pb", [P, 2, SW], F32)
        xno = csb("xn", [P, DC, SW], BF16)
        epsb = csb("eps", [1, 1], F32)
        pstat = ctx.enter_context(nc.psum_tensor(f"{name}_ps", [1, 2, SW], F32))
        pbc = ctx.enter_context(nc.psum_tensor(f"{name}_pbc", [P, 2, SW], F32))
        s_d = b.sem(f"{name}_sd")
        sdv = 0

        with b.block() as block:
            @block.vector
            def _(v):
                v.memset(epsb[:], float(eps))

        for i in range(nst):
            with b.block() as block:
                @block.sync
                def _(sy):
                    nonlocal sdv
                    sy.dma_start(xs[:], fmv(x_dram if not x_store else x_store[0])[:, :, i * SW:(i + 1) * SW]).then_inc(s_d, 16)
                    sdv += 16
                    if x_store:
                        for k in range(DC):
                            sy.dma_start(xbf[:, k, :],
                                         x_store[2][i * SW:(i + 1) * SW, k * P:(k + 1) * P],
                                         transpose=True).then_inc(s_d, 16)
                        sdv += 16 * DC
                    sy.wait_ge(s_d, sdv)
                if x_store:
                    @block.gpsimd
                    def _(g):
                        nonlocal sdv
                        g.dma_start(xadd[:], fmv(x_store[1])[:, :, i * SW:(i + 1) * SW]).then_inc(s_d, 16)
                        sdv += 16
                        g.wait_ge(s_d, sdv)
            if x_store:
                with b.block() as block:
                    @block.vector
                    def _(v):
                        for k in range(0, DC, 2):
                            v.tensor_tensor(out=xs[:, k, :], in0=xs[:, k, :], in1=xadd[:, k, :], op=ALU.add)
                            v.tensor_tensor(out=xs[:, k, :], in0=xs[:, k, :], in1=xbf[:, k, :], op=ALU.add)
                    @block.gpsimd
                    def _(g):
                        for k in range(1, DC, 2):
                            g.tensor_tensor(out=xs[:, k, :], in0=xs[:, k, :], in1=xadd[:, k, :], op=ALU.add)
                            g.tensor_tensor(out=xs[:, k, :], in0=xs[:, k, :], in1=xbf[:, k, :], op=ALU.add)
                with b.block() as block:
                    @block.sync
                    def _(sy):
                        nonlocal sdv
                        sy.dma_start(fmv(x_dram)[:, :, i * SW:(i + 1) * SW], xs[:]).then_inc(s_d, 16)
                        sdv += 16
                        sy.wait_ge(s_d, sdv)
            with b.block() as block:
                @block.vector
                def _(v):
                    v.tensor_copy(xbf[:], xs[:])
                @block.scalar
                def _(s):
                    s.activation(out=xsq[:], in_=xs[:], func=AF.Square)
            with b.block() as block:
                @block.tensor
                def _(t):
                    for tc in range(SW // 512):
                        sl = slice(tc * 512, tc * 512 + 512)
                        for k in range(DC):
                            t.matmul(pstat[0:1, 0, sl], b.ones_col[:], xbf[:, k, sl],
                                     start=(k == 0), stop=(k == DC - 1))
                        for k in range(DC):
                            t.matmul(pstat[0:1, 1, sl], b.ones_col[:], xsq[:, k, sl],
                                     start=(k == 0), stop=(k == DC - 1))
            with b.block() as block:
                @block.scalar
                def _(s):
                    s.copy(stat[0:1, :, i * SW:(i + 1) * SW], pstat[0:1, :, :])

        with b.block() as block:
            @block.vector
            def _(v):
                v.tensor_scalar(out=mu[:], in0=stat[0:1, 0, :], scalar1=1.0 / D, scalar2=None, op0=ALU.mult)
                v.tensor_tensor(out=rstd[:], in0=mu[:], in1=mu[:], op=ALU.mult)
        with b.block() as block:
            @block.vector
            def _(v):
                v.tensor_scalar(out=stat[0:1, 1, :], in0=stat[0:1, 1, :], scalar1=1.0 / D,
                                scalar2=None, op0=ALU.mult)
                v.tensor_tensor(out=rstd[:], in0=stat[0:1, 1, :], in1=rstd[:], op=ALU.subtract)
        with b.block() as block:
            @block.scalar
            def _(s):
                s.activation(out=rstd[:], in_=rstd[:], func=AF.Sqrt, bias=epsb[:])
        with b.block() as block:
            @block.vector
            def _(v):
                v.reciprocal(out=rstd[:], in_=rstd[:])

        for i in range(nst):
            with b.block() as block:
                @block.sync
                def _(sy):
                    nonlocal sdv
                    sy.dma_start(xs[:], fmv(x_dram)[:, :, i * SW:(i + 1) * SW]).then_inc(s_d, 16)
                    sdv += 16
                    sy.wait_ge(s_d, sdv)
                @block.tensor
                def _(t):
                    for tc in range(SW // 512):
                        sl = slice(tc * 512, tc * 512 + 512)
                        gl = slice(i * SW + tc * 512, i * SW + tc * 512 + 512)
                        t.matmul(pbc[:, 0, sl], b.onesf_row[:], mu[0:1, gl], start=True, stop=True)
                        t.matmul(pbc[:, 1, sl], b.onesf_row[:], rstd[0:1, gl], start=True, stop=True)
            with b.block() as block:
                @block.scalar
                def _(s):
                    s.copy(pbs[:], pbc[:])
            with b.block() as block:
                @block.vector
                def _(v):
                    for k in range(0, DC, 2):
                        v.tensor_tensor(out=xs[:, k, :], in0=xs[:, k, :], in1=pbs[:, 0, :], op=ALU.subtract)
                        v.tensor_tensor(out=xno[:, k, :], in0=xs[:, k, :], in1=pbs[:, 1, :], op=ALU.mult)
                @block.gpsimd
                def _(g):
                    for k in range(1, DC, 2):
                        g.tensor_tensor(out=xs[:, k, :], in0=xs[:, k, :], in1=pbs[:, 0, :], op=ALU.subtract)
                        g.tensor_tensor(out=xno[:, k, :], in0=xs[:, k, :], in1=pbs[:, 1, :], op=ALU.mult)
            with b.block() as block:
                @block.sync
                def _(sy):
                    nonlocal sdv
                    sy.dma_start(fmv(xn_dram)[:, :, i * SW:(i + 1) * SW], xno[:]).then_inc(s_d, 16)
                    sdv += 16
                    sy.wait_ge(s_d, sdv)





def emit_attn0(b, name, xn_dram, wq_dram, wk_dram, wv_dram, wg_dram,
               masks_dram, eye_dram, yhat_dram, dbg=None):
    nc = b.nc
    LNS = float(np.log(ATT_SCALE))
    EPS_SC = float(np.sqrt(1e-6 / (ATT_SCALE * ATT_SCALE)))
    ctx = ExitStack()
    with ctx:
        def csb(n, shape, dt):
            return ctx.enter_context(nc.sbuf_tensor(f"{name}_{n}", shape, dt))

        def cps(n, shape, dt=F32):
            return ctx.enter_context(nc.psum_tensor(f"{name}_{n}", shape, dt))

        xn = csb("xn", [P, DC, TSEQ], BF16)
        wgt = csb("wg", [P, DC, 8], BF16)
        wqt = csb("wq", [P, DC, 256], BF16)
        wkt = csb("wk", [P, DC, P], BF16)
        wvt = csb("wv", [P, DC, P], BF16)
        qsb = csb("q", [P, 2, TSEQ], BF16)
        ksb = csb("k", [P, TSEQ], BF16)
        vtmp = csb("vt", [P, TSEQ], BF16)
        vtok = csb("vk", [P, DC, 132], BF16)
        logg = csb("lg", [8, TSEQ], F32)
        logG = csb("lG", [8, TSEQ], F32)
        svh = csb("sv", [8, TSEQ], F32)
        scr8 = csb("s8", [8, TSEQ], F32)
        svT = csb("sT", [P, DC, 4, 8], F32)
        epT = csb("eT", [P, DC, 8], F32)
        refc = csb("rc", [8, 1], F32)
        goff = csb("go", [8, 1], F32)
        one8 = csb("o8", [8, 1], F32)
        eyeb = csb("eyb", [P, P], BF16)
        eyef = csb("eyf", [P, P], F32)
        masks = csb("mk", [P, 4, 512], mybir.dt.uint8)
        zeros = csb("z", [P, 512], BF16)
        abuf = csb("a", [P, 2, 512], BF16)
        ambuf = csb("am", [P, 2, 512], BF16)
        ncol4 = csb("ncl", [P, 4], F32)
        rcol4 = csb("rl", [P, 4], F32)
        ytok4 = csb("yk", [P, 4, P], BF16)
        ystage = csb("ys", [P, 2, TSEQ], BF16)
        dbgbuf = csb("db", [P, 4, 132], F32) if dbg is not None else None
        dbgfin = csb("df", [P, 8], F32) if dbg is not None else None

        psc = [cps(f"sc{i}", [P, 512]) for i in range(2)]
        pyy = [cps(f"y{i}", [P, 132]) for i in range(4)]
        ptr = cps("tr", [P, P], BF16)
        g8s = csb("g8s", [8, TSEQ], F32)

        s_d = b.sem(f"{name}_d")
        s_sc = b.sem(f"{name}_sc")
        s_aq = b.sem(f"{name}_aq")
        s_am = b.sem(f"{name}_am")
        s_ym = b.sem(f"{name}_ym")
        s_fv = b.sem(f"{name}_fv")
        s_tr = b.sem(f"{name}_tr")
        s_ys = b.sem(f"{name}_ys")
        s_o = b.sem(f"{name}_o")
        c = dict(d=0, sc=0, aq=0, am=0, ym=0, fv=0, tr=0, ys=0, o=0)

        with b.block() as block:
            @block.vector
            def _(v):
                v.memset(goff[:], GATE_OFFSET)
                v.memset(one8[:], 1.0)
                v.memset(zeros[:], 0.0)
                v.memset(vtok[:], 0.0)
            @block.gpsimd
            def _(g):
                g.dma_start(masks[:], masks_dram).then_inc(s_d, 16)
                g.dma_start(wgt[:], wg_dram.rearrange("(c p) m -> p c m", p=P)).then_inc(s_d, 16)
                g.dma_start(eyeb[:], eye_dram).then_inc(s_d, 16)
                g.dma_start(eyef[:], eye_dram).then_inc(s_d, 16)
                g.wait_ge(s_d, c["d"] + 64)
        c["d"] += 64

        for seq in range(NSEQ):
            tok0 = seq * TSEQ
            with b.block() as block:
                @block.sync
                def _(sy):
                    sy.dma_start(xn[:], fmv(xn_dram)[:, :, tok0:tok0 + TSEQ]).then_inc(s_d, 16)
                    sy.wait_ge(s_d, c["d"] + 16)
            c["d"] += 16

            with b.block() as block:
                @block.tensor
                def _(t):
                    for tc in range(TSEQ // 512):
                        sl = slice(tc * 512, tc * 512 + 512)
                        if tc >= 2:
                            t.wait_ge(s_aq, c["aq"] + tc - 1)
                        for k in range(DC):
                            ins = t.matmul(psc[tc % 2][0:8, :], wgt[:, k, :], xn[:, k, sl],
                                           start=(k == 0), stop=(k == DC - 1))
                        ins.then_inc(s_sc, 1)
                @block.scalar
                def _(s):
                    for tc in range(TSEQ // 512):
                        sl = slice(tc * 512, tc * 512 + 512)
                        s.wait_ge(s_sc, c["sc"] + tc + 1)
                        s.copy(g8s[:, sl], psc[tc % 2][0:8, :]).then_inc(s_aq, 1)
            c["sc"] += 4
            c["aq"] += 4
            with b.block() as block:
                @block.scalar
                def _(s):
                    s.activation(out=svh[:], in_=g8s[:], func=AF.Abs, bias=goff[:])
                @block.vector
                def _(v):
                    v.tensor_scalar(out=logg[:], in0=g8s[:], scalar1=GATE_OFFSET,
                                    scalar2=0.0, op0=ALU.add, op1=ALU.min)
            with b.block() as block:
                @block.scalar
                def _(s):
                    s.activation(out=svh[:], in_=svh[:], func=AF.Exp, scale=-1.0)
            with b.block() as block:
                @block.scalar
                def _(s):
                    s.activation(out=scr8[:], in_=svh[:], func=AF.Ln, bias=one8[:])
            with b.block() as block:
                @block.vector
                def _(v):
                    v.tensor_tensor(out=logg[:], in0=logg[:], in1=scr8[:], op=ALU.subtract)
            with b.block() as block:
                @block.vector
                def _(v):
                    v.tensor_tensor_scan(out=logG[:], data0=logg[:], data1=logg[:],
                                         initial=0.0, op0=ALU.add, op1=ALU.bypass)
            if dbg is not None:
                with b.block() as block:
                    @block.sync
                    def _(sy):
                        sy.dma_start(dbg["logG"][0:8, tok0:tok0 + TSEQ], logG[:]).then_inc(s_o, 16)
                        sy.wait_ge(s_o, c["o"] + 16)
                c["o"] += 16

            for j in range(4):
                send = 512 * (j + 1)
                with b.block() as block:
                    @block.vector
                    def _(v):
                        v.tensor_scalar(out=refc[:], in0=logG[:, 512 * j:512 * j + 1],
                                        scalar1=0.5, scalar2=LNS, op0=ALU.mult, op1=ALU.add)
                        v.tensor_copy(scr8[:, 0:256], logG[:, 0:256])
                        v.tensor_copy(scr8[:, 0:256], logG[:, 0:256])
                        v.tensor_copy(scr8[:, 0:256], logG[:, 0:256])
                        v.tensor_copy(scr8[:, 0:256], logG[:, 0:256])
                with b.block() as block:
                    @block.scalar
                    def _(s):
                        s.activation(out=svh[0:8, 0:send], in_=logG[0:8, 0:send],
                                     func=AF.Exp, scale=-0.5, bias=refc[:])
                with b.block() as block:
                    @block.tensor
                    def _(t):
                        for st in range(4 * (j + 1)):
                            if st >= 1:
                                t.wait_ge(s_ys, c["ys"] + st)
                            t.transpose(psc[1][:, 0:8], svh[0:8, st * P:(st + 1) * P],
                                        eyef[0:8, 0:8]).then_inc(s_tr, 1)
                    @block.scalar
                    def _(s):
                        for st in range(4 * (j + 1)):
                            s.wait_ge(s_tr, c["tr"] + st + 1)
                            s.copy(svT[:, st, j, :], psc[1][:, 0:8]).then_inc(s_ys, 1)
                c["tr"] += 4 * (j + 1)
                c["ys"] += 4 * (j + 1)
                with b.block() as block:
                    @block.scalar
                    def _(s):
                        for qt in range(4 * j, 4 * j + 4):
                            s.activation(out=epT[:, qt, :], in_=svT[:, qt, j, :],
                                         func=AF.Square, scale=EPS_SC)

            if dbg is not None and seq == 0:
                with b.block() as block:
                    @block.sync
                    def _(sy):
                        sy.dma_start(dbg["svT"], svT[:]).then_inc(s_o, 16)
                        sy.dma_start(dbg["epT"], epT[:]).then_inc(s_o, 16)
                        sy.wait_ge(s_o, c["o"] + 32)
                c["o"] += 32

            for h in range(8):
                with b.block() as block:
                    @block.gpsimd
                    def _(g):
                        g.dma_start(wqt[:], wq_dram.rearrange("(c p) m -> p c m", p=P)[:, :, 256 * h:256 * h + 256]).then_inc(s_d, 16)
                        g.dma_start(wkt[:], wk_dram.rearrange("(c p) m -> p c m", p=P)[:, :, P * h:P * h + P]).then_inc(s_d, 16)
                        g.dma_start(wvt[:], wv_dram.rearrange("(c p) m -> p c m", p=P)[:, :, P * h:P * h + P]).then_inc(s_d, 16)
                        g.wait_ge(s_d, c["d"] + 48)
                c["d"] += 48

                groups = []
                for qh in range(2):
                    for tc in range(4):
                        groups.append(("q", qh, tc))
                for tc in range(4):
                    groups.append(("k", 0, tc))
                for tc in range(4):
                    groups.append(("v", 0, tc))
                with b.block() as block:
                    @block.tensor
                    def _(t):
                        for i, (kind, qh, tc) in enumerate(groups):
                            sl = slice(tc * 512, tc * 512 + 512)
                            if i >= 2:
                                t.wait_ge(s_aq, c["aq"] + i - 1)
                            for k in range(DC):
                                lh = (wqt[:, k, 128 * qh:128 * qh + 128] if kind == "q"
                                      else (wkt[:, k, :] if kind == "k" else wvt[:, k, :]))
                                ins = t.matmul(psc[i % 2][:, :], lh, xn[:, k, sl],
                                               start=(k == 0), stop=(k == DC - 1))
                            ins.then_inc(s_sc, 1)
                    @block.scalar
                    def _(s):
                        for i, (kind, qh, tc) in enumerate(groups):
                            sl = slice(tc * 512, tc * 512 + 512)
                            s.wait_ge(s_sc, c["sc"] + i + 1)
                            dst = (qsb[:, qh, sl] if kind == "q"
                                   else (ksb[:, sl] if kind == "k" else vtmp[:, sl]))
                            s.copy(dst, psc[i % 2][:, :]).then_inc(s_aq, 1)
                c["sc"] += len(groups)
                c["aq"] += len(groups)

                with b.block() as block:
                    @block.tensor
                    def _(t):
                        for st in range(DC):
                            if st >= 1:
                                t.wait_ge(s_ys, c["ys"] + st)
                            t.transpose(ptr[:], vtmp[:, st * P:(st + 1) * P],
                                        eyeb[:]).then_inc(s_tr, 1)
                    @block.scalar
                    def _(s):
                        for st in range(DC):
                            s.wait_ge(s_tr, c["tr"] + st + 1)
                            s.copy(vtok[:, st, 0:P], ptr[:]).then_inc(s_ys, 1)
                    @block.vector
                    def _(v):
                        v.memset(vtok[:, :, P:P + 1], 1.0)
                c["tr"] += DC
                c["ys"] += DC
                if dbg is not None and seq == 0 and h == 0:
                    with b.block() as block:
                        @block.sync
                        def _(sy):
                            sy.dma_start(dbg["q"], qsb[:]).then_inc(s_o, 16)
                            sy.dma_start(dbg["k"], ksb[:]).then_inc(s_o, 16)
                            sy.dma_start(dbg["vtok"], vtok[:]).then_inc(s_o, 16)
                            sy.wait_ge(s_o, c["o"] + 48)
                    c["o"] += 48

                for qh in range(2):
                    for j in range(4):
                        ns = 4 * (j + 1)
                        aq0, am0, ym0, sc0 = c["aq"], c["am"], c["ym"], c["fv"]
                        diag_upto = [0] * ns
                        nmk = 0
                        for st in range(ns):
                            if st >= 4 * j:
                                nmk += 1
                            diag_upto[st] = nmk
                        with b.block() as block:
                            @block.tensor
                            def _(t):
                                def ymms(st):
                                    kmin = max(0, st - 4 * j)
                                    src = ambuf if st >= 4 * j else abuf
                                    if st >= 4 * j:
                                        t.wait_ge(s_am, c["am"] + diag_upto[st])
                                    else:
                                        t.wait_ge(s_aq, c["aq"] + st + 1)
                                    for k in range(kmin, 4):
                                        ins = t.matmul(pyy[k][:, 0:129],
                                                       src[:, st % 2, k * P:(k + 1) * P],
                                                       vtok[:, st, 0:129],
                                                       start=(st == 0), stop=(st == 4 * j + k))
                                    ins.then_inc(s_ym, 1)
                                for st in range(ns):
                                    if st >= 2:
                                        t.wait_ge(s_aq, c["aq"] + st - 1)
                                    t.matmul(psc[st % 2][:, :],
                                             ksb[:, st * P:(st + 1) * P],
                                             qsb[:, qh, j * 512:(j + 1) * 512],
                                             start=True, stop=True).then_inc(s_sc, 1)
                                    if st >= 1:
                                        ymms(st - 1)
                                ymms(ns - 1)

                            @block.scalar
                            def _(s):
                                for st in range(ns):
                                    s.wait_ge(s_sc, c["sc"] + st + 1)
                                    if st >= 2:
                                        s.wait_ge(s_ym, c["ym"] + st - 1)
                                    s.activation(out=abuf[:, st % 2, :], in_=psc[st % 2][:, :],
                                                 func=AF.Square,
                                                 scale=svT[:, st, j, h:h + 1]
                                                 ).then_inc(s_aq, 1)

                            @block.vector
                            def _(v):
                                for st in range(4 * j, ns):
                                    v.wait_ge(s_aq, c["aq"] + st + 1)
                                    if st >= 4 * j + 2:
                                        v.wait_ge(s_ym, c["ym"] + st - 1)
                                    v.select(out=ambuf[:, st % 2, :], mask=masks[:, st - 4 * j, :],
                                             on_true=abuf[:, st % 2, :], on_false=zeros[:],
                                             add_drain=True).then_inc(s_am, 1)
                        c["sc"] += ns
                        c["aq"] += ns
                        c["am"] += nmk
                        c["ym"] += ns

                        if dbg is not None and seq == 0 and h == 0 and qh == 0 and j == dbg.get("j", 0):
                            with b.block() as block:
                                @block.scalar
                                def _(s):
                                    for k in range(4):
                                        s.copy(dbgbuf[:, k, :], pyy[k][:, 0:132])
                            with b.block() as block:
                                @block.sync
                                def _(sy):
                                    sy.dma_start(dbg["pyy"], dbgbuf[:]).then_inc(s_o, 16)
                                    sy.dma_start(dbg["ab"], abuf[:, (4 * (j + 1) - 1) % 2, :]).then_inc(s_o, 16)
                                    sy.dma_start(dbg["amb"], ambuf[:, (4 * (j + 1) - 1) % 2, :]).then_inc(s_o, 16)
                                    sy.wait_ge(s_o, c["o"] + 48)
                            c["o"] += 48

                        with b.block() as block:
                            dbg_this = dbg is not None and seq == 0 and h == 0 and qh == 0 and j == dbg.get("j", 0)

                            @block.vector
                            def _(v):
                                for k in range(4):
                                    v.tensor_tensor(out=ncol4[:, k:k + 1], in0=pyy[k][:, 128:129],
                                                    in1=epT[:, 4 * j + k, h:h + 1], op=ALU.add)
                                v.tensor_copy(rcol4[:, 0:1], ncol4[:, 3:4])
                                for k in range(4):
                                    v.reciprocal(out=rcol4[:, k:k + 1], in_=ncol4[:, k:k + 1])
                                if dbg_this:
                                    v.tensor_copy(dbgfin[:, 0:4], rcol4[:])
                                    v.tensor_copy(dbgfin[:, 4:8], ncol4[:])
                                for k in range(4):
                                    v.tensor_scalar(out=ytok4[:, k, :], in0=pyy[k][:, 0:P],
                                                    scalar1=rcol4[:, k:k + 1], scalar2=None,
                                                    op0=ALU.mult).then_inc(s_fv, 1)
                            @block.tensor
                            def _(t):
                                for k in range(4):
                                    t.wait_ge(s_fv, c["fv"] + k + 1)
                                    if k >= 1:
                                        t.wait_ge(s_ys, c["ys"] + k)
                                    t.transpose(ptr[:], ytok4[:, k, :], eyeb[:]).then_inc(s_tr, 1)
                            @block.scalar
                            def _(s):
                                for k in range(4):
                                    qt = 4 * j + k
                                    s.wait_ge(s_tr, c["tr"] + k + 1)
                                    s.copy(ystage[:, qh, qt * P:(qt + 1) * P],
                                           ptr[:]).then_inc(s_ys, 1)
                        c["fv"] += 4
                        c["tr"] += 4
                        c["ys"] += 4
                        if dbg is not None and seq == 0 and h == 0 and qh == 0 and j == dbg.get("j", 0):
                            with b.block() as block:
                                @block.sync
                                def _(sy):
                                    sy.dma_start(dbg["fin"], dbgfin[:]).then_inc(s_o, 16)
                                    sy.wait_ge(s_o, c["o"] + 16)
                            c["o"] += 16

                with b.block() as block:
                    @block.sync
                    def _(sy):
                        for qh in range(2):
                            sy.dma_start(fmv(yhat_dram)[:, 2 * h + qh, tok0:tok0 + TSEQ],
                                         ystage[:, qh, :]).then_inc(s_o, 16)
                        sy.wait_ge(s_o, c["o"] + 32)
                c["o"] += 32




def emit_load_fm(b, name, src_dram, dst_sb, kc, tdim):
    s_d = b.sem(f"{name}_ld")
    with b.block() as block:
        @block.sync
        def _(sy):
            sy.dma_start(dst_sb[:, 0:kc, 0:tdim],
                         src_dram.rearrange("(c p) t -> p c t", p=P)).then_inc(s_d, 16)
            sy.wait_ge(s_d, 16)




def emit_ffn0(b, name, xn_dram, wgt_dram, bias_dram, iota_dram,
              w1t_dram, w3t_dram, w2t_dram, out_dram):
    nc = b.nc
    HT = T // 2
    NTT = HT // P
    ctx = ExitStack()
    with ctx:
        def csb(n, shape, dt):
            return ctx.enter_context(nc.sbuf_tensor(f"{name}_{n}", shape, dt))

        def cps(n, shape, dt=F32):
            return ctx.enter_context(nc.psum_tensor(f"{name}_{n}", shape, dt))

        xn = csb("xn", [P, DC, HT], BF16)
        acc = csb("acc", [P, NTT, D], BF16)
        gbuf = csb("g", [P, 8, HT], BF16)
        wgt = csb("wg", [P, DC, 8], BF16)
        w13 = [csb(f"w13_{i}", [P, DC, 2, P], BF16) for i in range(2)]
        w2c = [csb(f"w2_{i}", [P, 8, 512], BF16) for i in range(2)]
        s1st = csb("s1", [P, 2, 512], BF16)
        biasb = csb("bb", [P, 8], F32)
        iotab = csb("io", [P, 8], F32)
        lg = csb("lg", [P, NTT, 8], F32)
        mx8 = csb("mx", [P, NTT, 8], F32)
        mi8 = csb("mi", [P, NTT, 8], U32)
        e12 = csb("e12", [P, NTT, 2], F32)
        oh1 = csb("oh1", [P, NTT, 8], F32)
        oh2 = csb("oh2", [P, NTT, 8], F32)
        wv = csb("wv", [P, NTT, 2], F32)
        ba12 = csb("ba", [P, NTT, 2], F32)
        rs = csb("rs", [P, NTT, 1], F32)
        ew = csb("ew", [P, NTT, 8], F32)
        ps = [cps(f"p{i}", [P, 512]) for i in range(4)]

        s_d = b.sem(f"{name}_d")
        s_m = b.sem(f"{name}_m")
        s_a = b.sem(f"{name}_a")
        s_v = b.sem(f"{name}_v")
        s_o = b.sem(f"{name}_o")
        c = dict(d=0, m=0, a=0, v=0, o=0)

        with b.block() as block:
            @block.gpsimd
            def _(g):
                g.dma_start(wgt[:], wgt_dram.rearrange("(c p) m -> p c m", p=P)).then_inc(s_d, 16)
                g.dma_start(biasb[0:1, :], bias_dram).then_inc(s_d, 16)
                g.dma_start(iotab[0:1, :], iota_dram).then_inc(s_d, 16)
                g.wait_ge(s_d, c["d"] + 48)
        c["d"] += 48
        with b.block() as block:
            @block.tensor
            def _(t):
                t.matmul(ps[0][:, 0:8], b.onesf_row[:], biasb[0:1, :], start=True, stop=True)
                t.matmul(ps[1][:, 0:8], b.onesf_row[:], iotab[0:1, :], start=True, stop=True)
        with b.block() as block:
            @block.scalar
            def _(s):
                s.copy(biasb[:], ps[0][:, 0:8])
                s.copy(iotab[:], ps[1][:, 0:8])
                s.copy(s1st[:, 0, :], s1st[:, 0, :])
                s.copy(s1st[:, 0, :], s1st[:, 0, :])

        for half in range(2):
            tok0 = half * HT
            with b.block() as block:
                @block.sync
                def _(sy):
                    sy.dma_start(xn[:], fmv(xn_dram)[:, :, tok0:tok0 + HT]).then_inc(s_d, 16)
                    sy.wait_ge(s_d, c["d"] + 16)
                @block.vector
                def _(v):
                    v.memset(acc[:], 0.0)
            c["d"] += 16

            with b.block() as block:
                @block.tensor
                def _(t):
                    for tt in range(NTT):
                        if tt >= 2:
                            t.wait_ge(s_a, c["a"] + tt - 1)
                        for k in range(DC):
                            ins = t.matmul(ps[tt % 2][0:P, 0:8],
                                           xn[:, k, tt * P:(tt + 1) * P],
                                           wgt[:, k, :], start=(k == 0), stop=(k == DC - 1))
                        ins.then_inc(s_m, 1)
                @block.scalar
                def _(s):
                    for tt in range(NTT):
                        s.wait_ge(s_m, c["m"] + tt + 1)
                        s.activation(out=lg[:, tt, :], in_=ps[tt % 2][0:P, 0:8],
                                     func=AF.Sigmoid).then_inc(s_a, 1)
            c["m"] += NTT
            c["a"] += NTT
            with b.block() as block:
                @block.vector
                def _(v):
                    for tt in range(NTT):
                        v.tensor_tensor(out=lg[:, tt, :], in0=lg[:, tt, :], in1=biasb[:], op=ALU.add)
                    for tt in range(NTT):
                        v.max(out=mx8[:, tt, :], in_=lg[:, tt, :])
            with b.block() as block:
                @block.vector
                def _(v):
                    for tt in range(NTT):
                        v.max_index(out=mi8[:, tt, :], in_max=mx8[:, tt, :], in_values=lg[:, tt, :])
            with b.block() as block:
                @block.vector
                def _(v):
                    for tt in range(NTT):
                        v.tensor_copy(e12[:, tt, :], mi8[:, tt, 0:2])
                    for tt in range(NTT):
                        v.tensor_scalar(out=oh1[:, tt, :], in0=iotab[:], scalar1=e12[:, tt, 0:1],
                                        scalar2=None, op0=ALU.is_equal)
                    for tt in range(NTT):
                        v.tensor_scalar(out=oh2[:, tt, :], in0=iotab[:], scalar1=e12[:, tt, 1:2],
                                        scalar2=None, op0=ALU.is_equal)
                    for tt in range(NTT):
                        v.tensor_tensor(out=ew[:, tt, :], in0=oh1[:, tt, :], in1=biasb[:], op=ALU.mult)
                    for tt in range(NTT):
                        v.reduce_sum(out=ba12[:, tt, 0:1], in_=ew[:, tt, :], axis=AX.X)
                    for tt in range(NTT):
                        v.tensor_tensor(out=ew[:, tt, :], in0=oh2[:, tt, :], in1=biasb[:], op=ALU.mult)
                    for tt in range(NTT):
                        v.reduce_sum(out=ba12[:, tt, 1:2], in_=ew[:, tt, :], axis=AX.X)
                    for tt in range(NTT):
                        v.tensor_tensor(out=wv[:, tt, :], in0=mx8[:, tt, 0:2], in1=ba12[:, tt, :],
                                        op=ALU.subtract)
                    for tt in range(NTT):
                        v.reduce_sum(out=rs[:, tt, 0:1], in_=wv[:, tt, :], axis=AX.X)
                    for tt in range(NTT):
                        v.reciprocal(out=rs[:, tt, 0:1], in_=rs[:, tt, 0:1])
                    for tt in range(NTT):
                        v.tensor_scalar(out=wv[:, tt, :], in0=wv[:, tt, :], scalar1=rs[:, tt, 0:1],
                                        scalar2=None, op0=ALU.mult)
                    for tt in range(NTT):
                        v.tensor_scalar(out=ew[:, tt, :], in0=oh1[:, tt, :], scalar1=wv[:, tt, 0:1],
                                        scalar2=None, op0=ALU.mult)
                    for tt in range(NTT):
                        v.scalar_tensor_tensor(out=ew[:, tt, :], in0=oh2[:, tt, :],
                                               scalar=wv[:, tt, 1:2], in1=ew[:, tt, :],
                                               op0=ALU.mult, op1=ALU.add)

            for e in range(8):
                with b.block() as block:
                    @block.gpsimd
                    def _(g):
                        for fc in range(8):
                            if fc >= 2:
                                g.wait_ge(s_m, c["m"] + 4 * (fc - 1))
                            g.dma_start(w13[fc % 2][:, :, 0, :],
                                        w1t_dram[e].rearrange("(c p) m -> p c m", p=P)[:, :, fc * P:(fc + 1) * P]).then_inc(s_d, 16)
                            g.dma_start(w13[fc % 2][:, :, 1, :],
                                        w3t_dram[e].rearrange("(c p) m -> p c m", p=P)[:, :, fc * P:(fc + 1) * P]).then_inc(s_d, 16)
                    @block.tensor
                    def _(t):
                        n = 0
                        for fc in range(8):
                            t.wait_ge(s_d, c["d"] + 32 * (fc + 1))
                            for tc in range(4):
                                sl = slice(tc * 512, tc * 512 + 512)
                                if n >= 2:
                                    t.wait_ge(s_v, c["v"] + n - 1)
                                for k in range(DC):
                                    t.matmul(ps[2 * (n % 2)][:, :], w13[fc % 2][:, k, 0, :],
                                             xn[:, k, sl], start=(k == 0), stop=(k == DC - 1))
                                for k in range(DC):
                                    ins = t.matmul(ps[2 * (n % 2) + 1][:, :], w13[fc % 2][:, k, 1, :],
                                                   xn[:, k, sl], start=(k == 0), stop=(k == DC - 1))
                                ins.then_inc(s_m, 1)
                                n += 1
                    @block.scalar
                    def _(s):
                        n = 0
                        for fc in range(8):
                            for tc in range(4):
                                n += 1
                                s.wait_ge(s_m, c["m"] + n)
                                if n >= 3:
                                    s.wait_ge(s_v, c["v"] + n - 2)
                                s.activation(out=s1st[:, (n - 1) % 2, :], in_=ps[2 * ((n - 1) % 2)][:, :],
                                             func=AF.Silu).then_inc(s_a, 1)
                    @block.vector
                    def _(v):
                        n = 0
                        for fc in range(8):
                            for tc in range(4):
                                sl = slice(tc * 512, tc * 512 + 512)
                                n += 1
                                v.wait_ge(s_a, c["a"] + n)
                                v.tensor_tensor(out=gbuf[:, fc, sl], in0=s1st[:, (n - 1) % 2, :],
                                                in1=ps[2 * ((n - 1) % 2) + 1][:, :],
                                                op=ALU.mult).then_inc(s_v, 1)
                c["d"] += 32 * 8
                c["m"] += 32
                c["a"] += 32
                c["v"] += 32

                with b.block() as block:
                    @block.gpsimd
                    def _(g):
                        for hc in range(4):
                            if hc >= 2:
                                g.wait_ge(s_m, c["m"] + NTT * (hc - 1))
                            g.dma_start(w2c[hc % 2][:],
                                        w2t_dram[e].rearrange("(c p) m -> p c m", p=P)[:, :, hc * 512:(hc + 1) * 512]).then_inc(s_d, 16)
                    @block.tensor
                    def _(t):
                        n = 0
                        for hc in range(4):
                            t.wait_ge(s_d, c["d"] + 16 * (hc + 1))
                            for tt in range(NTT):
                                if n >= 2:
                                    t.wait_ge(s_v, c["v"] + n - 1)
                                for fc in range(8):
                                    ins = t.matmul(ps[n % 2][:, :], gbuf[:, fc, tt * P:(tt + 1) * P],
                                                   w2c[hc % 2][:, fc, :], start=(fc == 0), stop=(fc == 7))
                                ins.then_inc(s_m, 1)
                                n += 1
                    @block.vector
                    def _(v):
                        n = 0
                        for hc in range(4):
                            for tt in range(NTT):
                                n += 1
                                v.wait_ge(s_m, c["m"] + n)
                                v.scalar_tensor_tensor(out=acc[:, tt, hc * 512:(hc + 1) * 512],
                                                       in0=ps[(n - 1) % 2][:, :],
                                                       scalar=ew[:, tt, e:e + 1],
                                                       in1=acc[:, tt, hc * 512:(hc + 1) * 512],
                                                       op0=ALU.mult, op1=ALU.add).then_inc(s_v, 1)
                c["d"] += 16 * 4
                c["m"] += 4 * NTT
                c["v"] += 4 * NTT

            with b.block() as block:
                @block.sync
                def _(sy):
                    sy.dma_start(out_dram.rearrange("(n p) h -> p n h", p=P)[:, half * NTT:(half + 1) * NTT, :],
                                 acc[:]).then_inc(s_o, 16)
                    sy.wait_ge(s_o, c["o"] + 16)
            c["o"] += 16




def emit_mm_tok(b, name, wt_dram, x_sb, kdim, mdim, tdim, out_dram,
                act_fn=None, out_dt=BF16, mchunk=512):
    nc = b.nc
    kc = kdim // P
    moc = mdim // mchunk
    ntt = tdim // P
    with ExitStack() as ctx:
        wb = [ctx.enter_context(nc.sbuf_tensor(f"{name}_w{i}", [P, kc, mchunk], BF16))
              for i in range(2)]
        pt = [ctx.enter_context(nc.psum_tensor(f"{name}_p{i}", [P, mchunk], F32))
              for i in range(2)]
        stage = [ctx.enter_context(nc.sbuf_tensor(f"{name}_st{i}", [P, mchunk], out_dt))
                 for i in range(3)]
        s_w = b.sem(f"{name}_sw")
        s_m = b.sem(f"{name}_sm")
        s_e = b.sem(f"{name}_se")
        s_o = b.sem(f"{name}_so")
        wsrc = wt_dram.rearrange("(c p) m -> p c m", p=P)
        odst = out_dram.rearrange("(n p) m -> p n m", p=P)
        with b.block() as block:
            @block.gpsimd
            def _(g):
                for oc in range(moc):
                    if oc >= 2:
                        g.wait_ge(s_m, (oc - 1) * ntt)
                    g.dma_start(wb[oc % 2][:],
                                wsrc[:, :, oc * mchunk:(oc + 1) * mchunk]).then_inc(s_w, 16)

            @block.tensor
            def _(t):
                cnt = 0
                for oc in range(moc):
                    t.wait_ge(s_w, 16 * (oc + 1))
                    for tt in range(ntt):
                        cnt += 1
                        if cnt > 2:
                            t.wait_ge(s_e, cnt - 2)
                        for k in range(kc):
                            ins = t.matmul(pt[(cnt - 1) % 2][:, :],
                                           x_sb[:, k, tt * P:(tt + 1) * P],
                                           wb[oc % 2][:, k, :],
                                           start=(k == 0), stop=(k == kc - 1))
                        ins.then_inc(s_m, 1)

            @block.scalar
            def _(s):
                cnt = 0
                for oc in range(moc):
                    for tt in range(ntt):
                        cnt += 1
                        s.wait_ge(s_m, cnt)
                        if cnt > 3:
                            s.wait_ge(s_o, 16 * (cnt - 3))
                        s.activation(out=stage[(cnt - 1) % 3][:, :],
                                     in_=pt[(cnt - 1) % 2][:, :],
                                     func=(act_fn or AF.Identity)).then_inc(s_e, 1)

            @block.sync
            def _(sy):
                cnt = 0
                for oc in range(moc):
                    for tt in range(ntt):
                        cnt += 1
                        sy.wait_ge(s_e, cnt)
                        sy.dma_start(odst[:, tt, oc * mchunk:(oc + 1) * mchunk],
                                     stage[(cnt - 1) % 3][:, :]).then_inc(s_o, 16)
                sy.wait_ge(s_o, 16 * cnt)




def emit_inproj(b, name, xn_dram, wint_dram, gate_dram, xbc_dram, dtraw_dram):
    nc = b.nc
    with ExitStack() as ctx:
        xnsb = ctx.enter_context(nc.sbuf_tensor(f"{name}_xn", [P, DC, T], BF16))
        s_d = b.sem(f"{name}_ld")
        with b.block() as block:
            @block.sync
            def _(sy):
                sy.dma_start(xnsb[:], fmv(xn_dram)).then_inc(s_d, 16)
                sy.wait_ge(s_d, 16)
        emit_mm_tok(b, f"{name}_g", wint_dram[:, 0:4096], xnsb, D, 4096, T, gate_dram)
        emit_mm_fm(b, f"{name}_x", wint_dram[:, 4096:8448], xnsb, D, 4352, T,
                   out_dram=xbc_dram)
        emit_mm_fm(b, f"{name}_d", wint_dram[:, 8448:8512], xnsb, D, 64, T,
                   out_dram=dtraw_dram, out_dt=F32)




def emit_conv(b, name, xbc_dram, convw_dram, convb_dram, eye_dram,
              xtok_dram, bcfm_dram):
    nc = b.nc
    NCH = 34
    ctx = ExitStack()
    with ctx:
        def csb(n, shape, dt):
            return ctx.enter_context(nc.sbuf_tensor(f"{name}_{n}", shape, dt))
        xp = csb("xp", [P, NCH, TSEQ + 3], BF16)
        cw = csb("cw", [P, NCH, 4], F32)
        cb = csb("cb", [P, NCH], F32)
        t1 = csb("t1", [P, TSEQ], F32)
        t2 = csb("t2", [P, TSEQ], F32)
        acc = csb("ac", [P, TSEQ], F32)
        sil = csb("si", [P, TSEQ], BF16)
        eyeb = csb("ey", [P, P], BF16)
        stg = csb("sg", [P, 2, P], BF16)
        ptr = ctx.enter_context(nc.psum_tensor(f"{name}_tr", [P, P], BF16))
        s_d = b.sem(f"{name}_d")
        s_t = b.sem(f"{name}_t")
        s_a = b.sem(f"{name}_a")
        s_o = b.sem(f"{name}_o")
        c = dict(d=0, t=0, a=0, o=0)
        with b.block() as block:
            @block.gpsimd
            def _(g):
                g.dma_start(cw[:], convw_dram).then_inc(s_d, 16)
                g.dma_start(cb[:], convb_dram).then_inc(s_d, 16)
                g.dma_start(eyeb[:], eye_dram).then_inc(s_d, 16)
                g.wait_ge(s_d, 48)
        c["d"] = 48
        xsrc = xbc_dram.rearrange("(c p) t -> p c t", p=P)
        xdst = xtok_dram.rearrange("(n p) m -> p n m", p=P)
        bdst = bcfm_dram.rearrange("(c p) t -> p c t", p=P)
        for seq in range(NSEQ):
            tok0 = seq * TSEQ
            with b.block() as block:
                @block.vector
                def _(v):
                    v.memset(xp[:, :, 0:3], 0.0)
                @block.sync
                def _(sy):
                    sy.dma_start(xp[:, :, 3:3 + TSEQ],
                                 xsrc[:, :, tok0:tok0 + TSEQ]).then_inc(s_d, 16)
                    sy.wait_ge(s_d, c["d"] + 16)
            c["d"] += 16
            for ch in range(NCH):
                with b.block() as block:
                    @block.vector
                    def _(v):
                        v.tensor_scalar(out=acc[:], in0=xp[:, ch, 3:3 + TSEQ],
                                        scalar1=cw[:, ch, 3:4], scalar2=cb[:, ch:ch + 1],
                                        op0=ALU.mult, op1=ALU.add)
                        v.tensor_scalar(out=t1[:], in0=xp[:, ch, 1:1 + TSEQ],
                                        scalar1=cw[:, ch, 1:2], scalar2=None, op0=ALU.mult)
                    @block.gpsimd
                    def _(g):
                        g.tensor_scalar(out=t2[:], in0=xp[:, ch, 2:2 + TSEQ],
                                        scalar1=cw[:, ch, 2:3], scalar2=None, op0=ALU.mult)
                with b.block() as block:
                    @block.vector
                    def _(v):
                        v.scalar_tensor_tensor(out=t1[:], in0=xp[:, ch, 0:TSEQ],
                                               scalar=cw[:, ch, 0:1], in1=t1[:],
                                               op0=ALU.mult, op1=ALU.add)
                    @block.gpsimd
                    def _(g):
                        g.tensor_tensor(out=acc[:], in0=acc[:], in1=t2[:], op=ALU.add)
                with b.block() as block:
                    @block.vector
                    def _(v):
                        v.tensor_tensor(out=acc[:], in0=acc[:], in1=t1[:], op=ALU.add)
                with b.block() as block:
                    @block.scalar
                    def _(s):
                        s.activation(out=sil[:], in_=acc[:], func=AF.Silu)
                if ch < 32:
                    with b.block() as block:
                        @block.tensor
                        def _(t):
                            for st in range(DC):
                                if st >= 1:
                                    t.wait_ge(s_a, c["a"] + st)
                                t.transpose(ptr[:], sil[:, st * P:(st + 1) * P],
                                            eyeb[:]).then_inc(s_t, 1)
                        @block.scalar
                        def _(s):
                            for st in range(DC):
                                s.wait_ge(s_t, c["t"] + st + 1)
                                if st >= 2:
                                    s.wait_ge(s_o, c["o"] + 16 * (st - 1))
                                s.copy(stg[:, st % 2, :], ptr[:]).then_inc(s_a, 1)
                        @block.sync
                        def _(sy):
                            for st in range(DC):
                                sy.wait_ge(s_a, c["a"] + st + 1)
                                sy.dma_start(xdst[:, seq * DC + st, ch * P:(ch + 1) * P],
                                             stg[:, st % 2, :]).then_inc(s_o, 16)
                            sy.wait_ge(s_o, c["o"] + 16 * DC)
                    c["t"] += DC
                    c["a"] += DC
                    c["o"] += 16 * DC
                else:
                    with b.block() as block:
                        @block.sync
                        def _(sy):
                            sy.dma_start(bdst[:, ch - 32, tok0:tok0 + TSEQ],
                                         sil[:]).then_inc(s_o, 16)
                            sy.wait_ge(s_o, c["o"] + 16)
                    c["o"] += 16




def emit_ssm(b, name, xtok_dram, bcfm_dram, dtraw_dram, a_dram, dtb_dram,
             dv_dram, masks_dram, eye_dram, ytok_dram):
    nc = b.nc
    NH, HP = 64, 64
    NCH = DC
    ctx = ExitStack()
    with ctx:
        def csb(n, shape, dt):
            return ctx.enter_context(nc.sbuf_tensor(f"{name}_{n}", shape, dt))

        def cps(n, shape, dt=F32):
            return ctx.enter_context(nc.psum_tensor(f"{name}_{n}", shape, dt))

        xt = csb("xt", [P, NCH, 4096], BF16)
        bc = csb("bc", [P, 2, TSEQ], BF16)
        dt_ = csb("dt", [NH, TSEQ], F32)
        t1_ = csb("t1", [NH, TSEQ], F32)
        dacs = csb("dc", [NH, TSEQ], F32)
        dlast = csb("dl", [NH, NCH], F32)
        cde = csb("cd", [NH, NCH], F32)
        abuf = csb("ab", [NH, 1], F32)
        dtbb = csb("db", [NH, 1], F32)
        one64 = csb("o64", [NH, 1], F32)
        ndacsT = csb("nT", [P, NCH, NH], F32)
        eT = csb("eT", [P, NCH, NH], F32)
        dtT = csb("tT", [P, NCH, NH], F32)
        wtT = csb("wT", [P, NCH, NH], F32)
        eyeb = csb("ey", [P, P], BF16)
        eyef = csb("eyf", [P, P], F32)
        mask1 = csb("mk", [P, P], mybir.dt.uint8)
        dvb = csb("dv", [1, NH], F32)
        dbc = csb("dbc", [P, NH], F32)
        ddiag = csb("ddg", [P, NH, P], BF16)
        sbf = csb("sbf", [P, NH, HP], BF16)
        btok = csb("bt", [P, P], BF16)
        cbm = csb("cbm", [P, P], BF16)
        cdb = csb("cdb", [P, NH], F32)
        srow = csb("sr", [1, 2, P], F32)
        ebuf = csb("eb", [P, 2, P], BF16)
        embuf = csb("em", [P, 2, P], BF16)
        wbuf = csb("wb", [P, 2, P], BF16)
        btil = csb("bl", [P, 2, P], BF16)
        cdey = csb("ce", [P, 2, P], BF16)
        yibuf = csb("yi", [P, 2, HP], BF16)
        zeros = csb("z", [P, P], BF16)

        pdb = [cps(f"db{i}", [P, P]) for i in range(2)]
        pya = [cps(f"ya{i}", [P, HP]) for i in range(2)]
        pyb = [cps(f"yb{i}", [P, HP]) for i in range(2)]
        psc = [cps(f"sc{i}", [P, HP]) for i in range(2)]

        s_d = b.sem(f"{name}_d")
        s_pe = b.sem(f"{name}_pe")
        s_ac = b.sem(f"{name}_ac")
        s_dv = b.sem(f"{name}_dv")
        s_o = b.sem(f"{name}_o")
        c = dict(d=0, pe=0, ac=0, dv=0, o=0)

        with b.block() as block:
            @block.gpsimd
            def _(g):
                g.dma_start(abuf[:], a_dram).then_inc(s_d, 16)
                g.dma_start(dtbb[:], dtb_dram).then_inc(s_d, 16)
                g.dma_start(dvb[:], dv_dram).then_inc(s_d, 16)
                g.dma_start(eyeb[:], eye_dram).then_inc(s_d, 16)
                g.dma_start(eyef[:], eye_dram).then_inc(s_d, 16)
                g.dma_start(mask1[:], masks_dram[:, 0, 0:P]).then_inc(s_d, 16)
                g.wait_ge(s_d, 96)
            @block.vector
            def _(v):
                v.memset(one64[:], 1.0)
                v.memset(zeros[:], 0.0)
        c["d"] = 96
        with b.block() as block:
            @block.tensor
            def _(t):
                t.matmul(pdb[0][:, 0:NH], b.onesf_row[:], dvb[:], start=True, stop=True)
        with b.block() as block:
            @block.scalar
            def _(s):
                s.copy(dbc[:], pdb[0][:, 0:NH])
        with b.block() as block:
            @block.vector
            def _(v):
                for h in range(NH):
                    v.tensor_scalar(out=ddiag[:, h, :], in0=eyeb[:], scalar1=dbc[:, h:h + 1],
                                    scalar2=None, op0=ALU.mult)

        for seq in range(NSEQ):
            tok0 = seq * TSEQ
            with b.block() as block:
                @block.sync
                def _(sy):
                    sy.dma_start(xt[:], xtok_dram.rearrange("(n p) m -> p n m", p=P)[:, seq * NCH:(seq + 1) * NCH, :]).then_inc(s_d, 16)
                    sy.dma_start(bc[:], bcfm_dram.rearrange("(c p) t -> p c t", p=P)[:, :, tok0:tok0 + TSEQ]).then_inc(s_d, 16)
                    sy.dma_start(dt_[:], dtraw_dram.rearrange("(c p) t -> p c t", p=P)[0:NH, 0, tok0:tok0 + TSEQ]).then_inc(s_d, 16)
                    sy.wait_ge(s_d, c["d"] + 48)
                @block.vector
                def _(v):
                    v.memset(sbf[:], 0.0)
            c["d"] += 48

            with b.block() as block:
                @block.vector
                def _(v):
                    v.tensor_scalar(out=dt_[:], in0=dt_[:], scalar1=dtbb[:], scalar2=None, op0=ALU.add)
            with b.block() as block:
                @block.scalar
                def _(s):
                    s.activation(out=t1_[:], in_=dt_[:], func=AF.Abs)
            with b.block() as block:
                @block.scalar
                def _(s):
                    s.activation(out=t1_[:], in_=t1_[:], func=AF.Exp, scale=-1.0)
            with b.block() as block:
                @block.scalar
                def _(s):
                    s.activation(out=t1_[:], in_=t1_[:], func=AF.Ln, bias=one64[:])
            with b.block() as block:
                @block.vector
                def _(v):
                    v.tensor_scalar(out=dt_[:], in0=dt_[:], scalar1=0.0, scalar2=None, op0=ALU.max)
            with b.block() as block:
                @block.vector
                def _(v):
                    v.tensor_tensor(out=dt_[:], in0=dt_[:], in1=t1_[:], op=ALU.add)
            with b.block() as block:
                @block.vector
                def _(v):
                    v.tensor_scalar(out=t1_[:], in0=dt_[:], scalar1=abuf[:], scalar2=None, op0=ALU.mult)
            with b.block() as block:
                @block.vector
                def _(v):
                    for ch in range(NCH):
                        v.tensor_tensor_scan(out=dacs[:, ch * P:(ch + 1) * P],
                                             data0=t1_[:, ch * P:(ch + 1) * P],
                                             data1=t1_[:, ch * P:(ch + 1) * P],
                                             initial=0.0, op0=ALU.add, op1=ALU.bypass)
            with b.block() as block:
                @block.vector
                def _(v):
                    v.tensor_copy(dlast[:], dacs[:].rearrange("h (c p) -> h c p", p=P)[:, :, P - 1:P])
                    v.tensor_copy(t1_[:, 0:512], dacs[:, 0:512])
                    v.tensor_copy(t1_[:, 0:512], dacs[:, 0:512])
                    v.tensor_copy(t1_[:, 0:512], dacs[:, 0:512])
                    v.tensor_copy(t1_[:, 0:512], dacs[:, 0:512])
            with b.block() as block:
                @block.scalar
                def _(s):
                    for ch in range(NCH):
                        s.activation(out=t1_[:, ch * P:(ch + 1) * P],
                                     in_=dacs[:, ch * P:(ch + 1) * P],
                                     func=AF.Exp, scale=-1.0, bias=dlast[:, ch:ch + 1])
            with b.block() as block:
                @block.scalar
                def _(s):
                    s.activation(out=cde[:], in_=dlast[:], func=AF.Exp)
                @block.vector
                def _(v):
                    v.tensor_tensor(out=t1_[:], in0=t1_[:], in1=dt_[:], op=ALU.mult)
            with b.block() as block:
                @block.tensor
                def _(t):
                    for ch in range(NCH):
                        for which in range(3):
                            if not (ch == 0 and which == 0):
                                t.wait_ge(s_ac, c["ac"] + 3 * ch + which)
                            src = [dacs, dt_, t1_][which]
                            t.transpose(pdb[0][:, 0:NH], src[:, ch * P:(ch + 1) * P],
                                        eyef[0:NH, 0:NH]).then_inc(s_pe, 1)

                @block.scalar
                def _(s):
                    for ch in range(NCH):
                        for which in range(3):
                            s.wait_ge(s_pe, c["pe"] + 3 * ch + which + 1)
                            if which == 0:
                                s.activation(out=ndacsT[:, ch, :], in_=pdb[0][:, 0:NH],
                                             func=AF.Identity, scale=-1.0).then_inc(s_ac, 1)
                            else:
                                dst = [None, dtT, wtT][which]
                                s.copy(dst[:, ch, :], pdb[0][:, 0:NH]).then_inc(s_ac, 1)
            c["pe"] += 3 * NCH
            c["ac"] += 3 * NCH
            with b.block() as block:
                @block.scalar
                def _(s):
                    s.activation(out=eT[:], in_=ndacsT[:], func=AF.Exp, scale=-1.0)

            for ch in range(NCH):
                with b.block() as block:
                    @block.tensor
                    def _(t):
                        tsl = slice(ch * P, (ch + 1) * P)
                        t.transpose(pdb[0].bitcast(BF16)[:, 0:P], bc[:, 0, tsl], eyeb[:]).then_inc(s_pe, 1)
                        t.wait_ge(s_ac, c["ac"] + 1)
                        t.matmul(pdb[0][:, :], bc[:, 0, tsl], bc[:, 1, tsl],
                                 start=True, stop=True).then_inc(s_pe, 1)
                        t.transpose(psc[0][0:1, 0:NH], cde[:, ch:ch + 1],
                                    eyef[0:NH, 0:NH]).then_inc(s_pe, 1)
                        t.wait_ge(s_ac, c["ac"] + 2)
                        t.matmul(psc[1][:, 0:NH], b.onesf_row[:], srow[0:1, 0, 0:NH],
                                 start=True, stop=True).then_inc(s_pe, 1)
                    @block.scalar
                    def _(s):
                        s.wait_ge(s_pe, c["pe"] + 1)
                        s.copy(btok[:], pdb[0].bitcast(BF16)[:, 0:P]).then_inc(s_ac, 1)
                        s.wait_ge(s_pe, c["pe"] + 2)
                        s.copy(srow[0:1, 0, 0:NH], psc[0][0:1, 0:NH]).then_inc(s_ac, 1)
                        s.wait_ge(s_pe, c["pe"] + 4)
                        s.copy(cdb[:], psc[1][:, 0:NH]).then_inc(s_ac, 1)
                        s.copy(eT[:, 0, :], eT[:, 0, :])
                        s.copy(eT[:, 0, :], eT[:, 0, :])
                        s.copy(eT[:, 0, :], eT[:, 0, :])
                    @block.vector
                    def _(v):
                        v.wait_ge(s_pe, c["pe"] + 2)
                        v.select(out=cbm[:], mask=mask1[:], on_true=pdb[0][:, :],
                                 on_false=zeros[:], add_drain=True).then_inc(s_dv, 1)
                c["pe"] += 4
                c["ac"] += 3
                c["dv"] += 1

                pe0, ac0, dv0 = c["pe"], c["ac"], c["dv"]
                with b.block() as block:
                    @block.tensor
                    def _(t):
                        for h in range(NH):
                            if h >= 2:
                                t.wait_ge(s_ac, ac0 + 4 * h - 4)
                            t.transpose(pdb[h % 2][0:1, 0:P], ndacsT[:, ch, h:h + 1],
                                        eyef[:]).then_inc(s_pe, 1)
                            t.wait_ge(s_ac, ac0 + 4 * h + 1)
                            t.matmul(pdb[h % 2][:, :], b.onesf_row[:], srow[0:1, h % 2, :],
                                     start=True, stop=True).then_inc(s_pe, 1)
                            t.wait_ge(s_dv, dv0 + 5 * h + 4)
                            xsl = xt[:, ch, HP * h:HP * h + HP]
                            t.matmul(pya[h % 2][:, 0:HP], wbuf[:, h % 2, :], xsl,
                                     start=True, stop=False)
                            t.matmul(pya[h % 2][:, 0:HP], ddiag[:, h, :], xsl,
                                     start=False, stop=True).then_inc(s_pe, 1)
                            t.matmul(psc[h % 2][:, 0:HP], btil[:, h % 2, :], xsl,
                                     start=True, stop=False)
                            t.matmul(psc[h % 2][:, 0:HP], cdey[:, h % 2, :], sbf[:, h, :],
                                     start=False, stop=True).then_inc(s_pe, 1)
                            t.matmul(pyb[h % 2][:, 0:HP], bc[:, 1, ch * P:(ch + 1) * P],
                                     sbf[:, h, :], start=True, stop=True).then_inc(s_pe, 1)
                    @block.scalar
                    def _(s):
                        for h in range(NH):
                            s.wait_ge(s_pe, pe0 + 5 * h + 1)
                            s.copy(srow[0:1, h % 2, :], pdb[h % 2][0:1, 0:P]).then_inc(s_ac, 1)
                            s.wait_ge(s_pe, pe0 + 5 * h + 2)
                            if h >= 2:
                                s.wait_ge(s_dv, dv0 + 5 * (h - 2) + 1)
                            s.activation(out=ebuf[:, h % 2, :], in_=pdb[h % 2][:, :],
                                         func=AF.Exp, scale=-1.0, bias=ndacsT[:, ch, h:h + 1]).then_inc(s_ac, 1)
                            s.wait_ge(s_pe, pe0 + 5 * h + 5)
                            if h >= 2:
                                s.wait_ge(s_dv, dv0 + 5 * (h - 2) + 5)
                            s.activation(out=yibuf[:, h % 2, :], in_=pyb[h % 2][:, 0:HP],
                                         func=AF.Identity, scale=eT[:, ch, h:h + 1]).then_inc(s_ac, 1)
                            s.copy(sbf[:, h, :], psc[h % 2][:, 0:HP]).then_inc(s_ac, 1)
                    @block.vector
                    def _(v):
                        for h in range(NH):
                            v.wait_ge(s_ac, ac0 + 4 * h + 2)
                            if h >= 2:
                                v.wait_ge(s_pe, pe0 + 5 * (h - 2) + 5)
                            v.tensor_scalar(out=embuf[:, h % 2, :], in0=ebuf[:, h % 2, :],
                                            scalar1=1.0, scalar2=None, op0=ALU.min).then_inc(s_dv, 1)
                            v.tensor_scalar(out=btil[:, h % 2, :], in0=btok[:],
                                            scalar1=wtT[:, ch, h:h + 1], scalar2=None,
                                            op0=ALU.mult).then_inc(s_dv, 1)
                            v.tensor_scalar(out=cdey[:, h % 2, :], in0=eyeb[:],
                                            scalar1=cdb[:, h:h + 1], scalar2=None,
                                            op0=ALU.mult).then_inc(s_dv, 1)
                            v.scalar_tensor_tensor(out=wbuf[:, h % 2, :], in0=cbm[:],
                                                   scalar=dtT[:, ch, h:h + 1], in1=embuf[:, h % 2, :],
                                                   op0=ALU.mult, op1=ALU.mult).then_inc(s_dv, 1)
                            v.wait_ge(s_ac, ac0 + 4 * h + 3)
                            v.tensor_tensor(out=xt[:, ch, HP * h:HP * h + HP],
                                            in0=yibuf[:, h % 2, :], in1=pya[h % 2][:, 0:HP],
                                            op=ALU.add).then_inc(s_dv, 1)
                c["pe"] += 5 * NH
                c["ac"] += 4 * NH
                c["dv"] += 5 * NH

            with b.block() as block:
                @block.sync
                def _(sy):
                    sy.dma_start(ytok_dram.rearrange("(n p) m -> p n m", p=P)[:, seq * NCH:(seq + 1) * NCH, :],
                                 xt[:]).then_inc(s_o, 16)
                    sy.wait_ge(s_o, c["o"] + 16)
            c["o"] += 16




def emit_rms(b, name, ytok_dram, gate_dram, normw_dram, gn_dram, eps=1e-5):
    nc = b.nc
    II = 4096
    ctx = ExitStack()
    with ctx:
        def csb(n, shape, dt):
            return ctx.enter_context(nc.sbuf_tensor(f"{name}_{n}", shape, dt))
        GN = 8
        yb = csb("y", [P, GN, II], BF16)
        gt = csb("gt", [P, 2, II], BF16)
        sg = csb("sg", [P, II], BF16)
        scrap = csb("sc", [P, II], BF16)
        var = csb("va", [P, GN], F32)
        rst = csb("rs", [P, GN], F32)
        nwexp = csb("nw", [P, II], F32)
        nwrow = csb("nr", [1, II], F32)
        epsb = csb("ep", [P, 1], F32)
        stg = csb("st", [P, 2, II], BF16)
        pb = ctx.enter_context(nc.psum_tensor(f"{name}_pb", [P, 512], F32))
        s_d = b.sem(f"{name}_d")
        s_a = b.sem(f"{name}_a")
        s_v = b.sem(f"{name}_v")
        s_o = b.sem(f"{name}_o")
        c = dict(d=0, a=0, v=0, o=0)
        ysrc = ytok_dram.rearrange("(n p) m -> p n m", p=P)
        gsrc = gate_dram.rearrange("(n p) m -> p n m", p=P)
        gdst = gn_dram.rearrange("(n p) m -> p n m", p=P)
        with b.block() as block:
            @block.gpsimd
            def _(g):
                g.dma_start(nwrow[:], normw_dram).then_inc(s_d, 16)
                g.wait_ge(s_d, 16)
            @block.vector
            def _(v):
                v.memset(epsb[:], float(eps))
        c["d"] = 16
        with b.block() as block:
            @block.tensor
            def _(t):
                for i in range(II // 512):
                    t.matmul(pb[:, :], b.onesf_row[:], nwrow[0:1, i * 512:(i + 1) * 512],
                             start=True, stop=True).then_inc(s_a, 1)
                    t.wait_ge(s_v, c["v"] + i + 1)
            @block.vector
            def _(v):
                for i in range(II // 512):
                    v.wait_ge(s_a, c["a"] + i + 1)
                    v.tensor_copy(nwexp[:, i * 512:(i + 1) * 512], pb[:, :]).then_inc(s_v, 1)
        c["a"] += 8
        c["v"] += 8
        for grp in range(T // P // GN):
            with b.block() as block:
                @block.sync
                def _(sy):
                    sy.dma_start(yb[:], ysrc[:, grp * GN:(grp + 1) * GN, :]).then_inc(s_d, 16)
                    sy.wait_ge(s_d, c["d"] + 16)
            c["d"] += 16
            for tt in range(GN):
                gtok = grp * GN + tt
                with b.block() as block:
                    @block.sync
                    def _(sy):
                        sy.dma_start(gt[:, tt % 2, :], gsrc[:, gtok, :]).then_inc(s_d, 16)
                        sy.wait_ge(s_d, c["d"] + 16)
                c["d"] += 16
                with b.block() as block:
                    @block.scalar
                    def _(s):
                        s.activation(out=sg[:], in_=gt[:, tt % 2, :], func=AF.Silu)
                with b.block() as block:
                    @block.vector
                    def _(v):
                        v.tensor_tensor(out=yb[:, tt, :], in0=yb[:, tt, :], in1=sg[:], op=ALU.mult)
                with b.block() as block:
                    @block.scalar
                    def _(s):
                        s.activation(out=scrap[:], in_=yb[:, tt, :], func=AF.Square,
                                     accum_out=var[:, tt:tt + 1])
            with b.block() as block:
                @block.vector
                def _(v):
                    v.tensor_scalar(out=rst[:], in0=var[:], scalar1=1.0 / II, scalar2=None,
                                    op0=ALU.mult)
                    v.tensor_copy(sg[:, 0:512], sg[:, 0:512])
                    v.tensor_copy(sg[:, 0:512], sg[:, 0:512])
                    v.tensor_copy(sg[:, 0:512], sg[:, 0:512])
            with b.block() as block:
                @block.scalar
                def _(s):
                    s.activation(out=rst[:], in_=rst[:], func=AF.Sqrt, bias=epsb[:])
                    s.copy(sg[:, 0:512], sg[:, 0:512])
                    s.copy(sg[:, 0:512], sg[:, 0:512])
            with b.block() as block:
                @block.vector
                def _(v):
                    v.reciprocal(out=rst[:], in_=rst[:])
                    v.tensor_copy(sg[:, 0:512], sg[:, 0:512])
                    v.tensor_copy(sg[:, 0:512], sg[:, 0:512])
                    v.tensor_copy(sg[:, 0:512], sg[:, 0:512])
            with b.block() as block:
                @block.vector
                def _(v):
                    for tt in range(GN):
                        if tt >= 2:
                            v.wait_ge(s_o, c["o"] + 16 * (tt - 1))
                        v.scalar_tensor_tensor(out=stg[:, tt % 2, :], in0=yb[:, tt, :],
                                               scalar=rst[:, tt:tt + 1], in1=nwexp[:],
                                               op0=ALU.mult, op1=ALU.mult).then_inc(s_v, 1)
                @block.sync
                def _(sy):
                    for tt in range(GN):
                        sy.wait_ge(s_v, c["v"] + tt + 1)
                        sy.dma_start(gdst[:, grp * GN + tt, :], stg[:, tt % 2, :]).then_inc(s_o, 16)
                    sy.wait_ge(s_o, c["o"] + 16 * GN)
            c["v"] += GN
            c["o"] += 16 * GN




def emit_outproj(b, name, gn_dram, wout_dram, attn1_dram):
    nc = b.nc
    II = 4096
    ctx0 = ExitStack()
    gnfm = ctx0.enter_context(nc.sbuf_tensor(f"{name}_x", [P, II // P, TSEQ], BF16))
    s_d = b.sem(f"{name}_ld")
    dv = 0
    for seq in range(NSEQ):
        tok0 = seq * TSEQ
        with b.block() as block:
            @block.sync
            def _(sy):
                nonlocal dv
                for st in range(TSEQ // 512):
                    for k in range(II // P):
                        sy.dma_start(gnfm[:, k, st * 512:(st + 1) * 512],
                                     gn_dram[tok0 + st * 512:tok0 + (st + 1) * 512,
                                             k * P:(k + 1) * P],
                                     transpose=True).then_inc(s_d, 16)
                        dv += 16
                sy.wait_ge(s_d, dv)
        emit_mm_fm(b, f"{name}_m{seq}", wout_dram, gnfm, II, D, TSEQ,
                   out_dram=attn1_dram, out_tok0=tok0)
    ctx0.close()




def emit_ffn1(b, name, xn_dram, rt_dram, w1t_dram, b1_dram, w2t_dram, b2_dram,
              prevmask_dram, iota_dram, eye_dram, h_dram, out_dram, CAP=512):
    nc = b.nc
    HT = T // 2
    NTT = T // P
    ctx = ExitStack()
    with ctx:
        def csb(n, shape, dt):
            return ctx.enter_context(nc.sbuf_tensor(f"{name}_{n}", shape, dt))

        def cps(n, shape, dt=F32):
            return ctx.enter_context(nc.psum_tensor(f"{name}_{n}", shape, dt))

        xn = csb("xn", [P, DC, HT], BF16)
        rtw = csb("rt", [P, DC, 8], BF16)
        eyef = csb("ey", [P, P], F32)
        lg = csb("lg", [P, NTT, 8], F32)
        mx8 = csb("mx", [P, NTT, 8], F32)
        mi8 = csb("mi", [P, NTT, 8], U32)
        negm = csb("nm", [P, NTT, 1], F32)
        e12 = csb("e12", [P, NTT, 2], F32)
        oh1 = csb("oh1", [P, NTT, 8], F32)
        oh2 = csb("oh2", [P, NTT, 8], F32)
        p12 = csb("p12", [P, NTT, 2], F32)
        esum = csb("es", [P, NTT, 1], F32)
        stc = csb("stc", [P, NTT, 1], F32)
        poscol = csb("po", [P, NTT, 2], F32)
        iotab = csb("io", [P, 8], F32)
        ohT = csb("ohT", [8, T], F32)
        scnT = csb("scT", [8, T], F32)
        lincT = csb("lcT", [P, NTT, 2, 8], F32)
        G = csb("G", [8, 16], F32)
        pmsk = csb("pm", [8, 1], F32)
        offr = csb("of", [1, 2, 8], F32)
        offb = csb("ofb", [P, 2, 8], F32)
        b1sb = csb("b1", [P, 64], F32)
        b2sb = csb("b2", [1, D], BF16)
        cntr = csb("cn", [1, 16], F32)
        hti = [csb(f"ht{i}", [P, 64, P], BF16) for i in range(2)]
        w2c = [csb(f"w2{i}", [P, 64, 256], BF16) for i in range(1)]
        stg = csb("sg", [P, 3, 512], BF16)
        ps = [cps(f"p{i}", [P, 512]) for i in range(2)]

        s_d = b.sem(f"{name}_d")
        s_m = b.sem(f"{name}_m")
        s_a = b.sem(f"{name}_a")
        s_v = b.sem(f"{name}_v")
        s_o = b.sem(f"{name}_o")
        s_cc = b.sem(f"{name}_cc")
        c = dict(d=0, m=0, a=0, v=0, o=0)
        cc_in = b.dram(f"{name}_ccin", [1, 16], F32)
        cc_out = b.dram(f"{name}_ccout", [NCORE, 16], F32)

        with b.block() as block:
            @block.gpsimd
            def _(g):
                g.dma_start(rtw[:], rt_dram.rearrange("(c p) m -> p c m", p=P)).then_inc(s_d, 16)
                g.dma_start(iotab[0:1, :], iota_dram).then_inc(s_d, 16)
                g.dma_start(pmsk[:], prevmask_dram).then_inc(s_d, 16)
                g.dma_start(b1sb[:], b1_dram).then_inc(s_d, 16)
                g.dma_start(b2sb[:], b2_dram).then_inc(s_d, 16)
                g.dma_start(eyef[:], eye_dram).then_inc(s_d, 16)
                g.wait_ge(s_d, 96)
        c["d"] = 96
        with b.block() as block:
            @block.tensor
            def _(t):
                t.matmul(ps[0][:, 0:8], b.onesf_row[:], iotab[0:1, :], start=True, stop=True)
        with b.block() as block:
            @block.scalar
            def _(s):
                s.copy(iotab[:], ps[0][:, 0:8])

        for half in range(2):
            tok0 = half * HT
            tt0 = half * (HT // P)
            with b.block() as block:
                @block.sync
                def _(sy):
                    sy.dma_start(xn[:], fmv(xn_dram)[:, :, tok0:tok0 + HT]).then_inc(s_d, 16)
                    sy.wait_ge(s_d, c["d"] + 16)
            c["d"] += 16
            with b.block() as block:
                @block.tensor
                def _(t):
                    for tt in range(HT // P):
                        if tt >= 2:
                            t.wait_ge(s_a, c["a"] + tt - 1)
                        for k in range(DC):
                            ins = t.matmul(ps[tt % 2][0:P, 0:8],
                                           xn[:, k, tt * P:(tt + 1) * P],
                                           rtw[:, k, :], start=(k == 0), stop=(k == DC - 1))
                        ins.then_inc(s_m, 1)
                @block.scalar
                def _(s):
                    for tt in range(HT // P):
                        s.wait_ge(s_m, c["m"] + tt + 1)
                        s.copy(lg[:, tt0 + tt, :], ps[tt % 2][0:P, 0:8]).then_inc(s_a, 1)
            c["m"] += HT // P
            c["a"] += HT // P

        with b.block() as block:
            @block.vector
            def _(v):
                for tt in range(NTT):
                    v.max(out=mx8[:, tt, :], in_=lg[:, tt, :])
        with b.block() as block:
            @block.vector
            def _(v):
                for tt in range(NTT):
                    v.max_index(out=mi8[:, tt, :], in_max=mx8[:, tt, :], in_values=lg[:, tt, :])
                for tt in range(NTT):
                    v.tensor_scalar(out=negm[:, tt, :], in0=mx8[:, tt, 0:1], scalar1=-1.0,
                                    scalar2=None, op0=ALU.mult)
        with b.block() as block:
            @block.scalar
            def _(s):
                for tt in range(NTT):
                    s.activation(out=lg[:, tt, :], in_=lg[:, tt, :], func=AF.Exp,
                                 bias=negm[:, tt, 0:1], accum_out=esum[:, tt, 0:1])
        with b.block() as block:
            @block.vector
            def _(v):
                for tt in range(NTT):
                    v.tensor_copy(e12[:, tt, :], mi8[:, tt, 0:2])
                for tt in range(NTT):
                    v.tensor_scalar(out=oh1[:, tt, :], in0=iotab[:], scalar1=e12[:, tt, 0:1],
                                    scalar2=None, op0=ALU.is_equal)
                for tt in range(NTT):
                    v.tensor_scalar(out=oh2[:, tt, :], in0=iotab[:], scalar1=e12[:, tt, 1:2],
                                    scalar2=None, op0=ALU.is_equal)
                for tt in range(NTT):
                    v.reciprocal(out=p12[:, tt, 0:1], in_=esum[:, tt, 0:1])
                for tt in range(NTT):
                    v.tensor_tensor(out=p12[:, tt, 1:2], in0=mx8[:, tt, 1:2],
                                    in1=mx8[:, tt, 0:1], op=ALU.subtract)
        with b.block() as block:
            @block.scalar
            def _(s):
                for tt in range(NTT):
                    s.activation(out=p12[:, tt, 1:2], in_=p12[:, tt, 1:2], func=AF.Exp)
        with b.block() as block:
            @block.vector
            def _(v):
                for tt in range(NTT):
                    v.tensor_scalar(out=p12[:, tt, 1:2], in0=p12[:, tt, 1:2],
                                    scalar1=p12[:, tt, 0:1], scalar2=None, op0=ALU.mult)

        with b.block() as block:
            @block.tensor
            def _(t):
                for tt in range(NTT):
                    t.matmul(ps[0][0:1, 0:8], b.onesf_col[:], oh1[:, tt, :],
                             start=(tt == 0), stop=(tt == NTT - 1))
                for tt in range(NTT):
                    t.matmul(ps[0][0:1, 8:16], b.onesf_col[:], oh2[:, tt, :],
                             start=(tt == 0), stop=(tt == NTT - 1))
        with b.block() as block:
            @block.scalar
            def _(s):
                s.copy(cntr[:], ps[0][0:1, 0:16])
        with b.block() as block:
            @block.gpsimd
            def _(g):
                g.dma_start(cc_in, cntr[:]).then_inc(s_d, 16)
                g.wait_ge(s_d, c["d"] + 16)
                g.collective_compute("AllGather", ALU.bypass,
                                     replica_groups=[list(range(NCORE))],
                                     ins=[cc_in.opt()], outs=[cc_out.opt()]).then_inc(s_cc, 1)
                g.wait_ge(s_cc, 1)
                g.dma_start(G[:], cc_out).then_inc(s_d, 16)
                g.wait_ge(s_d, c["d"] + 32)
        c["d"] += 32
        with b.block() as block:
            @block.tensor
            def _(t):
                t.matmul(ps[0][0:1, 0:16], pmsk[:], G[:], start=True, stop=True)
                t.matmul(ps[0][0:1, 16:24], b.onesf_col[0:8, :], G[:, 0:8], start=True, stop=True)
        with b.block() as block:
            @block.scalar
            def _(s):
                s.copy(offr[0:1, 0, :], ps[0][0:1, 0:8])
                s.copy(offr[0:1, 1, :], ps[0][0:1, 8:16])
        with b.block() as block:
            @block.vector
            def _(v):
                v.tensor_tensor(out=offr[0:1, 1, :], in0=offr[0:1, 1, :],
                                in1=ps[0][0:1, 16:24], op=ALU.add)
                v.tensor_copy(ohT[0:8, 0:512], ohT[0:8, 0:512])
                v.tensor_copy(ohT[0:8, 0:512], ohT[0:8, 0:512])
                v.tensor_copy(ohT[0:8, 0:512], ohT[0:8, 0:512])
        with b.block() as block:
            @block.tensor
            def _(t):
                t.matmul(ps[1][:, 0:8], b.onesf_row[:], offr[0:1, 0, :], start=True, stop=True)
                t.matmul(ps[1][:, 8:16], b.onesf_row[:], offr[0:1, 1, :], start=True, stop=True)
        with b.block() as block:
            @block.scalar
            def _(s):
                s.copy(offb[:, 0, :], ps[1][:, 0:8])
                s.copy(offb[:, 1, :], ps[1][:, 8:16])
                s.copy(stg[:, 0, :], stg[:, 0, :])
                s.copy(stg[:, 0, :], stg[:, 0, :])

        for rank in range(2):
            oh = [oh1, oh2][rank]
            with b.block() as block:
                @block.tensor
                def _(t):
                    for tt in range(NTT):
                        if tt >= 1:
                            t.wait_ge(s_a, c["a"] + tt)
                        t.transpose(ps[0][0:8, 0:P], oh[:, tt, :], eyef[:]).then_inc(s_m, 1)
                @block.scalar
                def _(s):
                    for tt in range(NTT):
                        s.wait_ge(s_m, c["m"] + tt + 1)
                        s.copy(ohT[0:8, tt * P:(tt + 1) * P], ps[0][0:8, 0:P]).then_inc(s_a, 1)
            c["m"] += NTT
            c["a"] += NTT
            with b.block() as block:
                @block.vector
                def _(v):
                    v.tensor_tensor_scan(out=scnT[:], data0=ohT[:], data1=ohT[:],
                                         initial=0.0, op0=ALU.add, op1=ALU.bypass)
            with b.block() as block:
                @block.tensor
                def _(t):
                    for tt in range(NTT):
                        if tt >= 1:
                            t.wait_ge(s_a, c["a"] + tt)
                        t.transpose(ps[0][:, 0:8], scnT[0:8, tt * P:(tt + 1) * P],
                                    eyef[0:8, 0:8]).then_inc(s_m, 1)
                @block.scalar
                def _(s):
                    for tt in range(NTT):
                        s.wait_ge(s_m, c["m"] + tt + 1)
                        s.copy(lincT[:, tt, rank, :], ps[0][:, 0:8]).then_inc(s_a, 1)
            c["m"] += NTT
            c["a"] += NTT

        with b.block() as block:
            @block.vector
            def _(v):
                for rank in range(2):
                    oh = [oh1, oh2][rank]
                    for tt in range(NTT):
                        v.tensor_tensor(out=lincT[:, tt, rank, :], in0=lincT[:, tt, rank, :],
                                        in1=offb[:, rank, :], op=ALU.add)
                for rank in range(2):
                    oh = [oh1, oh2][rank]
                    for tt in range(NTT):
                        v.tensor_tensor(out=lincT[:, tt, rank, :], in0=lincT[:, tt, rank, :],
                                        in1=oh[:, tt, :], op=ALU.mult)
                for rank in range(2):
                    for tt in range(NTT):
                        v.reduce_sum(out=poscol[:, tt, rank:rank + 1],
                                     in_=lincT[:, tt, rank, :], axis=AX.X)
                for tt in range(NTT):
                    v.tensor_scalar(out=poscol[:, tt, :], in0=poscol[:, tt, :],
                                    scalar1=-1.0, scalar2=float(CAP), op0=ALU.add, op1=ALU.is_lt)
                for tt in range(NTT):
                    v.tensor_tensor(out=p12[:, tt, :], in0=p12[:, tt, :],
                                    in1=poscol[:, tt, :], op=ALU.mult)
                for tt in range(NTT):
                    v.reduce_sum(out=stc[:, tt, 0:1], in_=p12[:, tt, :], axis=AX.X)

        for half in range(2):
            tok0 = half * HT
            with b.block() as block:
                @block.sync
                def _(sy):
                    sy.dma_start(xn[:], fmv(xn_dram)[:, :, tok0:tok0 + HT]).then_inc(s_d, 16)
                    sy.wait_ge(s_d, c["d"] + 16)
            c["d"] += 16

            emit_mm_fm(b, f"{name}_h{half}", w1t_dram, xn, D, 8192, HT,
                       out_dram=h_dram, bias_sb=b1sb, act_fn=AF.Gelu, out_tok0=tok0)

            NTTh = HT // P
            NOC = 8
            with b.block() as block:
                @block.gpsimd
                def _(g):
                    for oc in range(NOC):
                        if oc >= 1:
                            g.wait_ge(s_m, c["m"] + oc * NTTh)
                        g.dma_start(w2c[0][:],
                                    w2t_dram.rearrange("(c p) m -> p c m", p=P)[:, :, oc * 256:(oc + 1) * 256]).then_inc(s_d, 16)

                @block.sync
                def _(sy):
                    n = 0
                    for oc in range(NOC):
                        for tt in range(NTTh):
                            n += 1
                            if n > 2:
                                sy.wait_ge(s_m, c["m"] + n - 2)
                            sy.dma_start(hti[(n - 1) % 2][:],
                                         h_dram.rearrange("(c p) t -> p c t", p=P)[:, :, tok0 + tt * P:tok0 + (tt + 1) * P]).then_inc(s_o, 16)

                @block.tensor
                def _(t):
                    n = 0
                    for oc in range(NOC):
                        t.wait_ge(s_d, c["d"] + 16 * (oc + 1))
                        for tt in range(NTTh):
                            n += 1
                            t.wait_ge(s_o, c["o"] + 16 * n)
                            if n > 2:
                                t.wait_ge(s_v, c["v"] + n - 2)
                            for fc in range(64):
                                t.matmul(ps[(n - 1) % 2][:, 0:256], hti[(n - 1) % 2][:, fc, :],
                                         w2c[0][:, fc, :], start=(fc == 0), stop=False)
                            t.matmul(ps[(n - 1) % 2][:, 0:256], b.ones_row[:],
                                     b2sb[0:1, oc * 256:(oc + 1) * 256],
                                     start=False, stop=True).then_inc(s_m, 1)

                @block.vector
                def _(v):
                    n = 0
                    for oc in range(NOC):
                        for tt in range(NTTh):
                            n += 1
                            v.wait_ge(s_m, c["m"] + n)
                            if n > 3:
                                v.wait_ge(s_a, c["a"] + 16 * (n - 3))
                            v.tensor_scalar(out=stg[:, (n - 1) % 3, 0:256], in0=ps[(n - 1) % 2][:, 0:256],
                                            scalar1=stc[:, half * NTTh + tt, 0:1],
                                            scalar2=None, op0=ALU.mult).then_inc(s_v, 1)

                @block.scalar
                def _(s):
                    n = 0
                    for oc in range(NOC):
                        for tt in range(NTTh):
                            n += 1
                            s.wait_ge(s_v, c["v"] + n)
                            s.dma_start(out_dram.rearrange("(n p) m -> p n m", p=P)[:, half * NTTh + tt, oc * 256:(oc + 1) * 256],
                                        stg[:, (n - 1) % 3, 0:256]).then_inc(s_a, 16)
                    s.wait_ge(s_a, c["a"] + 16 * NOC * NTTh)
            c["d"] += 16 * NOC
            c["m"] += NOC * NTTh
            c["v"] += NOC * NTTh
            c["o"] += 16 * NOC * NTTh
            c["a"] += 16 * NOC * NTTh




def emit_wc(b, name, yhat_dram, wc_dram, bc_dram, a0_dram):
    nc = b.nc
    with ExitStack() as ctx:
        ysb = ctx.enter_context(nc.sbuf_tensor(f"{name}_y", [P, DC, T], BF16))
        bcsb = ctx.enter_context(nc.sbuf_tensor(f"{name}_b", [P, DC], F32))
        s_d = b.sem(f"{name}_ld")
        with b.block() as block:
            @block.sync
            def _(sy):
                sy.dma_start(ysb[:], fmv(yhat_dram)).then_inc(s_d, 16)
                sy.dma_start(bcsb[:], bc_dram).then_inc(s_d, 16)
                sy.wait_ge(s_d, 32)
        emit_mm_fm(b, name + "m", wc_dram, ysb, D, D, T, out_dram=a0_dram, bias_sb=bcsb)




def emit_combine(b, name, x_dram, a_dram, ftok_dram, out_dram):
    nc = b.nc
    SW = 512
    nst = T // SW
    ctx = ExitStack()
    with ctx:
        def csb(n, shape, dt):
            return ctx.enter_context(nc.sbuf_tensor(f"{name}_{n}", shape, dt))
        xs = csb("x", [P, DC, SW], F32)
        aa = csb("a", [P, DC, SW], F32)
        ff = csb("f", [P, DC, SW], BF16)
        s_d = b.sem(f"{name}_d")
        dv = 0
        for i in range(nst):
            with b.block() as block:
                @block.sync
                def _(sy):
                    nonlocal dv
                    sy.dma_start(xs[:], fmv(x_dram)[:, :, i * SW:(i + 1) * SW]).then_inc(s_d, 16)
                    for k in range(DC):
                        sy.dma_start(ff[:, k, :],
                                     ftok_dram[i * SW:(i + 1) * SW, k * P:(k + 1) * P],
                                     transpose=True).then_inc(s_d, 16)
                    dv += 16 * (DC + 1)
                    sy.wait_ge(s_d, dv)
                @block.gpsimd
                def _(g):
                    nonlocal dv
                    g.dma_start(aa[:], fmv(a_dram)[:, :, i * SW:(i + 1) * SW]).then_inc(s_d, 16)
                    dv += 16
                    g.wait_ge(s_d, dv)
            with b.block() as block:
                @block.vector
                def _(v):
                    for k in range(0, DC, 2):
                        v.tensor_tensor(out=xs[:, k, :], in0=xs[:, k, :], in1=aa[:, k, :], op=ALU.add)
                        v.tensor_tensor(out=xs[:, k, :], in0=xs[:, k, :], in1=ff[:, k, :], op=ALU.add)
                @block.gpsimd
                def _(g):
                    for k in range(1, DC, 2):
                        g.tensor_tensor(out=xs[:, k, :], in0=xs[:, k, :], in1=aa[:, k, :], op=ALU.add)
                        g.tensor_tensor(out=xs[:, k, :], in0=xs[:, k, :], in1=ff[:, k, :], op=ALU.add)
            with b.block() as block:
                @block.sync
                def _(sy):
                    nonlocal dv
                    sy.dma_start(fmv(out_dram)[:, :, i * SW:(i + 1) * SW], xs[:]).then_inc(s_d, 16)
                    dv += 16
                    sy.wait_ge(s_d, dv)




def _make_masks():
    m = np.zeros((128, 4, 512), np.uint8)
    for mm in range(4):
        for s in range(128):
            m[s, mm, 128 * mm + s:] = 1
    return m


def _prep_weights(inputs):
    w = {}
    w["wq"] = np.ascontiguousarray(inputs["L0_attn_Wq"], np.float32)
    w["wk"] = np.ascontiguousarray(inputs["L0_attn_Wk"], np.float32)
    w["wv"] = np.ascontiguousarray(inputs["L0_attn_Wv"], np.float32)
    w["wg0"] = np.ascontiguousarray(inputs["L0_attn_Wg"], np.float32)
    w["wc"] = np.ascontiguousarray(inputs["L0_attn_Wc"], np.float32)
    w["bc"] = np.ascontiguousarray(np.asarray(inputs["L0_attn_bc"], np.float32).reshape(DC, P).T)
    w["fg"] = np.ascontiguousarray(np.asarray(inputs["L0_ffn_Wg"], np.float32).T)
    w["fbias"] = np.asarray(inputs["L0_ffn_bias"], np.float32).reshape(1, 8)
    w["w1t"] = np.ascontiguousarray(np.asarray(inputs["L0_ffn_W1"], np.float32).transpose(0, 2, 1))
    w["w3t"] = np.ascontiguousarray(np.asarray(inputs["L0_ffn_W3"], np.float32).transpose(0, 2, 1))
    w["w2t"] = np.ascontiguousarray(np.asarray(inputs["L0_ffn_W2"], np.float32).transpose(0, 2, 1))
    w["wint"] = np.ascontiguousarray(np.asarray(inputs["L1_attn_in_proj_w"], np.float32).T)
    w["cw"] = np.ascontiguousarray(np.asarray(inputs["L1_attn_conv_w"], np.float32).reshape(34, P, 4).transpose(1, 0, 2))
    w["cb"] = np.ascontiguousarray(np.asarray(inputs["L1_attn_conv_b"], np.float32).reshape(34, P).T)
    w["a1A"] = np.asarray(inputs["L1_attn_A"], np.float32).reshape(64, 1)
    w["dtb"] = np.asarray(inputs["L1_attn_dt_bias"], np.float32).reshape(64, 1)
    w["a1D"] = np.asarray(inputs["L1_attn_D"], np.float32).reshape(1, 64)
    w["nw"] = np.asarray(inputs["L1_attn_norm_w"], np.float32).reshape(1, 4096)
    w["wout"] = np.ascontiguousarray(np.asarray(inputs["L1_attn_out_proj_w"], np.float32).T)
    w["rt"] = np.ascontiguousarray(np.asarray(inputs["L1_ffn_router_w"], np.float32).T)
    w["f1w1t"] = np.ascontiguousarray(np.asarray(inputs["L1_ffn_w1"], np.float32).T)
    w["f1b1"] = np.ascontiguousarray(np.asarray(inputs["L1_ffn_b1"], np.float32).reshape(64, P).T)
    w["f1w2t"] = np.ascontiguousarray(np.asarray(inputs["L1_ffn_w2"], np.float32).T)
    w["f1b2"] = np.asarray(inputs["L1_ffn_b2"], np.float32).reshape(1, D)
    w["masks"] = _make_masks()
    w["eye"] = np.eye(P, dtype=np.float32)
    w["iota"] = np.arange(8, dtype=np.float32).reshape(1, 8)
    return w


def _launch(builder_fn, in_maps):
    b = B()
    builder_fn(b)
    nc = b.finish()
    res = run_bass_kernel_spmd(nc, in_maps, core_ids=list(range(NCORE)))
    return res.results


USE_FUSED = True


def kernel(**inputs):
    if USE_FUSED:
        return kernel_fused(inputs)
    return kernel_multi(inputs)


def kernel_multi(inputs):
    w = _prep_weights(inputs)
    x = np.asarray(inputs["L0_attn_hidden_states"], np.float32)
    xfm_c = [np.ascontiguousarray(
        np.concatenate([x[2 * cc].T, x[2 * cc + 1].T], axis=1)) for cc in range(NCORE)]

    st = {}

    def pc(key):
        return st[key]

    def b_ln0(b):
        xd = b.dram("x", [D, T], F32, kind="ExternalInput")
        xnd = b.dram("xn", [D, T], BF16, kind="ExternalOutput")
        b.init_consts()
        emit_ln(b, "ln0", xd, xnd)
    r = _launch(b_ln0, [{"x": xfm_c[cc]} for cc in range(NCORE)])
    st["xn0"] = [r[cc]["xn"] for cc in range(NCORE)]

    def b_at0(b):
        xn = b.dram("xn", [D, T], BF16, kind="ExternalInput")
        wq = b.dram("wq", [D, D], F32, kind="ExternalInput")
        wk = b.dram("wk", [D, 1024], F32, kind="ExternalInput")
        wv = b.dram("wv", [D, 1024], F32, kind="ExternalInput")
        wg = b.dram("wg", [D, 8], F32, kind="ExternalInput")
        masks = b.dram("masks", [P, 4, 512], mybir.dt.uint8, kind="ExternalInput")
        eye = b.dram("eye", [P, P], F32, kind="ExternalInput")
        yhat = b.dram("yhat", [D, T], BF16, kind="ExternalOutput")
        b.init_consts()
        emit_attn0(b, "at0", xn, wq, wk, wv, wg, masks, eye, yhat)
    r = _launch(b_at0, [{"xn": st["xn0"][cc], "wq": w["wq"], "wk": w["wk"],
                         "wv": w["wv"], "wg": w["wg0"], "masks": w["masks"],
                         "eye": w["eye"]} for cc in range(NCORE)])
    st["yhat"] = [r[cc]["yhat"] for cc in range(NCORE)]

    def b_wc(b):
        yhat = b.dram("yhat", [D, T], BF16, kind="ExternalInput")
        wc = b.dram("wc", [D, D], F32, kind="ExternalInput")
        bc = b.dram("bc", [P, DC], F32, kind="ExternalInput")
        a0 = b.dram("a0", [D, T], BF16, kind="ExternalOutput")
        b.init_consts()
        ysb = b.sb("ysb", [P, DC, T], BF16)
        bcsb = b.sb("bcsb", [P, DC], F32)
        s_d = b.sem("wcld")
        with b.block() as block:
            @block.sync
            def _(sy):
                sy.dma_start(ysb[:], fmv(yhat)).then_inc(s_d, 16)
                sy.dma_start(bcsb[:], bc).then_inc(s_d, 16)
                sy.wait_ge(s_d, 32)
        emit_mm_fm(b, "wc", wc, ysb, D, D, T, out_dram=a0, bias_sb=bcsb)
    r = _launch(b_wc, [{"yhat": st["yhat"][cc], "wc": w["wc"], "bc": w["bc"]}
                       for cc in range(NCORE)])
    st["a0"] = [r[cc]["a0"] for cc in range(NCORE)]

    def b_f0(b):
        xn = b.dram("xn", [D, T], BF16, kind="ExternalInput")
        wg = b.dram("wg", [D, 8], F32, kind="ExternalInput")
        bias = b.dram("bias", [1, 8], F32, kind="ExternalInput")
        iota = b.dram("iota", [1, 8], F32, kind="ExternalInput")
        w1t = b.dram("w1t", [8, D, 1024], F32, kind="ExternalInput")
        w3t = b.dram("w3t", [8, D, 1024], F32, kind="ExternalInput")
        w2t = b.dram("w2t", [8, 1024, D], F32, kind="ExternalInput")
        out = b.dram("out", [T, D], BF16, kind="ExternalOutput")
        b.init_consts()
        emit_ffn0(b, "f0", xn, wg, bias, iota, w1t, w3t, w2t, out)
    r = _launch(b_f0, [{"xn": st["xn0"][cc], "wg": w["fg"], "bias": w["fbias"],
                        "iota": w["iota"], "w1t": w["w1t"], "w3t": w["w3t"],
                        "w2t": w["w2t"]} for cc in range(NCORE)])
    st["f0"] = [r[cc]["out"] for cc in range(NCORE)]

    def b_c1(b):
        xd = b.dram("x", [D, T], F32, kind="ExternalInput")
        a0 = b.dram("a0", [D, T], BF16, kind="ExternalInput")
        f0 = b.dram("f0", [T, D], BF16, kind="ExternalInput")
        x1 = b.dram("x1", [D, T], F32, kind="ExternalOutput")
        xn1 = b.dram("xn1", [D, T], BF16, kind="ExternalOutput")
        b.init_consts()
        emit_ln(b, "c1", x1, xn1, x_store=(xd, a0, f0))
    r = _launch(b_c1, [{"x": xfm_c[cc], "a0": st["a0"][cc], "f0": st["f0"][cc]}
                       for cc in range(NCORE)])
    st["x1"] = [r[cc]["x1"] for cc in range(NCORE)]
    st["xn1"] = [r[cc]["xn1"] for cc in range(NCORE)]

    def b_ip(b):
        xn = b.dram("xn", [D, T], BF16, kind="ExternalInput")
        wint = b.dram("wint", [D, 8512], F32, kind="ExternalInput")
        gate = b.dram("gate", [T, 4096], BF16, kind="ExternalOutput")
        xbc = b.dram("xbc", [4352, T], BF16, kind="ExternalOutput")
        dtr = b.dram("dtr", [P, T], F32, kind="ExternalOutput")
        b.init_consts()
        emit_inproj(b, "ip", xn, wint, gate, xbc, dtr)
    r = _launch(b_ip, [{"xn": st["xn1"][cc], "wint": w["wint"]} for cc in range(NCORE)])
    st["gate"] = [r[cc]["gate"] for cc in range(NCORE)]
    st["xbc"] = [r[cc]["xbc"] for cc in range(NCORE)]
    st["dtr"] = [r[cc]["dtr"] for cc in range(NCORE)]

    def b_cv(b):
        xbc = b.dram("xbc", [4352, T], BF16, kind="ExternalInput")
        cw = b.dram("cw", [P, 34, 4], F32, kind="ExternalInput")
        cb = b.dram("cb", [P, 34], F32, kind="ExternalInput")
        eye = b.dram("eye", [P, P], F32, kind="ExternalInput")
        xtok = b.dram("xtok", [T, 4096], BF16, kind="ExternalOutput")
        bcf = b.dram("bcf", [256, T], BF16, kind="ExternalOutput")
        b.init_consts()
        emit_conv(b, "cv", xbc, cw, cb, eye, xtok, bcf)
    r = _launch(b_cv, [{"xbc": st["xbc"][cc], "cw": w["cw"], "cb": w["cb"],
                        "eye": w["eye"]} for cc in range(NCORE)])
    st["xtok"] = [r[cc]["xtok"] for cc in range(NCORE)]
    st["bcf"] = [r[cc]["bcf"] for cc in range(NCORE)]

    def b_ss(b):
        xtok = b.dram("xtok", [T, 4096], BF16, kind="ExternalInput")
        bcf = b.dram("bcf", [256, T], BF16, kind="ExternalInput")
        dtr = b.dram("dtr", [P, T], F32, kind="ExternalInput")
        a = b.dram("a", [64, 1], F32, kind="ExternalInput")
        dtb = b.dram("dtb", [64, 1], F32, kind="ExternalInput")
        dv = b.dram("dv", [1, 64], F32, kind="ExternalInput")
        masks = b.dram("masks", [P, 4, 512], mybir.dt.uint8, kind="ExternalInput")
        eye = b.dram("eye", [P, P], F32, kind="ExternalInput")
        ytok = b.dram("ytok", [T, 4096], BF16, kind="ExternalOutput")
        b.init_consts()
        emit_ssm(b, "ss", xtok, bcf, dtr, a, dtb, dv, masks, eye, ytok)
    r = _launch(b_ss, [{"xtok": st["xtok"][cc], "bcf": st["bcf"][cc],
                        "dtr": st["dtr"][cc], "a": w["a1A"], "dtb": w["dtb"],
                        "dv": w["a1D"], "masks": w["masks"], "eye": w["eye"]}
                       for cc in range(NCORE)])
    st["ytok"] = [r[cc]["ytok"] for cc in range(NCORE)]

    def b_rm(b):
        ytok = b.dram("ytok", [T, 4096], BF16, kind="ExternalInput")
        gate = b.dram("gate", [T, 4096], BF16, kind="ExternalInput")
        nw = b.dram("nw", [1, 4096], F32, kind="ExternalInput")
        gn = b.dram("gn", [T, 4096], BF16, kind="ExternalOutput")
        b.init_consts()
        emit_rms(b, "rm", ytok, gate, nw, gn)
    r = _launch(b_rm, [{"ytok": st["ytok"][cc], "gate": st["gate"][cc],
                        "nw": w["nw"]} for cc in range(NCORE)])
    st["gn"] = [r[cc]["gn"] for cc in range(NCORE)]

    def b_op(b):
        gn = b.dram("gn", [T, 4096], BF16, kind="ExternalInput")
        wout = b.dram("wout", [4096, D], F32, kind="ExternalInput")
        a1 = b.dram("a1", [D, T], BF16, kind="ExternalOutput")
        b.init_consts()
        emit_outproj(b, "op", gn, wout, a1)
    r = _launch(b_op, [{"gn": st["gn"][cc], "wout": w["wout"]} for cc in range(NCORE)])
    st["a1"] = [r[cc]["a1"] for cc in range(NCORE)]

    def b_f1(b):
        xn = b.dram("xn", [D, T], BF16, kind="ExternalInput")
        rt = b.dram("rt", [D, 8], F32, kind="ExternalInput")
        w1t = b.dram("w1t", [D, 8192], F32, kind="ExternalInput")
        b1 = b.dram("b1", [P, 64], F32, kind="ExternalInput")
        w2t = b.dram("w2t", [8192, D], F32, kind="ExternalInput")
        b2 = b.dram("b2", [1, D], F32, kind="ExternalInput")
        pm = b.dram("pm", [8, 1], F32, kind="ExternalInput")
        iota = b.dram("iota", [1, 8], F32, kind="ExternalInput")
        eye = b.dram("eye", [P, P], F32, kind="ExternalInput")
        h = b.dram("hbuf", [8192, T], BF16)
        out = b.dram("out", [T, D], BF16, kind="ExternalOutput")
        b.init_consts()
        emit_ffn1(b, "f1", xn, rt, w1t, b1, w2t, b2, pm, iota, eye, h, out)
    ims = []
    for cc in range(NCORE):
        pmv = np.zeros((8, 1), np.float32)
        pmv[:cc] = 1.0
        ims.append({"xn": st["xn1"][cc], "rt": w["rt"], "w1t": w["f1w1t"],
                    "b1": w["f1b1"], "w2t": w["f1w2t"], "b2": w["f1b2"],
                    "pm": pmv, "iota": w["iota"], "eye": w["eye"]})
    r = _launch(b_f1, ims)
    st["f1"] = [r[cc]["out"] for cc in range(NCORE)]

    def b_fc(b):
        x1 = b.dram("x1", [D, T], F32, kind="ExternalInput")
        a1 = b.dram("a1", [D, T], BF16, kind="ExternalInput")
        f1 = b.dram("f1", [T, D], BF16, kind="ExternalInput")
        out = b.dram("out", [D, T], F32, kind="ExternalOutput")
        b.init_consts()
        emit_combine(b, "fc", x1, a1, f1, out)
    r = _launch(b_fc, [{"x1": st["x1"][cc], "a1": st["a1"][cc], "f1": st["f1"][cc]}
                       for cc in range(NCORE)])

    out = np.empty((16, TSEQ, D), np.float32)
    for cc in range(NCORE):
        o = np.asarray(r[cc]["out"])
        out[2 * cc] = o[:, 0:TSEQ].T
        out[2 * cc + 1] = o[:, TSEQ:].T
    return out




def build_fused(b, dbg=False):
    xd = b.dram("x", [D, T], F32, kind="ExternalInput")
    wq = b.dram("wq", [D, D], F32, kind="ExternalInput")
    wk = b.dram("wk", [D, 1024], F32, kind="ExternalInput")
    wv = b.dram("wv", [D, 1024], F32, kind="ExternalInput")
    wg0 = b.dram("wg0", [D, 8], F32, kind="ExternalInput")
    wc = b.dram("wc", [D, D], F32, kind="ExternalInput")
    bc = b.dram("bc", [P, DC], F32, kind="ExternalInput")
    fg = b.dram("fg", [D, 8], F32, kind="ExternalInput")
    fbias = b.dram("fbias", [1, 8], F32, kind="ExternalInput")
    w1t = b.dram("w1t", [8, D, 1024], F32, kind="ExternalInput")
    w3t = b.dram("w3t", [8, D, 1024], F32, kind="ExternalInput")
    w2t = b.dram("w2t", [8, 1024, D], F32, kind="ExternalInput")
    wint = b.dram("wint", [D, 8512], F32, kind="ExternalInput")
    cw = b.dram("cw", [P, 34, 4], F32, kind="ExternalInput")
    cb = b.dram("cb", [P, 34], F32, kind="ExternalInput")
    a1A = b.dram("a1A", [64, 1], F32, kind="ExternalInput")
    dtb = b.dram("dtb", [64, 1], F32, kind="ExternalInput")
    a1D = b.dram("a1D", [1, 64], F32, kind="ExternalInput")
    nw = b.dram("nw", [1, 4096], F32, kind="ExternalInput")
    wout = b.dram("wout", [4096, D], F32, kind="ExternalInput")
    rt = b.dram("rt", [D, 8], F32, kind="ExternalInput")
    f1w1t = b.dram("f1w1t", [D, 8192], F32, kind="ExternalInput")
    f1b1 = b.dram("f1b1", [P, 64], F32, kind="ExternalInput")
    f1w2t = b.dram("f1w2t", [8192, D], F32, kind="ExternalInput")
    f1b2 = b.dram("f1b2", [1, D], F32, kind="ExternalInput")
    pm = b.dram("pm", [8, 1], F32, kind="ExternalInput")
    masks = b.dram("masks", [P, 4, 512], mybir.dt.uint8, kind="ExternalInput")
    eye = b.dram("eye", [P, P], F32, kind="ExternalInput")
    iota = b.dram("iota", [1, 8], F32, kind="ExternalInput")
    out = b.dram("out", [D, T], F32, kind="ExternalOutput")

    kd = "ExternalOutput" if dbg else None
    xn0 = b.dram("i_xn0", [D, T], BF16, kind=kd)
    yhat = b.dram("i_yhat", [D, T], BF16, kind=kd)
    a0 = b.dram("i_a0", [D, T], BF16, kind=kd)
    f0 = b.dram("i_f0", [T, D], BF16, kind=kd)
    x1 = b.dram("i_x1", [D, T], F32, kind=kd)
    xn1 = b.dram("i_xn1", [D, T], BF16, kind=kd)
    gate = b.dram("i_gate", [T, 4096], BF16, kind=kd)
    xbc = b.dram("i_xbc", [4352, T], BF16, kind=kd)
    dtr = b.dram("i_dtr", [P, T], F32, kind=kd)
    xtok = b.dram("i_xtok", [T, 4096], BF16, kind=kd)
    bcf = b.dram("i_bcf", [256, T], BF16, kind=kd)
    ytok = b.dram("i_ytok", [T, 4096], BF16, kind=kd)
    gn = b.dram("i_gn", [T, 4096], BF16, kind=kd)
    a1 = b.dram("i_a1", [D, T], BF16, kind=kd)
    f1 = b.dram("i_f1", [T, D], BF16, kind=kd)
    hbuf = b.dram("i_hbuf", [8192, T], BF16)

    b.init_consts()
    emit_ln(b, "ln0", xd, xn0)
    emit_attn0(b, "at0", xn0, wq, wk, wv, wg0, masks, eye, yhat)
    emit_wc(b, "wc", yhat, wc, bc, a0)
    emit_ffn0(b, "f0", xn0, fg, fbias, iota, w1t, w3t, w2t, f0)
    emit_ln(b, "c1", x1, xn1, x_store=(xd, a0, f0))
    emit_inproj(b, "ip", xn1, wint, gate, xbc, dtr)
    emit_conv(b, "cv", xbc, cw, cb, eye, xtok, bcf)
    emit_ssm(b, "ss", xtok, bcf, dtr, a1A, dtb, a1D, masks, eye, ytok)
    emit_rms(b, "rm", ytok, gate, nw, gn)
    emit_outproj(b, "op", gn, wout, a1)
    emit_ffn1(b, "f1", xn1, rt, f1w1t, f1b1, f1w2t, f1b2, pm, iota, eye, hbuf, f1)
    emit_combine(b, "fc", x1, a1, f1, out)


def kernel_fused(inputs):
    w = _prep_weights(inputs)
    x = np.asarray(inputs["L0_attn_hidden_states"], np.float32)
    b = B()
    build_fused(b)
    nc = b.finish()
    ims = []
    for cc in range(NCORE):
        pmv = np.zeros((8, 1), np.float32)
        pmv[:cc] = 1.0
        ims.append({
            "x": np.ascontiguousarray(np.concatenate([x[2 * cc].T, x[2 * cc + 1].T], axis=1)),
            "wq": w["wq"], "wk": w["wk"], "wv": w["wv"], "wg0": w["wg0"],
            "wc": w["wc"], "bc": w["bc"], "fg": w["fg"], "fbias": w["fbias"],
            "w1t": w["w1t"], "w3t": w["w3t"], "w2t": w["w2t"], "wint": w["wint"],
            "cw": w["cw"], "cb": w["cb"], "a1A": w["a1A"], "dtb": w["dtb"],
            "a1D": w["a1D"], "nw": w["nw"], "wout": w["wout"], "rt": w["rt"],
            "f1w1t": w["f1w1t"], "f1b1": w["f1b1"], "f1w2t": w["f1w2t"],
            "f1b2": w["f1b2"], "pm": pmv, "masks": w["masks"], "eye": w["eye"],
            "iota": w["iota"]})
    res = run_bass_kernel_spmd(nc, ims, core_ids=list(range(NCORE)))
    out = np.empty((16, TSEQ, D), np.float32)
    for cc in range(NCORE):
        o = np.asarray(res.results[cc]["out"])
        out[2 * cc] = o[:, 0:TSEQ].T
        out[2 * cc + 1] = o[:, TSEQ:].T
    return out


# version 04
# speedup vs baseline: 1.2336x; 1.2336x over previous
import numpy as np
import ml_dtypes
from contextlib import ExitStack

import concourse.bass as bass
import concourse.mybir as mybir
from concourse.bass import IndirectOffsetOnAxis
from concourse.bass_utils import run_bass_kernel_spmd

F32 = mybir.dt.float32
BF16 = mybir.dt.bfloat16
I32 = mybir.dt.int32
U32 = mybir.dt.uint32
AF = mybir.ActivationFunctionType
ALU = mybir.AluOpType
AX = mybir.AxisListType

P = 128
D = 2048
DC = D // P
TSEQ = 2048
NSEQ = 2
T = TSEQ * NSEQ
NCORE = 8

GATE_OFFSET = 6.906768
ATT_SCALE = 1.0 / (128.0 ** 0.5)
BF = ml_dtypes.bfloat16


def fmv(ap):
    return ap.rearrange("(c p) t -> p c t", p=P)


class B:

    def __init__(self):
        self.nc = bass.Bass("TRN2", target_bir_lowering=False)
        self.ctx = ExitStack()
        self._semv = {}
        self.ones_row = None
        self.ones_col = None
        self.onesf_row = None

    def sb(self, name, shape, dt):
        return self.ctx.enter_context(self.nc.sbuf_tensor(name, shape, dt))

    def psum(self, name, shape, dt=F32):
        return self.ctx.enter_context(self.nc.psum_tensor(name, shape, dt))

    def sem(self, name):
        h = self.ctx.enter_context(self.nc.semaphore(name))
        self._semv[name] = 0
        return h

    def dram(self, name, shape, dt, kind=None):
        if kind:
            t = self.nc.dram_tensor(name, shape, dt, kind=kind)
        else:
            t = self.nc.dram_tensor(name, shape, dt)
        return t.ap()

    def block(self):
        return self.nc.Block(no_gpsimd_drain=True)

    def init_consts(self):
        self.ones_row = self.sb("c_ones_row", [1, P], BF16)
        self.ones_col = self.sb("c_ones_col", [P, 1], BF16)
        self.onesf_row = self.sb("c_onesf_row", [1, P], F32)
        self.onesf_col = self.sb("c_onesf_col", [P, 1], F32)
        with self.block() as block:
            @block.vector
            def _(v):
                v.memset(self.ones_row[:], 1.0)
                v.memset(self.ones_col[:], 1.0)
                v.memset(self.onesf_row[:], 1.0)
                v.memset(self.onesf_col[:], 1.0)

    def finish(self):
        self.ctx.close()
        return self.nc




def emit_mm_fm(b, name, wt_dram, x_sb, kdim, mdim, tdim, evac=None,
               tchunk=512, x_kc0=0, tok0=0, out_dram=None, bias_sb=None,
               act_fn=None, wt_f32=False, out_dt=BF16, out_tok0=0):
    nc = b.nc
    kc = kdim // P
    mc = (mdim + P - 1) // P
    ntc = (tdim + tchunk - 1) // tchunk
    with ExitStack() as ctx:
        wdt = F32 if wt_f32 else BF16
        wb = [ctx.enter_context(nc.sbuf_tensor(f"{name}_w{i}", [P, kc, P], wdt))
              for i in range(2)]
        pt = [ctx.enter_context(nc.psum_tensor(f"{name}_p{i}", [P, tchunk], F32))
              for i in range(2)]
        stage = None
        if out_dram is not None:
            stage = [ctx.enter_context(nc.sbuf_tensor(f"{name}_st{i}", [P, tchunk], out_dt))
                     for i in range(3)]
            odst = fmv(out_dram)
        s_w = b.sem(f"{name}_sw")
        s_f = b.sem(f"{name}_sf")
        s_m = b.sem(f"{name}_sm")
        s_e = b.sem(f"{name}_se")
        s_o = b.sem(f"{name}_so")
        wsrc = wt_dram.rearrange("(c p) m -> p c m", p=P)
        with b.block() as block:
            @block.gpsimd
            def _(g):
                for m in range(mc):
                    mw = min(P, mdim - m * P)
                    if m >= 2:
                        g.wait_ge(s_m, (m - 1) * ntc)
                    g.dma_start(wb[m % 2][:, :, 0:mw],
                                wsrc[:, :, m * P:m * P + mw]).then_inc(s_w, 16)

            @block.tensor
            def _(t):
                cnt = 0
                for m in range(mc):
                    mw = min(P, mdim - m * P)
                    t.wait_ge(s_w, 16 * (m + 1))
                    for tc in range(ntc):
                        t0 = tc * tchunk
                        tw = min(tchunk, tdim - t0)
                        cnt += 1
                        if cnt > 2:
                            t.wait_ge(s_e, cnt - 2)
                        for k in range(kc):
                            ins = t.matmul(pt[(cnt - 1) % 2][0:mw, 0:tw],
                                           wb[m % 2][:, k, 0:mw],
                                           x_sb[:, x_kc0 + k, tok0 + t0:tok0 + t0 + tw],
                                           start=(k == 0), stop=(k == kc - 1))
                        ins.then_inc(s_m, 1)

            @block.scalar
            def _(s):
                cnt = 0
                for m in range(mc):
                    mw = min(P, mdim - m * P)
                    for tc in range(ntc):
                        t0 = tc * tchunk
                        tw = min(tchunk, tdim - t0)
                        cnt += 1
                        s.wait_ge(s_m, cnt)
                        if out_dram is not None:
                            if cnt > 3:
                                s.wait_ge(s_o, 16 * (cnt - 3))
                            ins = s.activation(out=stage[(cnt - 1) % 3][0:mw, 0:tw],
                                               in_=pt[(cnt - 1) % 2][0:mw, 0:tw],
                                               func=(act_fn or AF.Identity),
                                               bias=(bias_sb[:, m:m + 1] if bias_sb is not None else 0.0))
                        else:
                            ins = evac(s, pt[(cnt - 1) % 2][0:mw, 0:tw], m, t0, tw)
                        ins.then_inc(s_e, 1)

            if out_dram is not None:
                @block.sync
                def _(g):
                    cnt = 0
                    for m in range(mc):
                        mw = min(P, mdim - m * P)
                        for tc in range(ntc):
                            t0 = tc * tchunk
                            tw = min(tchunk, tdim - t0)
                            cnt += 1
                            g.wait_ge(s_e, cnt)
                            g.dma_start(odst[0:mw, m, out_tok0 + t0:out_tok0 + t0 + tw],
                                        stage[(cnt - 1) % 3][0:mw, 0:tw]).then_inc(s_o, 16)
                    g.wait_ge(s_o, 16 * cnt)




def emit_ln(b, name, x_dram, xn_dram, tdim=T, eps=1e-5, x_store=None):
    nc = b.nc
    SW = 512
    nst = tdim // SW
    with ExitStack() as ctx:
        def csb(n, shape, dt):
            return ctx.enter_context(nc.sbuf_tensor(f"{name}_{n}", shape, dt))
        xs = csb("x", [P, DC, SW], F32)
        xadd = csb("xa", [P, DC, SW], F32) if x_store else None
        xbf = csb("bf", [P, DC, SW], BF16)
        xsq = csb("sq", [P, DC, SW], BF16)
        stat = csb("st", [1, 2, tdim], F32)
        rstd = csb("rs", [1, tdim], F32)
        mu = csb("mu", [1, tdim], F32)
        pbs = csb("pb", [P, 2, SW], F32)
        xno = csb("xn", [P, DC, SW], BF16)
        epsb = csb("eps", [1, 1], F32)
        pstat = ctx.enter_context(nc.psum_tensor(f"{name}_ps", [1, 2, SW], F32))
        pbc = ctx.enter_context(nc.psum_tensor(f"{name}_pbc", [P, 2, SW], F32))
        s_d = b.sem(f"{name}_sd")
        sdv = 0

        with b.block() as block:
            @block.vector
            def _(v):
                v.memset(epsb[:], float(eps))

        for i in range(nst):
            with b.block() as block:
                @block.sync
                def _(sy):
                    nonlocal sdv
                    sy.dma_start(xs[:], fmv(x_dram if not x_store else x_store[0])[:, :, i * SW:(i + 1) * SW]).then_inc(s_d, 16)
                    sdv += 16
                    if x_store:
                        for k in range(DC):
                            sy.dma_start(xbf[:, k, :],
                                         x_store[2][i * SW:(i + 1) * SW, k * P:(k + 1) * P],
                                         transpose=True).then_inc(s_d, 16)
                        sdv += 16 * DC
                    sy.wait_ge(s_d, sdv)
                if x_store:
                    @block.gpsimd
                    def _(g):
                        nonlocal sdv
                        g.dma_start(xadd[:], fmv(x_store[1])[:, :, i * SW:(i + 1) * SW]).then_inc(s_d, 16)
                        sdv += 16
                        g.wait_ge(s_d, sdv)
            if x_store:
                with b.block() as block:
                    @block.vector
                    def _(v):
                        for k in range(0, DC, 2):
                            v.tensor_tensor(out=xs[:, k, :], in0=xs[:, k, :], in1=xadd[:, k, :], op=ALU.add)
                            v.tensor_tensor(out=xs[:, k, :], in0=xs[:, k, :], in1=xbf[:, k, :], op=ALU.add)
                    @block.gpsimd
                    def _(g):
                        for k in range(1, DC, 2):
                            g.tensor_tensor(out=xs[:, k, :], in0=xs[:, k, :], in1=xadd[:, k, :], op=ALU.add)
                            g.tensor_tensor(out=xs[:, k, :], in0=xs[:, k, :], in1=xbf[:, k, :], op=ALU.add)
                with b.block() as block:
                    @block.sync
                    def _(sy):
                        nonlocal sdv
                        sy.dma_start(fmv(x_dram)[:, :, i * SW:(i + 1) * SW], xs[:]).then_inc(s_d, 16)
                        sdv += 16
                        sy.wait_ge(s_d, sdv)
            with b.block() as block:
                @block.vector
                def _(v):
                    v.tensor_copy(xbf[:], xs[:])
                @block.scalar
                def _(s):
                    s.activation(out=xsq[:], in_=xs[:], func=AF.Square)
            with b.block() as block:
                @block.tensor
                def _(t):
                    for tc in range(SW // 512):
                        sl = slice(tc * 512, tc * 512 + 512)
                        for k in range(DC):
                            t.matmul(pstat[0:1, 0, sl], b.ones_col[:], xbf[:, k, sl],
                                     start=(k == 0), stop=(k == DC - 1))
                        for k in range(DC):
                            t.matmul(pstat[0:1, 1, sl], b.ones_col[:], xsq[:, k, sl],
                                     start=(k == 0), stop=(k == DC - 1))
            with b.block() as block:
                @block.scalar
                def _(s):
                    s.copy(stat[0:1, :, i * SW:(i + 1) * SW], pstat[0:1, :, :])

        with b.block() as block:
            @block.vector
            def _(v):
                v.tensor_scalar(out=mu[:], in0=stat[0:1, 0, :], scalar1=1.0 / D, scalar2=None, op0=ALU.mult)
                v.tensor_tensor(out=rstd[:], in0=mu[:], in1=mu[:], op=ALU.mult)
        with b.block() as block:
            @block.vector
            def _(v):
                v.tensor_scalar(out=stat[0:1, 1, :], in0=stat[0:1, 1, :], scalar1=1.0 / D,
                                scalar2=None, op0=ALU.mult)
                v.tensor_tensor(out=rstd[:], in0=stat[0:1, 1, :], in1=rstd[:], op=ALU.subtract)
        with b.block() as block:
            @block.scalar
            def _(s):
                s.activation(out=rstd[:], in_=rstd[:], func=AF.Sqrt, bias=epsb[:])
        with b.block() as block:
            @block.vector
            def _(v):
                v.reciprocal(out=rstd[:], in_=rstd[:])

        for i in range(nst):
            with b.block() as block:
                @block.sync
                def _(sy):
                    nonlocal sdv
                    sy.dma_start(xs[:], fmv(x_dram)[:, :, i * SW:(i + 1) * SW]).then_inc(s_d, 16)
                    sdv += 16
                    sy.wait_ge(s_d, sdv)
                @block.tensor
                def _(t):
                    for tc in range(SW // 512):
                        sl = slice(tc * 512, tc * 512 + 512)
                        gl = slice(i * SW + tc * 512, i * SW + tc * 512 + 512)
                        t.matmul(pbc[:, 0, sl], b.onesf_row[:], mu[0:1, gl], start=True, stop=True)
                        t.matmul(pbc[:, 1, sl], b.onesf_row[:], rstd[0:1, gl], start=True, stop=True)
            with b.block() as block:
                @block.scalar
                def _(s):
                    s.copy(pbs[:], pbc[:])
            with b.block() as block:
                @block.vector
                def _(v):
                    for k in range(0, DC, 2):
                        v.tensor_tensor(out=xs[:, k, :], in0=xs[:, k, :], in1=pbs[:, 0, :], op=ALU.subtract)
                        v.tensor_tensor(out=xno[:, k, :], in0=xs[:, k, :], in1=pbs[:, 1, :], op=ALU.mult)
                @block.gpsimd
                def _(g):
                    for k in range(1, DC, 2):
                        g.tensor_tensor(out=xs[:, k, :], in0=xs[:, k, :], in1=pbs[:, 0, :], op=ALU.subtract)
                        g.tensor_tensor(out=xno[:, k, :], in0=xs[:, k, :], in1=pbs[:, 1, :], op=ALU.mult)
            with b.block() as block:
                @block.sync
                def _(sy):
                    nonlocal sdv
                    sy.dma_start(fmv(xn_dram)[:, :, i * SW:(i + 1) * SW], xno[:]).then_inc(s_d, 16)
                    sdv += 16
                    sy.wait_ge(s_d, sdv)





def emit_attn0(b, name, xn_dram, wq_dram, wk_dram, wv_dram, wg_dram,
               masks_dram, eye_dram, yhat_dram, dbg=None):
    nc = b.nc
    LNS = float(np.log(ATT_SCALE))
    EPS_SC = float(np.sqrt(1e-6 / (ATT_SCALE * ATT_SCALE)))
    ctx = ExitStack()
    with ctx:
        def csb(n, shape, dt):
            return ctx.enter_context(nc.sbuf_tensor(f"{name}_{n}", shape, dt))

        def cps(n, shape, dt=F32):
            return ctx.enter_context(nc.psum_tensor(f"{name}_{n}", shape, dt))

        xn = csb("xn", [P, DC, TSEQ], BF16)
        wgt = csb("wg", [P, DC, 8], BF16)
        wqt = csb("wq", [P, DC, 256], BF16)
        wkt = csb("wk", [P, DC, P], BF16)
        wvt = csb("wv", [P, DC, P], BF16)
        qsb = csb("q", [P, 2, TSEQ], BF16)
        ksb = csb("k", [P, TSEQ], BF16)
        vtmp = csb("vt", [P, TSEQ], BF16)
        vtok = csb("vk", [P, DC, 132], BF16)
        logg = csb("lg", [8, TSEQ], F32)
        logG = csb("lG", [8, TSEQ], F32)
        svh = csb("sv", [8, TSEQ], F32)
        scr8 = csb("s8", [8, TSEQ], F32)
        svT = csb("sT", [P, DC, 4, 8], F32)
        epT = csb("eT", [P, DC, 8], F32)
        refc = csb("rc", [8, 1], F32)
        goff = csb("go", [8, 1], F32)
        one8 = csb("o8", [8, 1], F32)
        eyeb = csb("eyb", [P, P], BF16)
        eyef = csb("eyf", [P, P], F32)
        masks = csb("mk", [P, 4, 512], mybir.dt.uint8)
        zeros = csb("z", [P, 512], BF16)
        abuf = csb("a", [P, 2, 512], BF16)
        ambuf = csb("am", [P, 2, 512], BF16)
        ncol4 = csb("ncl", [P, 4], F32)
        rcol4 = csb("rl", [P, 4], F32)
        ytok4 = csb("yk", [P, 4, P], BF16)
        ystage = csb("ys", [P, 2, TSEQ], BF16)
        dbgbuf = csb("db", [P, 4, 132], F32) if dbg is not None else None
        dbgfin = csb("df", [P, 8], F32) if dbg is not None else None

        psc = [cps(f"sc{i}", [P, 512]) for i in range(2)]
        pyy = [cps(f"y{i}", [P, 132]) for i in range(4)]
        ptr = cps("tr", [P, P], BF16)
        g8s = csb("g8s", [8, TSEQ], F32)

        s_d = b.sem(f"{name}_d")
        s_sc = b.sem(f"{name}_sc")
        s_aq = b.sem(f"{name}_aq")
        s_am = b.sem(f"{name}_am")
        s_ym = b.sem(f"{name}_ym")
        s_fv = b.sem(f"{name}_fv")
        s_tr = b.sem(f"{name}_tr")
        s_ys = b.sem(f"{name}_ys")
        s_o = b.sem(f"{name}_o")
        c = dict(d=0, sc=0, aq=0, am=0, ym=0, fv=0, tr=0, ys=0, o=0)

        with b.block() as block:
            @block.vector
            def _(v):
                v.memset(goff[:], GATE_OFFSET)
                v.memset(one8[:], 1.0)
                v.memset(zeros[:], 0.0)
                v.memset(vtok[:], 0.0)
            @block.gpsimd
            def _(g):
                g.dma_start(masks[:], masks_dram).then_inc(s_d, 16)
                g.dma_start(wgt[:], wg_dram.rearrange("(c p) m -> p c m", p=P)).then_inc(s_d, 16)
                g.dma_start(eyeb[:], eye_dram).then_inc(s_d, 16)
                g.dma_start(eyef[:], eye_dram).then_inc(s_d, 16)
                g.wait_ge(s_d, c["d"] + 64)
        c["d"] += 64

        for seq in range(NSEQ):
            tok0 = seq * TSEQ
            with b.block() as block:
                @block.sync
                def _(sy):
                    sy.dma_start(xn[:], fmv(xn_dram)[:, :, tok0:tok0 + TSEQ]).then_inc(s_d, 16)
                    sy.wait_ge(s_d, c["d"] + 16)
            c["d"] += 16

            with b.block() as block:
                @block.tensor
                def _(t):
                    for tc in range(TSEQ // 512):
                        sl = slice(tc * 512, tc * 512 + 512)
                        if tc >= 2:
                            t.wait_ge(s_aq, c["aq"] + tc - 1)
                        for k in range(DC):
                            ins = t.matmul(psc[tc % 2][0:8, :], wgt[:, k, :], xn[:, k, sl],
                                           start=(k == 0), stop=(k == DC - 1))
                        ins.then_inc(s_sc, 1)
                @block.scalar
                def _(s):
                    for tc in range(TSEQ // 512):
                        sl = slice(tc * 512, tc * 512 + 512)
                        s.wait_ge(s_sc, c["sc"] + tc + 1)
                        s.copy(g8s[:, sl], psc[tc % 2][0:8, :]).then_inc(s_aq, 1)
            c["sc"] += 4
            c["aq"] += 4
            with b.block() as block:
                @block.scalar
                def _(s):
                    s.activation(out=svh[:], in_=g8s[:], func=AF.Abs, bias=goff[:])
                @block.vector
                def _(v):
                    v.tensor_scalar(out=logg[:], in0=g8s[:], scalar1=GATE_OFFSET,
                                    scalar2=0.0, op0=ALU.add, op1=ALU.min)
            with b.block() as block:
                @block.scalar
                def _(s):
                    s.activation(out=svh[:], in_=svh[:], func=AF.Exp, scale=-1.0)
            with b.block() as block:
                @block.scalar
                def _(s):
                    s.activation(out=scr8[:], in_=svh[:], func=AF.Ln, bias=one8[:])
            with b.block() as block:
                @block.vector
                def _(v):
                    v.tensor_tensor(out=logg[:], in0=logg[:], in1=scr8[:], op=ALU.subtract)
            with b.block() as block:
                @block.vector
                def _(v):
                    v.tensor_tensor_scan(out=logG[:], data0=logg[:], data1=logg[:],
                                         initial=0.0, op0=ALU.add, op1=ALU.bypass)
            if dbg is not None:
                with b.block() as block:
                    @block.sync
                    def _(sy):
                        sy.dma_start(dbg["logG"][0:8, tok0:tok0 + TSEQ], logG[:]).then_inc(s_o, 16)
                        sy.wait_ge(s_o, c["o"] + 16)
                c["o"] += 16

            for j in range(4):
                send = 512 * (j + 1)
                with b.block() as block:
                    @block.vector
                    def _(v):
                        v.tensor_scalar(out=refc[:], in0=logG[:, 512 * j:512 * j + 1],
                                        scalar1=0.5, scalar2=LNS, op0=ALU.mult, op1=ALU.add)
                        v.tensor_copy(scr8[:, 0:256], logG[:, 0:256])
                        v.tensor_copy(scr8[:, 0:256], logG[:, 0:256])
                        v.tensor_copy(scr8[:, 0:256], logG[:, 0:256])
                        v.tensor_copy(scr8[:, 0:256], logG[:, 0:256])
                with b.block() as block:
                    @block.scalar
                    def _(s):
                        s.activation(out=svh[0:8, 0:send], in_=logG[0:8, 0:send],
                                     func=AF.Exp, scale=-0.5, bias=refc[:])
                with b.block() as block:
                    @block.tensor
                    def _(t):
                        for st in range(4 * (j + 1)):
                            if st >= 1:
                                t.wait_ge(s_ys, c["ys"] + st)
                            t.transpose(psc[1][:, 0:8], svh[0:8, st * P:(st + 1) * P],
                                        eyef[0:8, 0:8]).then_inc(s_tr, 1)
                    @block.scalar
                    def _(s):
                        for st in range(4 * (j + 1)):
                            s.wait_ge(s_tr, c["tr"] + st + 1)
                            s.copy(svT[:, st, j, :], psc[1][:, 0:8]).then_inc(s_ys, 1)
                c["tr"] += 4 * (j + 1)
                c["ys"] += 4 * (j + 1)
                with b.block() as block:
                    @block.scalar
                    def _(s):
                        for qt in range(4 * j, 4 * j + 4):
                            s.activation(out=epT[:, qt, :], in_=svT[:, qt, j, :],
                                         func=AF.Square, scale=EPS_SC)

            if dbg is not None and seq == 0:
                with b.block() as block:
                    @block.sync
                    def _(sy):
                        sy.dma_start(dbg["svT"], svT[:]).then_inc(s_o, 16)
                        sy.dma_start(dbg["epT"], epT[:]).then_inc(s_o, 16)
                        sy.wait_ge(s_o, c["o"] + 32)
                c["o"] += 32

            for h in range(8):
                with b.block() as block:
                    @block.gpsimd
                    def _(g):
                        g.dma_start(wqt[:], wq_dram.rearrange("(c p) m -> p c m", p=P)[:, :, 256 * h:256 * h + 256]).then_inc(s_d, 16)
                        g.dma_start(wkt[:], wk_dram.rearrange("(c p) m -> p c m", p=P)[:, :, P * h:P * h + P]).then_inc(s_d, 16)
                        g.dma_start(wvt[:], wv_dram.rearrange("(c p) m -> p c m", p=P)[:, :, P * h:P * h + P]).then_inc(s_d, 16)
                        g.wait_ge(s_d, c["d"] + 48)
                c["d"] += 48

                groups = []
                for qh in range(2):
                    for tc in range(4):
                        groups.append(("q", qh, tc))
                for tc in range(4):
                    groups.append(("k", 0, tc))
                for tc in range(4):
                    groups.append(("v", 0, tc))
                with b.block() as block:
                    @block.tensor
                    def _(t):
                        for i, (kind, qh, tc) in enumerate(groups):
                            sl = slice(tc * 512, tc * 512 + 512)
                            if i >= 2:
                                t.wait_ge(s_aq, c["aq"] + i - 1)
                            for k in range(DC):
                                lh = (wqt[:, k, 128 * qh:128 * qh + 128] if kind == "q"
                                      else (wkt[:, k, :] if kind == "k" else wvt[:, k, :]))
                                ins = t.matmul(psc[i % 2][:, :], lh, xn[:, k, sl],
                                               start=(k == 0), stop=(k == DC - 1))
                            ins.then_inc(s_sc, 1)
                    @block.scalar
                    def _(s):
                        for i, (kind, qh, tc) in enumerate(groups):
                            sl = slice(tc * 512, tc * 512 + 512)
                            s.wait_ge(s_sc, c["sc"] + i + 1)
                            dst = (qsb[:, qh, sl] if kind == "q"
                                   else (ksb[:, sl] if kind == "k" else vtmp[:, sl]))
                            s.copy(dst, psc[i % 2][:, :]).then_inc(s_aq, 1)
                c["sc"] += len(groups)
                c["aq"] += len(groups)

                with b.block() as block:
                    @block.tensor
                    def _(t):
                        for st in range(DC):
                            if st >= 1:
                                t.wait_ge(s_ys, c["ys"] + st)
                            t.transpose(ptr[:], vtmp[:, st * P:(st + 1) * P],
                                        eyeb[:]).then_inc(s_tr, 1)
                    @block.scalar
                    def _(s):
                        for st in range(DC):
                            s.wait_ge(s_tr, c["tr"] + st + 1)
                            s.copy(vtok[:, st, 0:P], ptr[:]).then_inc(s_ys, 1)
                    @block.vector
                    def _(v):
                        v.memset(vtok[:, :, P:P + 1], 1.0)
                c["tr"] += DC
                c["ys"] += DC
                if dbg is not None and seq == 0 and h == 0:
                    with b.block() as block:
                        @block.sync
                        def _(sy):
                            sy.dma_start(dbg["q"], qsb[:]).then_inc(s_o, 16)
                            sy.dma_start(dbg["k"], ksb[:]).then_inc(s_o, 16)
                            sy.dma_start(dbg["vtok"], vtok[:]).then_inc(s_o, 16)
                            sy.wait_ge(s_o, c["o"] + 48)
                    c["o"] += 48

                for qh in range(2):
                    for j in range(4):
                        ns = 4 * (j + 1)
                        aq0, am0, ym0, sc0 = c["aq"], c["am"], c["ym"], c["fv"]
                        diag_upto = [0] * ns
                        nmk = 0
                        for st in range(ns):
                            if st >= 4 * j:
                                nmk += 1
                            diag_upto[st] = nmk
                        with b.block() as block:
                            @block.tensor
                            def _(t):
                                def ymms(st):
                                    kmin = max(0, st - 4 * j)
                                    src = ambuf if st >= 4 * j else abuf
                                    if st >= 4 * j:
                                        t.wait_ge(s_am, c["am"] + diag_upto[st])
                                    else:
                                        t.wait_ge(s_aq, c["aq"] + st + 1)
                                    for k in range(kmin, 4):
                                        ins = t.matmul(pyy[k][:, 0:129],
                                                       src[:, st % 2, k * P:(k + 1) * P],
                                                       vtok[:, st, 0:129],
                                                       start=(st == 0), stop=(st == 4 * j + k))
                                    ins.then_inc(s_ym, 1)
                                for st in range(ns):
                                    if st >= 2:
                                        t.wait_ge(s_aq, c["aq"] + st - 1)
                                    t.matmul(psc[st % 2][:, :],
                                             ksb[:, st * P:(st + 1) * P],
                                             qsb[:, qh, j * 512:(j + 1) * 512],
                                             start=True, stop=True).then_inc(s_sc, 1)
                                    if st >= 1:
                                        ymms(st - 1)
                                ymms(ns - 1)

                            @block.scalar
                            def _(s):
                                for st in range(ns):
                                    s.wait_ge(s_sc, c["sc"] + st + 1)
                                    if st >= 2:
                                        s.wait_ge(s_ym, c["ym"] + st - 1)
                                    s.activation(out=abuf[:, st % 2, :], in_=psc[st % 2][:, :],
                                                 func=AF.Square,
                                                 scale=svT[:, st, j, h:h + 1]
                                                 ).then_inc(s_aq, 1)

                            @block.vector
                            def _(v):
                                for st in range(4 * j, ns):
                                    v.wait_ge(s_aq, c["aq"] + st + 1)
                                    if st >= 4 * j + 2:
                                        v.wait_ge(s_ym, c["ym"] + st - 1)
                                    v.select(out=ambuf[:, st % 2, :], mask=masks[:, st - 4 * j, :],
                                             on_true=abuf[:, st % 2, :], on_false=zeros[:],
                                             add_drain=True).then_inc(s_am, 1)
                        c["sc"] += ns
                        c["aq"] += ns
                        c["am"] += nmk
                        c["ym"] += ns

                        if dbg is not None and seq == 0 and h == 0 and qh == 0 and j == dbg.get("j", 0):
                            with b.block() as block:
                                @block.scalar
                                def _(s):
                                    for k in range(4):
                                        s.copy(dbgbuf[:, k, :], pyy[k][:, 0:132])
                            with b.block() as block:
                                @block.sync
                                def _(sy):
                                    sy.dma_start(dbg["pyy"], dbgbuf[:]).then_inc(s_o, 16)
                                    sy.dma_start(dbg["ab"], abuf[:, (4 * (j + 1) - 1) % 2, :]).then_inc(s_o, 16)
                                    sy.dma_start(dbg["amb"], ambuf[:, (4 * (j + 1) - 1) % 2, :]).then_inc(s_o, 16)
                                    sy.wait_ge(s_o, c["o"] + 48)
                            c["o"] += 48

                        with b.block() as block:
                            dbg_this = dbg is not None and seq == 0 and h == 0 and qh == 0 and j == dbg.get("j", 0)

                            @block.vector
                            def _(v):
                                for k in range(4):
                                    v.tensor_tensor(out=ncol4[:, k:k + 1], in0=pyy[k][:, 128:129],
                                                    in1=epT[:, 4 * j + k, h:h + 1], op=ALU.add)
                                v.tensor_copy(rcol4[:, 0:1], ncol4[:, 3:4])
                                for k in range(4):
                                    v.reciprocal(out=rcol4[:, k:k + 1], in_=ncol4[:, k:k + 1])
                                if dbg_this:
                                    v.tensor_copy(dbgfin[:, 0:4], rcol4[:])
                                    v.tensor_copy(dbgfin[:, 4:8], ncol4[:])
                                for k in range(4):
                                    v.tensor_scalar(out=ytok4[:, k, :], in0=pyy[k][:, 0:P],
                                                    scalar1=rcol4[:, k:k + 1], scalar2=None,
                                                    op0=ALU.mult).then_inc(s_fv, 1)
                            @block.tensor
                            def _(t):
                                for k in range(4):
                                    t.wait_ge(s_fv, c["fv"] + k + 1)
                                    if k >= 1:
                                        t.wait_ge(s_ys, c["ys"] + k)
                                    t.transpose(ptr[:], ytok4[:, k, :], eyeb[:]).then_inc(s_tr, 1)
                            @block.scalar
                            def _(s):
                                for k in range(4):
                                    qt = 4 * j + k
                                    s.wait_ge(s_tr, c["tr"] + k + 1)
                                    s.copy(ystage[:, qh, qt * P:(qt + 1) * P],
                                           ptr[:]).then_inc(s_ys, 1)
                        c["fv"] += 4
                        c["tr"] += 4
                        c["ys"] += 4
                        if dbg is not None and seq == 0 and h == 0 and qh == 0 and j == dbg.get("j", 0):
                            with b.block() as block:
                                @block.sync
                                def _(sy):
                                    sy.dma_start(dbg["fin"], dbgfin[:]).then_inc(s_o, 16)
                                    sy.wait_ge(s_o, c["o"] + 16)
                            c["o"] += 16

                with b.block() as block:
                    @block.sync
                    def _(sy):
                        for qh in range(2):
                            sy.dma_start(fmv(yhat_dram)[:, 2 * h + qh, tok0:tok0 + TSEQ],
                                         ystage[:, qh, :]).then_inc(s_o, 16)
                        sy.wait_ge(s_o, c["o"] + 32)
                c["o"] += 32




def emit_load_fm(b, name, src_dram, dst_sb, kc, tdim):
    s_d = b.sem(f"{name}_ld")
    with b.block() as block:
        @block.sync
        def _(sy):
            sy.dma_start(dst_sb[:, 0:kc, 0:tdim],
                         src_dram.rearrange("(c p) t -> p c t", p=P)).then_inc(s_d, 16)
            sy.wait_ge(s_d, 16)




def emit_ffn0(b, name, xn_dram, wgt_dram, bias_dram, iota_dram,
              w1t_dram, w3t_dram, w2t_dram, out_dram):
    nc = b.nc
    HT = T // 2
    NTT = HT // P
    ctx = ExitStack()
    with ctx:
        def csb(n, shape, dt):
            return ctx.enter_context(nc.sbuf_tensor(f"{name}_{n}", shape, dt))

        def cps(n, shape, dt=F32):
            return ctx.enter_context(nc.psum_tensor(f"{name}_{n}", shape, dt))

        xn = csb("xn", [P, DC, HT], BF16)
        acc = csb("acc", [P, NTT, D], BF16)
        gbuf = csb("g", [P, 8, HT], BF16)
        wgt = csb("wg", [P, DC, 8], BF16)
        w13 = [csb(f"w13_{i}", [P, DC, 2, P], BF16) for i in range(2)]
        w2c = [csb(f"w2_{i}", [P, 8, 512], BF16) for i in range(2)]
        s1st = csb("s1", [P, 2, 512], BF16)
        biasb = csb("bb", [P, 8], F32)
        iotab = csb("io", [P, 8], F32)
        lg = csb("lg", [P, NTT, 8], F32)
        mx8 = csb("mx", [P, NTT, 8], F32)
        mi8 = csb("mi", [P, NTT, 8], U32)
        e12 = csb("e12", [P, NTT, 2], F32)
        oh1 = csb("oh1", [P, NTT, 8], F32)
        oh2 = csb("oh2", [P, NTT, 8], F32)
        wv = csb("wv", [P, NTT, 2], F32)
        ba12 = csb("ba", [P, NTT, 2], F32)
        rs = csb("rs", [P, NTT, 1], F32)
        ew = csb("ew", [P, NTT, 8], F32)
        ps = [cps(f"p{i}", [P, 512]) for i in range(4)]

        s_d = b.sem(f"{name}_d")
        s_m = b.sem(f"{name}_m")
        s_a = b.sem(f"{name}_a")
        s_v = b.sem(f"{name}_v")
        s_o = b.sem(f"{name}_o")
        c = dict(d=0, m=0, a=0, v=0, o=0)

        with b.block() as block:
            @block.gpsimd
            def _(g):
                g.dma_start(wgt[:], wgt_dram.rearrange("(c p) m -> p c m", p=P)).then_inc(s_d, 16)
                g.dma_start(biasb[0:1, :], bias_dram).then_inc(s_d, 16)
                g.dma_start(iotab[0:1, :], iota_dram).then_inc(s_d, 16)
                g.wait_ge(s_d, c["d"] + 48)
        c["d"] += 48
        with b.block() as block:
            @block.tensor
            def _(t):
                t.matmul(ps[0][:, 0:8], b.onesf_row[:], biasb[0:1, :], start=True, stop=True)
                t.matmul(ps[1][:, 0:8], b.onesf_row[:], iotab[0:1, :], start=True, stop=True)
        with b.block() as block:
            @block.scalar
            def _(s):
                s.copy(biasb[:], ps[0][:, 0:8])
                s.copy(iotab[:], ps[1][:, 0:8])
                s.copy(s1st[:, 0, :], s1st[:, 0, :])
                s.copy(s1st[:, 0, :], s1st[:, 0, :])

        for half in range(2):
            tok0 = half * HT
            with b.block() as block:
                @block.sync
                def _(sy):
                    sy.dma_start(xn[:], fmv(xn_dram)[:, :, tok0:tok0 + HT]).then_inc(s_d, 16)
                    sy.wait_ge(s_d, c["d"] + 16)
                @block.vector
                def _(v):
                    v.memset(acc[:], 0.0)
            c["d"] += 16

            with b.block() as block:
                @block.tensor
                def _(t):
                    for tt in range(NTT):
                        if tt >= 2:
                            t.wait_ge(s_a, c["a"] + tt - 1)
                        for k in range(DC):
                            ins = t.matmul(ps[tt % 2][0:P, 0:8],
                                           xn[:, k, tt * P:(tt + 1) * P],
                                           wgt[:, k, :], start=(k == 0), stop=(k == DC - 1))
                        ins.then_inc(s_m, 1)
                @block.scalar
                def _(s):
                    for tt in range(NTT):
                        s.wait_ge(s_m, c["m"] + tt + 1)
                        s.activation(out=lg[:, tt, :], in_=ps[tt % 2][0:P, 0:8],
                                     func=AF.Sigmoid).then_inc(s_a, 1)
            c["m"] += NTT
            c["a"] += NTT
            with b.block() as block:
                @block.vector
                def _(v):
                    for tt in range(NTT):
                        v.tensor_tensor(out=lg[:, tt, :], in0=lg[:, tt, :], in1=biasb[:], op=ALU.add)
                    for tt in range(NTT):
                        v.max(out=mx8[:, tt, :], in_=lg[:, tt, :])
            with b.block() as block:
                @block.vector
                def _(v):
                    for tt in range(NTT):
                        v.max_index(out=mi8[:, tt, :], in_max=mx8[:, tt, :], in_values=lg[:, tt, :])
            with b.block() as block:
                @block.vector
                def _(v):
                    for tt in range(NTT):
                        v.tensor_copy(e12[:, tt, :], mi8[:, tt, 0:2])
                    for tt in range(NTT):
                        v.tensor_scalar(out=oh1[:, tt, :], in0=iotab[:], scalar1=e12[:, tt, 0:1],
                                        scalar2=None, op0=ALU.is_equal)
                    for tt in range(NTT):
                        v.tensor_scalar(out=oh2[:, tt, :], in0=iotab[:], scalar1=e12[:, tt, 1:2],
                                        scalar2=None, op0=ALU.is_equal)
                    for tt in range(NTT):
                        v.tensor_tensor(out=ew[:, tt, :], in0=oh1[:, tt, :], in1=biasb[:], op=ALU.mult)
                    for tt in range(NTT):
                        v.reduce_sum(out=ba12[:, tt, 0:1], in_=ew[:, tt, :], axis=AX.X)
                    for tt in range(NTT):
                        v.tensor_tensor(out=ew[:, tt, :], in0=oh2[:, tt, :], in1=biasb[:], op=ALU.mult)
                    for tt in range(NTT):
                        v.reduce_sum(out=ba12[:, tt, 1:2], in_=ew[:, tt, :], axis=AX.X)
                    for tt in range(NTT):
                        v.tensor_tensor(out=wv[:, tt, :], in0=mx8[:, tt, 0:2], in1=ba12[:, tt, :],
                                        op=ALU.subtract)
                    for tt in range(NTT):
                        v.reduce_sum(out=rs[:, tt, 0:1], in_=wv[:, tt, :], axis=AX.X)
                    for tt in range(NTT):
                        v.reciprocal(out=rs[:, tt, 0:1], in_=rs[:, tt, 0:1])
                    for tt in range(NTT):
                        v.tensor_scalar(out=wv[:, tt, :], in0=wv[:, tt, :], scalar1=rs[:, tt, 0:1],
                                        scalar2=None, op0=ALU.mult)
                    for tt in range(NTT):
                        v.tensor_scalar(out=ew[:, tt, :], in0=oh1[:, tt, :], scalar1=wv[:, tt, 0:1],
                                        scalar2=None, op0=ALU.mult)
                    for tt in range(NTT):
                        v.scalar_tensor_tensor(out=ew[:, tt, :], in0=oh2[:, tt, :],
                                               scalar=wv[:, tt, 1:2], in1=ew[:, tt, :],
                                               op0=ALU.mult, op1=ALU.add)

            for e in range(8):
                with b.block() as block:
                    @block.gpsimd
                    def _(g):
                        for fc in range(8):
                            if fc >= 2:
                                g.wait_ge(s_m, c["m"] + 4 * (fc - 1))
                            g.dma_start(w13[fc % 2][:, :, 0, :],
                                        w1t_dram[e].rearrange("(c p) m -> p c m", p=P)[:, :, fc * P:(fc + 1) * P]).then_inc(s_d, 16)
                            g.dma_start(w13[fc % 2][:, :, 1, :],
                                        w3t_dram[e].rearrange("(c p) m -> p c m", p=P)[:, :, fc * P:(fc + 1) * P]).then_inc(s_d, 16)
                    @block.tensor
                    def _(t):
                        n = 0
                        for fc in range(8):
                            t.wait_ge(s_d, c["d"] + 32 * (fc + 1))
                            for tc in range(4):
                                sl = slice(tc * 512, tc * 512 + 512)
                                if n >= 2:
                                    t.wait_ge(s_v, c["v"] + n - 1)
                                for k in range(DC):
                                    t.matmul(ps[2 * (n % 2)][:, :], w13[fc % 2][:, k, 0, :],
                                             xn[:, k, sl], start=(k == 0), stop=(k == DC - 1))
                                for k in range(DC):
                                    ins = t.matmul(ps[2 * (n % 2) + 1][:, :], w13[fc % 2][:, k, 1, :],
                                                   xn[:, k, sl], start=(k == 0), stop=(k == DC - 1))
                                ins.then_inc(s_m, 1)
                                n += 1
                    @block.scalar
                    def _(s):
                        n = 0
                        for fc in range(8):
                            for tc in range(4):
                                n += 1
                                s.wait_ge(s_m, c["m"] + n)
                                if n >= 3:
                                    s.wait_ge(s_v, c["v"] + n - 2)
                                s.activation(out=s1st[:, (n - 1) % 2, :], in_=ps[2 * ((n - 1) % 2)][:, :],
                                             func=AF.Silu).then_inc(s_a, 1)
                    @block.vector
                    def _(v):
                        n = 0
                        for fc in range(8):
                            for tc in range(4):
                                sl = slice(tc * 512, tc * 512 + 512)
                                n += 1
                                v.wait_ge(s_a, c["a"] + n)
                                v.tensor_tensor(out=gbuf[:, fc, sl], in0=s1st[:, (n - 1) % 2, :],
                                                in1=ps[2 * ((n - 1) % 2) + 1][:, :],
                                                op=ALU.mult).then_inc(s_v, 1)
                c["d"] += 32 * 8
                c["m"] += 32
                c["a"] += 32
                c["v"] += 32

                with b.block() as block:
                    @block.gpsimd
                    def _(g):
                        for hc in range(4):
                            if hc >= 2:
                                g.wait_ge(s_m, c["m"] + NTT * (hc - 1))
                            g.dma_start(w2c[hc % 2][:],
                                        w2t_dram[e].rearrange("(c p) m -> p c m", p=P)[:, :, hc * 512:(hc + 1) * 512]).then_inc(s_d, 16)
                    @block.tensor
                    def _(t):
                        n = 0
                        for hc in range(4):
                            t.wait_ge(s_d, c["d"] + 16 * (hc + 1))
                            for tt in range(NTT):
                                if n >= 2:
                                    t.wait_ge(s_v, c["v"] + n - 1)
                                for fc in range(8):
                                    ins = t.matmul(ps[n % 2][:, :], gbuf[:, fc, tt * P:(tt + 1) * P],
                                                   w2c[hc % 2][:, fc, :], start=(fc == 0), stop=(fc == 7))
                                ins.then_inc(s_m, 1)
                                n += 1
                    @block.vector
                    def _(v):
                        n = 0
                        for hc in range(4):
                            for tt in range(NTT):
                                n += 1
                                v.wait_ge(s_m, c["m"] + n)
                                v.scalar_tensor_tensor(out=acc[:, tt, hc * 512:(hc + 1) * 512],
                                                       in0=ps[(n - 1) % 2][:, :],
                                                       scalar=ew[:, tt, e:e + 1],
                                                       in1=acc[:, tt, hc * 512:(hc + 1) * 512],
                                                       op0=ALU.mult, op1=ALU.add).then_inc(s_v, 1)
                c["d"] += 16 * 4
                c["m"] += 4 * NTT
                c["v"] += 4 * NTT

            with b.block() as block:
                @block.sync
                def _(sy):
                    sy.dma_start(out_dram.rearrange("(n p) h -> p n h", p=P)[:, half * NTT:(half + 1) * NTT, :],
                                 acc[:]).then_inc(s_o, 16)
                    sy.wait_ge(s_o, c["o"] + 16)
            c["o"] += 16




def emit_mm_tok(b, name, wt_dram, x_sb, kdim, mdim, tdim, out_dram,
                act_fn=None, out_dt=BF16, mchunk=512):
    nc = b.nc
    kc = kdim // P
    moc = mdim // mchunk
    ntt = tdim // P
    with ExitStack() as ctx:
        wb = [ctx.enter_context(nc.sbuf_tensor(f"{name}_w{i}", [P, kc, mchunk], BF16))
              for i in range(2)]
        pt = [ctx.enter_context(nc.psum_tensor(f"{name}_p{i}", [P, mchunk], F32))
              for i in range(2)]
        stage = [ctx.enter_context(nc.sbuf_tensor(f"{name}_st{i}", [P, mchunk], out_dt))
                 for i in range(3)]
        s_w = b.sem(f"{name}_sw")
        s_m = b.sem(f"{name}_sm")
        s_e = b.sem(f"{name}_se")
        s_o = b.sem(f"{name}_so")
        wsrc = wt_dram.rearrange("(c p) m -> p c m", p=P)
        odst = out_dram.rearrange("(n p) m -> p n m", p=P)
        with b.block() as block:
            @block.gpsimd
            def _(g):
                for oc in range(moc):
                    if oc >= 2:
                        g.wait_ge(s_m, (oc - 1) * ntt)
                    g.dma_start(wb[oc % 2][:],
                                wsrc[:, :, oc * mchunk:(oc + 1) * mchunk]).then_inc(s_w, 16)

            @block.tensor
            def _(t):
                cnt = 0
                for oc in range(moc):
                    t.wait_ge(s_w, 16 * (oc + 1))
                    for tt in range(ntt):
                        cnt += 1
                        if cnt > 2:
                            t.wait_ge(s_e, cnt - 2)
                        for k in range(kc):
                            ins = t.matmul(pt[(cnt - 1) % 2][:, :],
                                           x_sb[:, k, tt * P:(tt + 1) * P],
                                           wb[oc % 2][:, k, :],
                                           start=(k == 0), stop=(k == kc - 1))
                        ins.then_inc(s_m, 1)

            @block.scalar
            def _(s):
                cnt = 0
                for oc in range(moc):
                    for tt in range(ntt):
                        cnt += 1
                        s.wait_ge(s_m, cnt)
                        if cnt > 3:
                            s.wait_ge(s_o, 16 * (cnt - 3))
                        s.activation(out=stage[(cnt - 1) % 3][:, :],
                                     in_=pt[(cnt - 1) % 2][:, :],
                                     func=(act_fn or AF.Identity)).then_inc(s_e, 1)

            @block.sync
            def _(sy):
                cnt = 0
                for oc in range(moc):
                    for tt in range(ntt):
                        cnt += 1
                        sy.wait_ge(s_e, cnt)
                        sy.dma_start(odst[:, tt, oc * mchunk:(oc + 1) * mchunk],
                                     stage[(cnt - 1) % 3][:, :]).then_inc(s_o, 16)
                sy.wait_ge(s_o, 16 * cnt)




def emit_inproj(b, name, xn_dram, wint_dram, gate_dram, xbc_dram, dtraw_dram):
    nc = b.nc
    with ExitStack() as ctx:
        xnsb = ctx.enter_context(nc.sbuf_tensor(f"{name}_xn", [P, DC, T], BF16))
        s_d = b.sem(f"{name}_ld")
        with b.block() as block:
            @block.sync
            def _(sy):
                sy.dma_start(xnsb[:], fmv(xn_dram)).then_inc(s_d, 16)
                sy.wait_ge(s_d, 16)
        emit_mm_tok(b, f"{name}_g", wint_dram[:, 0:4096], xnsb, D, 4096, T, gate_dram)
        emit_mm_fm(b, f"{name}_x", wint_dram[:, 4096:8448], xnsb, D, 4352, T,
                   out_dram=xbc_dram)
        emit_mm_fm(b, f"{name}_d", wint_dram[:, 8448:8512], xnsb, D, 64, T,
                   out_dram=dtraw_dram, out_dt=F32)




def emit_conv(b, name, xbc_dram, convw_dram, convb_dram, eye_dram,
              xtok_dram, bcfm_dram):
    nc = b.nc
    NCH = 34
    ctx = ExitStack()
    with ctx:
        def csb(n, shape, dt):
            return ctx.enter_context(nc.sbuf_tensor(f"{name}_{n}", shape, dt))
        xp = csb("xp", [P, NCH, TSEQ + 3], BF16)
        cw = csb("cw", [P, NCH, 4], F32)
        cb = csb("cb", [P, NCH], F32)
        t1 = csb("t1", [P, TSEQ], F32)
        t2 = csb("t2", [P, TSEQ], F32)
        acc = csb("ac", [P, TSEQ], F32)
        sil = csb("si", [P, TSEQ], BF16)
        eyeb = csb("ey", [P, P], BF16)
        stg = csb("sg", [P, 2, P], BF16)
        ptr = ctx.enter_context(nc.psum_tensor(f"{name}_tr", [P, P], BF16))
        s_d = b.sem(f"{name}_d")
        s_t = b.sem(f"{name}_t")
        s_a = b.sem(f"{name}_a")
        s_o = b.sem(f"{name}_o")
        c = dict(d=0, t=0, a=0, o=0)
        with b.block() as block:
            @block.gpsimd
            def _(g):
                g.dma_start(cw[:], convw_dram).then_inc(s_d, 16)
                g.dma_start(cb[:], convb_dram).then_inc(s_d, 16)
                g.dma_start(eyeb[:], eye_dram).then_inc(s_d, 16)
                g.wait_ge(s_d, 48)
        c["d"] = 48
        xsrc = xbc_dram.rearrange("(c p) t -> p c t", p=P)
        xdst = xtok_dram.rearrange("(n p) m -> p n m", p=P)
        bdst = bcfm_dram.rearrange("(c p) t -> p c t", p=P)
        for seq in range(NSEQ):
            tok0 = seq * TSEQ
            with b.block() as block:
                @block.vector
                def _(v):
                    v.memset(xp[:, :, 0:3], 0.0)
                @block.sync
                def _(sy):
                    sy.dma_start(xp[:, :, 3:3 + TSEQ],
                                 xsrc[:, :, tok0:tok0 + TSEQ]).then_inc(s_d, 16)
                    sy.wait_ge(s_d, c["d"] + 16)
            c["d"] += 16
            for ch in range(NCH):
                with b.block() as block:
                    @block.vector
                    def _(v):
                        v.tensor_scalar(out=acc[:], in0=xp[:, ch, 3:3 + TSEQ],
                                        scalar1=cw[:, ch, 3:4], scalar2=cb[:, ch:ch + 1],
                                        op0=ALU.mult, op1=ALU.add)
                with b.block() as block:
                    @block.vector
                    def _(v):
                        v.scalar_tensor_tensor(out=acc[:], in0=xp[:, ch, 2:2 + TSEQ],
                                               scalar=cw[:, ch, 2:3], in1=acc[:],
                                               op0=ALU.mult, op1=ALU.add)
                with b.block() as block:
                    @block.vector
                    def _(v):
                        v.scalar_tensor_tensor(out=acc[:], in0=xp[:, ch, 1:1 + TSEQ],
                                               scalar=cw[:, ch, 1:2], in1=acc[:],
                                               op0=ALU.mult, op1=ALU.add)
                with b.block() as block:
                    @block.vector
                    def _(v):
                        v.scalar_tensor_tensor(out=acc[:], in0=xp[:, ch, 0:TSEQ],
                                               scalar=cw[:, ch, 0:1], in1=acc[:],
                                               op0=ALU.mult, op1=ALU.add)
                with b.block() as block:
                    @block.scalar
                    def _(s):
                        s.activation(out=sil[:], in_=acc[:], func=AF.Silu)
                if ch < 32:
                    with b.block() as block:
                        @block.tensor
                        def _(t):
                            for st in range(DC):
                                if st >= 1:
                                    t.wait_ge(s_a, c["a"] + st)
                                t.transpose(ptr[:], sil[:, st * P:(st + 1) * P],
                                            eyeb[:]).then_inc(s_t, 1)
                        @block.scalar
                        def _(s):
                            for st in range(DC):
                                s.wait_ge(s_t, c["t"] + st + 1)
                                if st >= 2:
                                    s.wait_ge(s_o, c["o"] + 16 * (st - 1))
                                s.copy(stg[:, st % 2, :], ptr[:]).then_inc(s_a, 1)
                        @block.sync
                        def _(sy):
                            for st in range(DC):
                                sy.wait_ge(s_a, c["a"] + st + 1)
                                sy.dma_start(xdst[:, seq * DC + st, ch * P:(ch + 1) * P],
                                             stg[:, st % 2, :]).then_inc(s_o, 16)
                            sy.wait_ge(s_o, c["o"] + 16 * DC)
                    c["t"] += DC
                    c["a"] += DC
                    c["o"] += 16 * DC
                else:
                    with b.block() as block:
                        @block.sync
                        def _(sy):
                            sy.dma_start(bdst[:, ch - 32, tok0:tok0 + TSEQ],
                                         sil[:]).then_inc(s_o, 16)
                            sy.wait_ge(s_o, c["o"] + 16)
                    c["o"] += 16




def emit_ssm(b, name, xtok_dram, bcfm_dram, dtraw_dram, a_dram, dtb_dram,
             dv_dram, masks_dram, eye_dram, ytok_dram):
    nc = b.nc
    NH, HP = 64, 64
    NCH = DC
    ctx = ExitStack()
    with ctx:
        def csb(n, shape, dt):
            return ctx.enter_context(nc.sbuf_tensor(f"{name}_{n}", shape, dt))

        def cps(n, shape, dt=F32):
            return ctx.enter_context(nc.psum_tensor(f"{name}_{n}", shape, dt))

        xt = csb("xt", [P, NCH, 4096], BF16)
        bc = csb("bc", [P, 2, TSEQ], BF16)
        dt_ = csb("dt", [NH, TSEQ], F32)
        t1_ = csb("t1", [NH, TSEQ], F32)
        dacs = csb("dc", [NH, TSEQ], F32)
        dlast = csb("dl", [NH, NCH], F32)
        cde = csb("cd", [NH, NCH], F32)
        abuf = csb("ab", [NH, 1], F32)
        dtbb = csb("db", [NH, 1], F32)
        one64 = csb("o64", [NH, 1], F32)
        ndacsT = csb("nT", [P, NCH, NH], F32)
        eT = csb("eT", [P, NCH, NH], F32)
        dtT = csb("tT", [P, NCH, NH], F32)
        wtT = csb("wT", [P, NCH, NH], F32)
        eyeb = csb("ey", [P, P], BF16)
        eyef = csb("eyf", [P, P], F32)
        mask1 = csb("mk", [P, P], mybir.dt.uint8)
        dvb = csb("dv", [1, NH], F32)
        dbc = csb("dbc", [P, NH], F32)
        ddiag = csb("ddg", [P, NH, P], BF16)
        sbf = csb("sbf", [P, NH, HP], BF16)
        btok = csb("bt", [P, P], BF16)
        cbm = csb("cbm", [P, P], BF16)
        cdb = csb("cdb", [P, NH], F32)
        srow = csb("sr", [1, 2, P], F32)
        ebuf = csb("eb", [P, 2, P], BF16)
        embuf = csb("em", [P, 2, P], BF16)
        wbuf = csb("wb", [P, 2, P], BF16)
        btil = csb("bl", [P, 2, P], BF16)
        cdey = csb("ce", [P, 2, P], BF16)
        yibuf = csb("yi", [P, 2, HP], BF16)
        zeros = csb("z", [P, P], BF16)

        pdb = [cps(f"db{i}", [P, P]) for i in range(2)]
        pya = [cps(f"ya{i}", [P, HP]) for i in range(2)]
        pyb = [cps(f"yb{i}", [P, HP]) for i in range(2)]
        psc = [cps(f"sc{i}", [P, HP]) for i in range(2)]

        s_d = b.sem(f"{name}_d")
        s_pe = b.sem(f"{name}_pe")
        s_ac = b.sem(f"{name}_ac")
        s_dv = b.sem(f"{name}_dv")
        s_o = b.sem(f"{name}_o")
        c = dict(d=0, pe=0, ac=0, dv=0, o=0)

        with b.block() as block:
            @block.gpsimd
            def _(g):
                g.dma_start(abuf[:], a_dram).then_inc(s_d, 16)
                g.dma_start(dtbb[:], dtb_dram).then_inc(s_d, 16)
                g.dma_start(dvb[:], dv_dram).then_inc(s_d, 16)
                g.dma_start(eyeb[:], eye_dram).then_inc(s_d, 16)
                g.dma_start(eyef[:], eye_dram).then_inc(s_d, 16)
                g.dma_start(mask1[:], masks_dram[:, 0, 0:P]).then_inc(s_d, 16)
                g.wait_ge(s_d, 96)
            @block.vector
            def _(v):
                v.memset(one64[:], 1.0)
                v.memset(zeros[:], 0.0)
        c["d"] = 96
        with b.block() as block:
            @block.tensor
            def _(t):
                t.matmul(pdb[0][:, 0:NH], b.onesf_row[:], dvb[:], start=True, stop=True)
        with b.block() as block:
            @block.scalar
            def _(s):
                s.copy(dbc[:], pdb[0][:, 0:NH])
        with b.block() as block:
            @block.vector
            def _(v):
                for h in range(NH):
                    v.tensor_scalar(out=ddiag[:, h, :], in0=eyeb[:], scalar1=dbc[:, h:h + 1],
                                    scalar2=None, op0=ALU.mult)

        for seq in range(NSEQ):
            tok0 = seq * TSEQ
            with b.block() as block:
                @block.sync
                def _(sy):
                    sy.dma_start(xt[:], xtok_dram.rearrange("(n p) m -> p n m", p=P)[:, seq * NCH:(seq + 1) * NCH, :]).then_inc(s_d, 16)
                    sy.dma_start(bc[:], bcfm_dram.rearrange("(c p) t -> p c t", p=P)[:, :, tok0:tok0 + TSEQ]).then_inc(s_d, 16)
                    sy.dma_start(dt_[:], dtraw_dram.rearrange("(c p) t -> p c t", p=P)[0:NH, 0, tok0:tok0 + TSEQ]).then_inc(s_d, 16)
                    sy.wait_ge(s_d, c["d"] + 48)
                @block.vector
                def _(v):
                    v.memset(sbf[:], 0.0)
            c["d"] += 48

            with b.block() as block:
                @block.vector
                def _(v):
                    v.tensor_scalar(out=dt_[:], in0=dt_[:], scalar1=dtbb[:], scalar2=None, op0=ALU.add)
            with b.block() as block:
                @block.scalar
                def _(s):
                    s.activation(out=t1_[:], in_=dt_[:], func=AF.Abs)
            with b.block() as block:
                @block.scalar
                def _(s):
                    s.activation(out=t1_[:], in_=t1_[:], func=AF.Exp, scale=-1.0)
            with b.block() as block:
                @block.scalar
                def _(s):
                    s.activation(out=t1_[:], in_=t1_[:], func=AF.Ln, bias=one64[:])
            with b.block() as block:
                @block.vector
                def _(v):
                    v.tensor_scalar(out=dt_[:], in0=dt_[:], scalar1=0.0, scalar2=None, op0=ALU.max)
            with b.block() as block:
                @block.vector
                def _(v):
                    v.tensor_tensor(out=dt_[:], in0=dt_[:], in1=t1_[:], op=ALU.add)
            with b.block() as block:
                @block.vector
                def _(v):
                    v.tensor_scalar(out=t1_[:], in0=dt_[:], scalar1=abuf[:], scalar2=None, op0=ALU.mult)
            with b.block() as block:
                @block.vector
                def _(v):
                    for ch in range(NCH):
                        v.tensor_tensor_scan(out=dacs[:, ch * P:(ch + 1) * P],
                                             data0=t1_[:, ch * P:(ch + 1) * P],
                                             data1=t1_[:, ch * P:(ch + 1) * P],
                                             initial=0.0, op0=ALU.add, op1=ALU.bypass)
            with b.block() as block:
                @block.vector
                def _(v):
                    v.tensor_copy(dlast[:], dacs[:].rearrange("h (c p) -> h c p", p=P)[:, :, P - 1:P])
                    v.tensor_copy(t1_[:, 0:512], dacs[:, 0:512])
                    v.tensor_copy(t1_[:, 0:512], dacs[:, 0:512])
                    v.tensor_copy(t1_[:, 0:512], dacs[:, 0:512])
                    v.tensor_copy(t1_[:, 0:512], dacs[:, 0:512])
            with b.block() as block:
                @block.scalar
                def _(s):
                    for ch in range(NCH):
                        s.activation(out=t1_[:, ch * P:(ch + 1) * P],
                                     in_=dacs[:, ch * P:(ch + 1) * P],
                                     func=AF.Exp, scale=-1.0, bias=dlast[:, ch:ch + 1])
            with b.block() as block:
                @block.scalar
                def _(s):
                    s.activation(out=cde[:], in_=dlast[:], func=AF.Exp)
                @block.vector
                def _(v):
                    v.tensor_tensor(out=t1_[:], in0=t1_[:], in1=dt_[:], op=ALU.mult)
            with b.block() as block:
                @block.tensor
                def _(t):
                    for ch in range(NCH):
                        for which in range(3):
                            if not (ch == 0 and which == 0):
                                t.wait_ge(s_ac, c["ac"] + 3 * ch + which)
                            src = [dacs, dt_, t1_][which]
                            t.transpose(pdb[0][:, 0:NH], src[:, ch * P:(ch + 1) * P],
                                        eyef[0:NH, 0:NH]).then_inc(s_pe, 1)

                @block.scalar
                def _(s):
                    for ch in range(NCH):
                        for which in range(3):
                            s.wait_ge(s_pe, c["pe"] + 3 * ch + which + 1)
                            if which == 0:
                                s.activation(out=ndacsT[:, ch, :], in_=pdb[0][:, 0:NH],
                                             func=AF.Identity, scale=-1.0).then_inc(s_ac, 1)
                            else:
                                dst = [None, dtT, wtT][which]
                                s.copy(dst[:, ch, :], pdb[0][:, 0:NH]).then_inc(s_ac, 1)
            c["pe"] += 3 * NCH
            c["ac"] += 3 * NCH
            with b.block() as block:
                @block.scalar
                def _(s):
                    s.activation(out=eT[:], in_=ndacsT[:], func=AF.Exp, scale=-1.0)

            for ch in range(NCH):
                with b.block() as block:
                    @block.tensor
                    def _(t):
                        tsl = slice(ch * P, (ch + 1) * P)
                        t.transpose(pdb[0].bitcast(BF16)[:, 0:P], bc[:, 0, tsl], eyeb[:]).then_inc(s_pe, 1)
                        t.wait_ge(s_ac, c["ac"] + 1)
                        t.matmul(pdb[0][:, :], bc[:, 0, tsl], bc[:, 1, tsl],
                                 start=True, stop=True).then_inc(s_pe, 1)
                        t.transpose(psc[0][0:1, 0:NH], cde[:, ch:ch + 1],
                                    eyef[0:NH, 0:NH]).then_inc(s_pe, 1)
                        t.wait_ge(s_ac, c["ac"] + 2)
                        t.matmul(psc[1][:, 0:NH], b.onesf_row[:], srow[0:1, 0, 0:NH],
                                 start=True, stop=True).then_inc(s_pe, 1)
                    @block.scalar
                    def _(s):
                        s.wait_ge(s_pe, c["pe"] + 1)
                        s.copy(btok[:], pdb[0].bitcast(BF16)[:, 0:P]).then_inc(s_ac, 1)
                        s.wait_ge(s_pe, c["pe"] + 2)
                        s.copy(srow[0:1, 0, 0:NH], psc[0][0:1, 0:NH]).then_inc(s_ac, 1)
                        s.wait_ge(s_pe, c["pe"] + 4)
                        s.copy(cdb[:], psc[1][:, 0:NH]).then_inc(s_ac, 1)
                        s.copy(eT[:, 0, :], eT[:, 0, :])
                        s.copy(eT[:, 0, :], eT[:, 0, :])
                        s.copy(eT[:, 0, :], eT[:, 0, :])
                    @block.vector
                    def _(v):
                        v.wait_ge(s_pe, c["pe"] + 2)
                        v.select(out=cbm[:], mask=mask1[:], on_true=pdb[0][:, :],
                                 on_false=zeros[:], add_drain=True).then_inc(s_dv, 1)
                c["pe"] += 4
                c["ac"] += 3
                c["dv"] += 1

                pe0, ac0, dv0 = c["pe"], c["ac"], c["dv"]
                with b.block() as block:
                    @block.tensor
                    def _(t):
                        for h in range(NH):
                            if h >= 2:
                                t.wait_ge(s_ac, ac0 + 4 * h - 4)
                            t.transpose(pdb[h % 2][0:1, 0:P], ndacsT[:, ch, h:h + 1],
                                        eyef[:]).then_inc(s_pe, 1)
                            t.wait_ge(s_ac, ac0 + 4 * h + 1)
                            t.matmul(pdb[h % 2][:, :], b.onesf_row[:], srow[0:1, h % 2, :],
                                     start=True, stop=True).then_inc(s_pe, 1)
                            t.wait_ge(s_dv, dv0 + 5 * h + 4)
                            xsl = xt[:, ch, HP * h:HP * h + HP]
                            t.matmul(pya[h % 2][:, 0:HP], wbuf[:, h % 2, :], xsl,
                                     start=True, stop=False)
                            t.matmul(pya[h % 2][:, 0:HP], ddiag[:, h, :], xsl,
                                     start=False, stop=True).then_inc(s_pe, 1)
                            t.matmul(psc[h % 2][:, 0:HP], btil[:, h % 2, :], xsl,
                                     start=True, stop=False)
                            t.matmul(psc[h % 2][:, 0:HP], cdey[:, h % 2, :], sbf[:, h, :],
                                     start=False, stop=True).then_inc(s_pe, 1)
                            t.matmul(pyb[h % 2][:, 0:HP], bc[:, 1, ch * P:(ch + 1) * P],
                                     sbf[:, h, :], start=True, stop=True).then_inc(s_pe, 1)
                    @block.scalar
                    def _(s):
                        for h in range(NH):
                            s.wait_ge(s_pe, pe0 + 5 * h + 1)
                            s.copy(srow[0:1, h % 2, :], pdb[h % 2][0:1, 0:P]).then_inc(s_ac, 1)
                            s.wait_ge(s_pe, pe0 + 5 * h + 2)
                            if h >= 2:
                                s.wait_ge(s_dv, dv0 + 5 * (h - 2) + 1)
                            s.activation(out=ebuf[:, h % 2, :], in_=pdb[h % 2][:, :],
                                         func=AF.Exp, scale=-1.0, bias=ndacsT[:, ch, h:h + 1]).then_inc(s_ac, 1)
                            s.wait_ge(s_pe, pe0 + 5 * h + 5)
                            if h >= 2:
                                s.wait_ge(s_dv, dv0 + 5 * (h - 2) + 5)
                            s.activation(out=yibuf[:, h % 2, :], in_=pyb[h % 2][:, 0:HP],
                                         func=AF.Identity, scale=eT[:, ch, h:h + 1]).then_inc(s_ac, 1)
                            s.copy(sbf[:, h, :], psc[h % 2][:, 0:HP]).then_inc(s_ac, 1)
                    @block.vector
                    def _(v):
                        for h in range(NH):
                            v.wait_ge(s_ac, ac0 + 4 * h + 2)
                            if h >= 2:
                                v.wait_ge(s_pe, pe0 + 5 * (h - 2) + 5)
                            v.tensor_scalar(out=embuf[:, h % 2, :], in0=ebuf[:, h % 2, :],
                                            scalar1=1.0, scalar2=None, op0=ALU.min).then_inc(s_dv, 1)
                            v.tensor_scalar(out=btil[:, h % 2, :], in0=btok[:],
                                            scalar1=wtT[:, ch, h:h + 1], scalar2=None,
                                            op0=ALU.mult).then_inc(s_dv, 1)
                            v.tensor_scalar(out=cdey[:, h % 2, :], in0=eyeb[:],
                                            scalar1=cdb[:, h:h + 1], scalar2=None,
                                            op0=ALU.mult).then_inc(s_dv, 1)
                            v.scalar_tensor_tensor(out=wbuf[:, h % 2, :], in0=cbm[:],
                                                   scalar=dtT[:, ch, h:h + 1], in1=embuf[:, h % 2, :],
                                                   op0=ALU.mult, op1=ALU.mult).then_inc(s_dv, 1)
                            v.wait_ge(s_ac, ac0 + 4 * h + 3)
                            v.tensor_tensor(out=xt[:, ch, HP * h:HP * h + HP],
                                            in0=yibuf[:, h % 2, :], in1=pya[h % 2][:, 0:HP],
                                            op=ALU.add).then_inc(s_dv, 1)
                c["pe"] += 5 * NH
                c["ac"] += 4 * NH
                c["dv"] += 5 * NH

            with b.block() as block:
                @block.sync
                def _(sy):
                    sy.dma_start(ytok_dram.rearrange("(n p) m -> p n m", p=P)[:, seq * NCH:(seq + 1) * NCH, :],
                                 xt[:]).then_inc(s_o, 16)
                    sy.wait_ge(s_o, c["o"] + 16)
            c["o"] += 16




NCHTOT = 32


def emit_ssm2(b, name, xtok_dram, bcfm_dram, dtraw_dram, a_dram, dtb_dram,
              dv_dram, masks_dram, eye_dram, ytok_dram, dbg_halves=None,
              dbg_skip_chunks=False, dbg_level=3):
    nc = b.nc
    NH, HP, Q = 64, 64, 128
    NHF = 8
    NPC = 16
    HT = 1024
    dq_dram = b.dram(f"{name}_dqd", [2, NH, HT], BF16)
    sel_dram = b.dram(f"{name}_sel1", [1, 128], F32)
    ctx = ExitStack()
    with ctx:
        def csb(n, shape, dt):
            return ctx.enter_context(nc.sbuf_tensor(f"{name}_{n}", shape, dt))

        def cps(n, shape, dt=F32):
            return ctx.enter_context(nc.psum_tensor(f"{name}_{n}", shape, dt))

        xt = csb("xt", [P, NHF, 4096], BF16)
        bc = csb("bc", [P, 2, TSEQ], BF16)
        dt_ = csb("dt", [NH, HT], F32)
        t1_ = csb("t1", [NH, HT], F32)
        dacs = csb("dc", [NH, HT], F32)
        mdl = csb("md", [NH, HT], F32)
        dlast = csb("dl", [NH, NCHTOT], F32)
        cde = csb("ce", [NH, NCHTOT], F32)
        cdeT8 = csb("ct", [NHF, NH], F32)
        abuf = csb("ab", [NH, 1], F32)
        dtbb = csb("db", [NH, 1], F32)
        one64 = csb("o64", [NH, 1], F32)
        dvb = csb("dv", [1, NH], F32)
        dbc = csb("dbc", [P, NH], F32)
        ddiag = csb("ddg", [P, NH, P], BF16)
        eyeb = csb("ey", [P, P], BF16)
        eyef = csb("eyf", [P, P], F32)
        mask1 = csb("mk", [P, P], mybir.dt.uint8)
        zeros = csb("z", [P, P], BF16)
        selh = csb("slh", [NHF, NHF, P], BF16)
        selhf = csb("slf", [NHF, NHF, P], F32)
        dqr = csb("dqr", [NHF, 2, NH * P], BF16)
        eT = csb("eT", [P, NHF, NH], F32)
        mdlT = csb("mT", [P, NHF, NH], F32)
        wtT = csb("wT", [P, NHF, NH], F32)
        argsb = csb("ar", [P, 2, 512], F32)
        ebuf = csb("eb", [P, 2, 512], BF16)
        WH = csb("wh", [P, 2, NH * P], BF16)
        cbm = csb("cb", [P, P], BF16)
        btok = csb("bt", [P, P], BF16)
        btil = csb("bl", [P, 2, P], BF16)
        cdb = csb("cdb", [P, NH], F32)
        tbuf = csb("tb", [P, 2, HP], F32)
        sbf = csb("sb", [P, NH, HP], BF16)

        pa0 = cps("pa0", [P, 512])
        pa1 = cps("pa1", [P, 512])
        pa = [pa0, pa1]
        pmisc = cps("pm", [P, 512])
        ptrp = cps("pt", [P, 512])
        pya = [cps("py0", [P, HP]), cps("py1", [P, HP])]
        pybsn = [cps("pq0", [P, 2, HP]), cps("pq1", [P, 2, HP])]

        s_d = b.sem(f"{name}_d")
        s_g = b.sem(f"{name}_g")
        s_pe = b.sem(f"{name}_pe")
        s_sc = b.sem(f"{name}_sc")
        s_dv = b.sem(f"{name}_dv")
        s_o = b.sem(f"{name}_o")
        cnt = dict(d=0, pe=0, sc=0, dv=0, o=0)

        with b.block() as block:
            @block.gpsimd
            def _(g):
                g.dma_start(abuf[:], a_dram).then_inc(s_g, 16)
                g.dma_start(dtbb[:], dtb_dram).then_inc(s_g, 16)
                g.dma_start(dvb[:], dv_dram).then_inc(s_g, 16)
                g.dma_start(eyeb[:], eye_dram).then_inc(s_g, 16)
                g.dma_start(eyef[:], eye_dram).then_inc(s_g, 16)
                g.dma_start(mask1[:], masks_dram[:, 0, 0:P]).then_inc(s_g, 16)
                g.wait_ge(s_g, 96)
            @block.vector
            def _(v):
                v.memset(one64[:], 1.0)
                v.memset(zeros[:], 0.0)
                v.memset(selh[:], 0.0)
                v.memset(selhf[:], 0.0)
        d_sel0 = 0
        with b.block() as block:
            @block.sync
            def _(sy):
                sy.dma_start(sel_dram, b.onesf_row[:]).then_inc(s_d, 16)
                sy.wait_ge(s_d, d_sel0 + 16)
                for k in range(NHF):
                    sy.dma_start(selhf[k:k + 1, k, :], sel_dram).then_inc(s_d, 16)
                sy.wait_ge(s_d, d_sel0 + 16 * 9)
            @block.gpsimd
            def _(g):
                g.wait_ge(s_d, d_sel0 + 16)
                for k in range(NHF):
                    g.dma_start(selh[k:k + 1, k, :], sel_dram).then_inc(s_g, 16)
                g.wait_ge(s_g, 96 + 16 * 8)
        cnt["d"] = 16 * 9
        with b.block() as block:
            @block.tensor
            def _(t):
                t.matmul(pmisc[:, 0:NH], b.onesf_row[:], dvb[:], start=True, stop=True)
        with b.block() as block:
            @block.scalar
            def _(s):
                s.copy(dbc[:], pmisc[:, 0:NH])
        with b.block() as block:
            @block.vector
            def _(v):
                for h in range(NH):
                    v.tensor_scalar(out=ddiag[:, h, :], in0=eyeb[:], scalar1=dbc[:, h:h + 1],
                                    scalar2=None, op0=ALU.mult)

        xview = xtok_dram.rearrange("(n p) m -> p n m", p=P)
        yview = ytok_dram.rearrange("(n p) m -> p n m", p=P)
        prev = None
        pe_end = {}

        nhalves = 0
        for seq in range(NSEQ):
            tok0 = seq * TSEQ
            for hh in range(2):
                if dbg_halves is not None and nhalves >= dbg_halves:
                    continue
                nhalves += 1
                ht0 = tok0 + hh * HT
                gchunk0 = seq * 16 + hh * NHF
                d0, v0, s0, p0 = cnt["d"], cnt["dv"], cnt["sc"], cnt["pe"]
                VPRE = (1 if hh == 0 else 0) + 16
                with b.block() as block:
                    @block.sync
                    def _(sy):
                        c = d0
                        sy.wait_ge(s_d, d0)
                        sy.dma_start(dt_[:], dtraw_dram.rearrange("(c p) t -> p c t", p=P)[0:NH, 0, ht0:ht0 + HT]).then_inc(s_d, 16)
                        c += 16
                        sy.wait_ge(s_d, c)
                        if hh == 0:
                            sy.dma_start(bc[:], bcfm_dram.rearrange("(c p) t -> p c t", p=P)[:, :, tok0:tok0 + TSEQ]).then_inc(s_d, 16)
                            c += 16
                            sy.wait_ge(s_d, c)
                        sy.wait_ge(s_o, cnt["o"])
                        for chl in range(NHF):
                            sy.dma_start(xt[:, chl, :],
                                         xview[:, gchunk0 + chl, :]).then_inc(s_d, 16)
                            c += 16
                            sy.wait_ge(s_d, c)
                        sy.wait_ge(s_dv, v0 + VPRE)
                        sy.dma_start(dq_dram[0], dt_.bitcast(BF16)[:, 0:HT]).then_inc(s_d, 16)
                        sy.dma_start(dq_dram[1], dt_.bitcast(BF16)[:, HT:2 * HT]).then_inc(s_d, 16)
                        c += 32
                        sy.wait_ge(s_d, c)
                        sy.dma_start(dqr[:, 0, :].rearrange("c (h q) -> c h q", q=P),
                                     dq_dram[0, :, :].rearrange("h (c q) -> c h q", q=P)).then_inc(s_d, 16)
                        sy.dma_start(dqr[:, 1, :].rearrange("c (h q) -> c h q", q=P),
                                     dq_dram[1, :, :].rearrange("h (c q) -> c h q", q=P)).then_inc(s_d, 16)
                        c += 32
                    @block.vector
                    def _(v):
                        q = v0
                        if hh == 0:
                            v.memset(sbf[:], 0.0).then_inc(s_dv, 1)
                            q += 1
                        v.wait_ge(s_d, d0 + 16)
                        v.tensor_scalar(out=dt_[:], in0=dt_[:], scalar1=dtbb[:], scalar2=None, op0=ALU.add).then_inc(s_dv, 1)
                        v.wait_ge(s_sc, s0 + 3)
                        v.scalar_tensor_tensor(out=dt_[:], in0=dt_[:], scalar=0.0, in1=t1_[:],
                                               op0=ALU.max, op1=ALU.add).then_inc(s_dv, 1)
                        v.wait_ge(s_dv, q + 2)
                        v.tensor_scalar(out=t1_[:], in0=dt_[:], scalar1=abuf[:], scalar2=None, op0=ALU.mult).then_inc(s_dv, 1)
                        v.wait_ge(s_dv, q + 3)
                        for chl in range(NHF):
                            v.tensor_tensor_scan(out=dacs[:, chl * P:(chl + 1) * P],
                                                 data0=t1_[:, chl * P:(chl + 1) * P],
                                                 data1=t1_[:, chl * P:(chl + 1) * P],
                                                 initial=0.0, op0=ALU.add, op1=ALU.bypass).then_inc(s_dv, 1)
                        v.wait_ge(s_dv, q + 11)
                        v.tensor_copy(dlast[:, gchunk0:gchunk0 + NHF],
                                      dacs[:].rearrange("h (c p) -> h c p", p=P)[:, :, P - 1:P]).then_inc(s_dv, 1)
                        v.wait_ge(s_sc, s0 + 4)
                        v.tensor_tensor(out=mdl[:], in0=mdl[:], in1=dacs[:], op=ALU.subtract).then_inc(s_dv, 1)
                        v.wait_ge(s_sc, s0 + 13)
                        v.tensor_tensor(out=t1_[:], in0=t1_[:], in1=dt_[:], op=ALU.mult).then_inc(s_dv, 1)
                        v.wait_ge(s_dv, q + 14)
                        v.tensor_copy(dt_.bitcast(BF16)[:, 0:HT], dacs[:]).then_inc(s_dv, 1)
                        v.wait_ge(s_dv, q + 15)
                        v.tensor_tensor(out=dt_.bitcast(BF16)[:, HT:2 * HT], in0=dacs[:],
                                        in1=dt_.bitcast(BF16)[:, 0:HT], op=ALU.subtract).then_inc(s_dv, 1)
                    @block.scalar
                    def _(s):
                        vm = 1 if hh == 0 else 0
                        s.wait_ge(s_dv, v0 + vm + 1)
                        s.activation(out=t1_[:], in_=dt_[:], func=AF.Abs).then_inc(s_sc, 1)
                        s.wait_ge(s_sc, s0 + 1)
                        s.activation(out=t1_[:], in_=t1_[:], func=AF.Exp, scale=-1.0).then_inc(s_sc, 1)
                        s.wait_ge(s_sc, s0 + 2)
                        s.activation(out=t1_[:], in_=t1_[:], func=AF.Ln, bias=one64[:]).then_inc(s_sc, 1)
                        s.wait_ge(s_dv, v0 + vm + 2)
                        s.activation(out=mdl[:], in_=dt_[:], func=AF.Ln).then_inc(s_sc, 1)
                        s.wait_ge(s_dv, v0 + vm + 12)
                        s.activation(out=cde[:, gchunk0:gchunk0 + NHF], in_=dlast[:, gchunk0:gchunk0 + NHF], func=AF.Exp).then_inc(s_sc, 1)
                        for chl in range(NHF):
                            s.activation(out=t1_[:, chl * P:(chl + 1) * P],
                                         in_=dacs[:, chl * P:(chl + 1) * P],
                                         func=AF.Exp, scale=-1.0,
                                         bias=dlast[:, gchunk0 + chl:gchunk0 + chl + 1]).then_inc(s_sc, 1)
                        s.wait_ge(s_pe, p0 + 1)
                        s.copy(cdeT8[:], ptrp[0:NHF, 192:256]).then_inc(s_sc, 1)
                        s.wait_ge(s_pe, p0 + 4)
                        s.activation(out=eT[:, 0, :], in_=ptrp[:, 0:64], func=AF.Exp).then_inc(s_sc, 1)
                        s.copy(mdlT[:, 0, :], ptrp[:, 64:128]).then_inc(s_sc, 1)
                        s.copy(wtT[:, 0, :], ptrp[:, 128:192]).then_inc(s_sc, 1)
                    @block.tensor
                    def _(t):
                        vm = 1 if hh == 0 else 0
                        t.wait_ge(s_sc, s0 + 5)
                        t.transpose(ptrp[0:NHF, 192:256], cde[:, gchunk0:gchunk0 + NHF],
                                    eyef[0:NH, 0:NH]).then_inc(s_pe, 1)
                        t.wait_ge(s_sc, s0 + 14)
                        t.wait_ge(s_dv, v0 + vm + 14)
                        t.transpose(ptrp[:, 0:64], dacs[:, 0:P], eyef[0:NH, 0:NH]).then_inc(s_pe, 1)
                        t.transpose(ptrp[:, 64:128], mdl[:, 0:P], eyef[0:NH, 0:NH]).then_inc(s_pe, 1)
                        t.transpose(ptrp[:, 128:192], t1_[:, 0:P], eyef[0:NH, 0:NH]).then_inc(s_pe, 1)
                cnt["d"] = d0 + 16 + (16 if hh == 0 else 0) + 16 * NHF + 64
                xd_base = d0 + 16 + (16 if hh == 0 else 0)
                gth_tgt = cnt["d"]
                cnt["dv"] = v0 + VPRE
                cnt["sc"] = s0 + 5 + NHF + 4
                cnt["pe"] = p0 + 4

                chtab = []
                fbt = dbg_level >= 0.2 or dbg_level >= 1
                fcb = dbg_level >= 0.4 or dbg_level >= 1
                ftr = dbg_level >= 0.6 or dbg_level >= 1
                fcdb = dbg_level >= 0.8 or dbg_level >= 1
                if dbg_level == 0.48:
                    fbt, fcb, ftr, fcdb = False, True, False, False
                if dbg_level == 1.5:
                    fbt, fcb, ftr, fcdb = False, False, False, False
                if dbg_level == 1.6:
                    fbt, fcb, ftr, fcdb = True, False, False, False
                if dbg_level == 1.7:
                    fbt, fcb, ftr, fcdb = False, True, False, False
                if dbg_level == 1.8:
                    fbt, fcb, ftr, fcdb = False, False, True, False
                if dbg_level == 1.9:
                    fbt, fcb, ftr, fcdb = False, False, False, True
                if dbg_level == 1.65:
                    fbt, fcb, ftr, fcdb = True, True, False, False
                if dbg_level == 1.85:
                    fbt, fcb, ftr, fcdb = False, False, True, True
                if dbg_level == 1.91:
                    fbt, fcb, ftr, fcdb = True, True, True, False
                if dbg_level == 1.92:
                    fbt, fcb, ftr, fcdb = True, True, False, True
                if dbg_level == 1.93:
                    fbt, fcb, ftr, fcdb = True, False, True, True
                if dbg_level == 1.94:
                    fbt, fcb, ftr, fcdb = False, True, True, True
                pe_c, sc_c, dv_c = cnt["pe"], cnt["sc"], cnt["dv"]
                for chl in range(NHF):
                    tab = dict(pe={}, sc={}, dv={}, pe_base=pe_c, sc_base=sc_c, dv_base=dv_c)
                    p = pe_c
                    if fbt:
                        p += 1; tab["pe"]["btok"] = p
                    if fcb:
                        p += 1; tab["pe"]["cb"] = p
                    if dbg_level >= 1:
                        for j in range(NPC):
                            p += 1; tab["pe"][("hi", j)] = p
                            p += 1; tab["pe"][("lo", j)] = p
                    if chl < NHF - 1 and ftr:
                        for k in range(3):
                            p += 1; tab["pe"][("tr", k)] = p
                    if fcdb:
                        p += 1; tab["pe"]["cdb"] = p
                    if dbg_level >= 2:
                        for h in range(NH):
                            for i in range(4):
                                p += 1; tab["pe"][("m", h, i)] = p
                    s = sc_c
                    if fbt:
                        s += 1; tab["sc"]["btokE"] = s
                    if dbg_level >= 1:
                        for h in range(NH):
                            s += 1; tab["sc"][("exp", h)] = s
                    if chl < NHF - 1 and ftr:
                        for k in range(3):
                            s += 1; tab["sc"][("trE", k)] = s
                    if fcdb:
                        s += 1; tab["sc"]["cdbE"] = s
                    if dbg_level >= 2:
                        for h in range(NH):
                            s += 1; tab["sc"][("tbuf", h)] = s
                    d = dv_c
                    if fcb:
                        d += 1; tab["dv"]["sel"] = d
                    if dbg_level >= 1:
                        d += 1; tab["dv"][("va", 0)] = d
                        for h in range(1, NH):
                            d += 1; tab["dv"][("va", h)] = d
                            d += 1; tab["dv"][("wh", h - 1)] = d
                        d += 1; tab["dv"][("wh", NH - 1)] = d
                    if dbg_level >= 2:
                        for k in range(2):
                            d += 1; tab["dv"][("btl", k)] = d
                        for h in range(NH):
                            d += 1; tab["dv"][("vy", h)] = d
                            d += 1; tab["dv"][("vsb", h)] = d
                            if h + 2 < NH:
                                d += 1; tab["dv"][("btl", h + 2)] = d
                    tab["pe_next"], tab["sc_next"], tab["dv_next"] = p, s, d
                    chtab.append(tab)
                    pe_c, sc_c, dv_c = p, s, d
                for chl in range(NHF):
                    pe_end[gchunk0 + chl] = chtab[chl]["pe_next"]
                if dbg_skip_chunks:
                    continue

                for chl_o in range(NHF):
                  with b.block() as block:
                    @block.tensor
                    def _(t):
                        for chl in [chl_o]:
                            gc = gchunk0 + chl
                            tab = chtab[chl]
                            pv = chtab[chl - 1] if chl > 0 else prev
                            tsl = slice(hh * HT + chl * P, hh * HT + (chl + 1) * P)
                            if fbt:
                                if chl == 0:
                                    t.wait_ge(s_sc, tab["sc_base"])
                                if pv is not None:
                                    if ("trE", 2) in pv["sc"]:
                                        t.wait_ge(s_sc, pv["sc"][("trE", 2)])
                                    else:
                                        t.wait_ge(s_sc, pv["sc"]["btokE"])
                                    t.wait_ge(s_pe, tab["pe_base"])
                                t.transpose(ptrp.bitcast(BF16)[:, 896:1024], bc[:, 0, tsl],
                                            eyeb[:]).then_inc(s_pe, 1)
                            if fcb:
                                if pv is not None:
                                    t.wait_ge(s_dv, pv["dv"]["sel"])
                                if fbt:
                                    t.wait_ge(s_pe, tab["pe"]["btok"])
                                t.matmul(pmisc[:, 0:128], bc[:, 0, tsl], bc[:, 1, tsl],
                                         start=True, stop=True).then_inc(s_pe, 1)
                            if chl == 0:
                                t.wait_ge(s_d, gth_tgt)
                            if fcb and dbg_level >= 1:
                                t.wait_ge(s_pe, tab["pe"]["cb"])
                            for j in range(NPC if dbg_level >= 1 else 0):
                                if j >= 2:
                                    t.wait_ge(s_dv, tab["dv"][("va", 4 * (j - 2) + 3)])
                                elif pv is not None:
                                    t.wait_ge(s_dv, pv["dv"][("va", 4 * (14 + j) + 3)])
                                t.matmul(pa[j % 2][:, :], selh[:, chl, :],
                                         dqr[:, 0, j * 512:(j + 1) * 512], start=True, stop=False).then_inc(s_pe, 1)
                                t.matmul(pa[j % 2][:, :], selh[:, chl, :],
                                         dqr[:, 1, j * 512:(j + 1) * 512], start=False, stop=True).then_inc(s_pe, 1)
                            if chl < NHF - 1 and ftr:
                                if dbg_level >= 1:
                                    t.wait_ge(s_pe, tab["pe"][("lo", NPC - 1)])
                                if fbt:
                                    t.wait_ge(s_sc, tab["sc"]["btokE"])
                                nsl = slice((chl + 1) * P, (chl + 2) * P)
                                t.transpose(ptrp[:, 0:64], dacs[:, nsl], eyef[0:NH, 0:NH]).then_inc(s_pe, 1)
                                t.transpose(ptrp[:, 64:128], mdl[:, nsl], eyef[0:NH, 0:NH]).then_inc(s_pe, 1)
                                t.transpose(ptrp[:, 128:192], t1_[:, nsl], eyef[0:NH, 0:NH]).then_inc(s_pe, 1)
                            if fcdb:
                                if pv is not None:
                                    t.wait_ge(s_sc, pv["sc"]["cdbE"])
                                if fcb:
                                    t.wait_ge(s_dv, tab["dv"]["sel"])
                                if chl < NHF - 1 and ftr:
                                    t.wait_ge(s_pe, tab["pe"][("tr", 2)])
                                t.matmul(pmisc[:, 128:192], selhf[:, chl, :],
                                         cdeT8[:, :], start=True, stop=True).then_inc(s_pe, 1)
                            if dbg_level < 2:
                                continue
                            t.wait_ge(s_d, xd_base + 16 * (chl + 1))
                            for h in range(NH):
                                t.wait_ge(s_dv, tab["dv"][("btl", h)])
                                if h >= 2:
                                    t.wait_ge(s_sc, tab["sc"][("tbuf", h - 2)])
                                elif pv is not None:
                                    t.wait_ge(s_sc, pv["sc"][("tbuf", 62 + h)])
                                xsl = xt[:, chl, HP * h:HP * h + HP]
                                t.matmul(pya[h % 2][:, :], WH[:, gc % 2, h * P:(h + 1) * P],
                                         xsl, start=True, stop=False).then_inc(s_pe, 1)
                                t.matmul(pya[h % 2][:, :], ddiag[:, h, :], xsl,
                                         start=False, stop=True).then_inc(s_pe, 1)
                                t.matmul(pybsn[h % 2][:, 0, :], bc[:, 1, tsl], sbf[:, h, :],
                                         start=True, stop=True).then_inc(s_pe, 1)
                                t.matmul(pybsn[h % 2][:, 1, :], btil[:, h % 2, :], xsl,
                                         start=True, stop=True).then_inc(s_pe, 1)

                    @block.scalar
                    def _(s):
                        for chl in [chl_o]:
                            tab = chtab[chl]
                            pv = chtab[chl - 1] if chl > 0 else prev
                            if fbt:
                                if pv is not None:
                                    s.wait_ge(s_dv, pv["dv_next"])
                                s.wait_ge(s_pe, tab["pe"]["btok"])
                                s.copy(btok[:], ptrp.bitcast(BF16)[:, 896:1024]).then_inc(s_sc, 1)
                            for h in range(NH if dbg_level >= 1 else 0):
                                s.wait_ge(s_dv, tab["dv"][("va", h)])
                                j = h // 4
                                s.activation(out=ebuf[:, j % 2, (h % 4) * P:(h % 4 + 1) * P],
                                             in_=argsb[:, j % 2, (h % 4) * P:(h % 4 + 1) * P],
                                             func=AF.Exp).then_inc(s_sc, 1)
                            if chl < NHF - 1 and ftr:
                                s.wait_ge(s_pe, tab["pe"][("tr", 2)])
                                s.activation(out=eT[:, chl + 1, :], in_=ptrp[:, 0:64], func=AF.Exp).then_inc(s_sc, 1)
                                s.copy(mdlT[:, chl + 1, :], ptrp[:, 64:128]).then_inc(s_sc, 1)
                                s.copy(wtT[:, chl + 1, :], ptrp[:, 128:192]).then_inc(s_sc, 1)
                            if fcdb:
                                s.wait_ge(s_pe, tab["pe"]["cdb"])
                                s.copy(cdb[:], pmisc[:, 128:192]).then_inc(s_sc, 1)
                            if dbg_level < 2:
                                continue
                            for h in range(NH):
                                s.wait_ge(s_pe, tab["pe"][("m", h, 1)])
                                if h >= 2:
                                    s.wait_ge(s_dv, tab["dv"][("vy", h - 2)])
                                elif pv is not None:
                                    s.wait_ge(s_dv, pv["dv"][("vy", 62 + h)])
                                s.copy(tbuf[:, h % 2, :], pya[h % 2][:, :]).then_inc(s_sc, 1)

                    @block.vector
                    def _(v):
                        for chl in [chl_o]:
                            gc = gchunk0 + chl
                            tab = chtab[chl]
                            pv = chtab[chl - 1] if chl > 0 else prev
                            if fcb:
                                v.wait_ge(s_pe, tab["pe"]["cb"])
                                if dbg_level == 0.45:
                                    v.tensor_copy(cbm[:], pmisc[:, 0:128]).then_inc(s_dv, 1)
                                else:
                                    v.select(out=cbm[:], mask=mask1[:], on_true=pmisc[:, 0:128],
                                             on_false=zeros[:], add_drain=True).then_inc(s_dv, 1)
                            def emit_va(h):
                                j = h // 4
                                v.wait_ge(s_pe, tab["pe"][("lo", j)])
                                if h >= 8:
                                    v.wait_ge(s_sc, tab["sc"][("exp", h - 8)])
                                elif pv is not None:
                                    v.wait_ge(s_sc, pv["sc"][("exp", 56 + h)])
                                v.tensor_scalar(out=argsb[:, j % 2, (h % 4) * P:(h % 4 + 1) * P],
                                                in0=pa[j % 2][:, (h % 4) * P:(h % 4 + 1) * P],
                                                scalar1=mdlT[:, chl, h:h + 1], scalar2=30.0,
                                                op0=ALU.add, op1=ALU.min).then_inc(s_dv, 1)

                            def emit_wh(h):
                                v.wait_ge(s_sc, tab["sc"][("exp", h)])
                                if h == 0:
                                    if "sel" in tab["dv"]:
                                        v.wait_ge(s_dv, tab["dv"]["sel"])
                                    if gc >= 2:
                                        v.wait_ge(s_pe, pe_end[gc - 2])
                                j = h // 4
                                v.tensor_tensor(out=WH[:, gc % 2, h * P:(h + 1) * P],
                                                in0=ebuf[:, j % 2, (h % 4) * P:(h % 4 + 1) * P],
                                                in1=cbm[:], op=ALU.mult).then_inc(s_dv, 1)
                            if dbg_level >= 1:
                                emit_va(0)
                                for h in range(1, NH):
                                    emit_va(h)
                                    emit_wh(h - 1)
                                emit_wh(NH - 1)
                            if dbg_level < 2:
                                continue
                            for k in range(2):
                                if k == 0:
                                    v.wait_ge(s_sc, tab["sc"]["btokE"])
                                    v.wait_ge(s_pe, tab["pe_base"])
                                v.tensor_scalar(out=btil[:, k % 2, :], in0=btok[:],
                                                scalar1=wtT[:, chl, k:k + 1], scalar2=None,
                                                op0=ALU.mult).then_inc(s_dv, 1)
                            for h in range(NH):
                                v.wait_ge(s_pe, tab["pe"][("m", h, 3)])
                                v.wait_ge(s_sc, tab["sc"][("tbuf", h)])
                                v.scalar_tensor_tensor(out=xt[:, chl, HP * h:HP * h + HP],
                                                       in0=pybsn[h % 2][:, 0, :],
                                                       scalar=eT[:, chl, h:h + 1],
                                                       in1=tbuf[:, h % 2, :],
                                                       op0=ALU.mult, op1=ALU.add).then_inc(s_dv, 1)
                                v.scalar_tensor_tensor(out=sbf[:, h, :], in0=sbf[:, h, :],
                                                       scalar=cdb[:, h:h + 1],
                                                       in1=pybsn[h % 2][:, 1, :],
                                                       op0=ALU.mult, op1=ALU.add).then_inc(s_dv, 1)
                                if h + 2 < NH:
                                    v.tensor_scalar(out=btil[:, (h + 2) % 2, :], in0=btok[:],
                                                    scalar1=wtT[:, chl, h + 2:h + 3], scalar2=None,
                                                    op0=ALU.mult).then_inc(s_dv, 1)

                    if dbg_level >= 3:
                        @block.sync
                        def _(sy):
                            for chl in [chl_o]:
                                tab = chtab[chl]
                                sy.wait_ge(s_dv, tab["dv"][("vsb", NH - 1)])
                                sy.dma_start(yview[:, gchunk0 + chl, :], xt[:, chl, :]).then_inc(s_o, 16)
                                sy.wait_ge(s_o, cnt["o"] + 16 * (chl + 1))

                prev = chtab[NHF - 1]
                cnt["pe"], cnt["sc"], cnt["dv"] = pe_c, sc_c, dv_c
                cnt["o"] += 16 * NHF




def emit_rms(b, name, ytok_dram, gate_dram, normw_dram, gn_dram, eps=1e-5):
    nc = b.nc
    II = 4096
    ctx = ExitStack()
    with ctx:
        def csb(n, shape, dt):
            return ctx.enter_context(nc.sbuf_tensor(f"{name}_{n}", shape, dt))
        GN = 8
        yb = csb("y", [P, GN, II], BF16)
        gt = csb("gt", [P, 2, II], BF16)
        sg = csb("sg", [P, II], BF16)
        scrap = csb("sc", [P, II], BF16)
        var = csb("va", [P, GN], F32)
        rst = csb("rs", [P, GN], F32)
        nwexp = csb("nw", [P, II], F32)
        nwrow = csb("nr", [1, II], F32)
        epsb = csb("ep", [P, 1], F32)
        stg = csb("st", [P, 2, II], BF16)
        pb = ctx.enter_context(nc.psum_tensor(f"{name}_pb", [P, 512], F32))
        s_d = b.sem(f"{name}_d")
        s_a = b.sem(f"{name}_a")
        s_v = b.sem(f"{name}_v")
        s_o = b.sem(f"{name}_o")
        c = dict(d=0, a=0, v=0, o=0)
        ysrc = ytok_dram.rearrange("(n p) m -> p n m", p=P)
        gsrc = gate_dram.rearrange("(n p) m -> p n m", p=P)
        gdst = gn_dram.rearrange("(n p) m -> p n m", p=P)
        with b.block() as block:
            @block.gpsimd
            def _(g):
                g.dma_start(nwrow[:], normw_dram).then_inc(s_d, 16)
                g.wait_ge(s_d, 16)
            @block.vector
            def _(v):
                v.memset(epsb[:], float(eps))
        c["d"] = 16
        with b.block() as block:
            @block.tensor
            def _(t):
                for i in range(II // 512):
                    t.matmul(pb[:, :], b.onesf_row[:], nwrow[0:1, i * 512:(i + 1) * 512],
                             start=True, stop=True).then_inc(s_a, 1)
                    t.wait_ge(s_v, c["v"] + i + 1)
            @block.vector
            def _(v):
                for i in range(II // 512):
                    v.wait_ge(s_a, c["a"] + i + 1)
                    v.tensor_copy(nwexp[:, i * 512:(i + 1) * 512], pb[:, :]).then_inc(s_v, 1)
        c["a"] += 8
        c["v"] += 8
        for grp in range(T // P // GN):
            with b.block() as block:
                @block.sync
                def _(sy):
                    sy.dma_start(yb[:], ysrc[:, grp * GN:(grp + 1) * GN, :]).then_inc(s_d, 16)
                    sy.wait_ge(s_d, c["d"] + 16)
            c["d"] += 16
            for tt in range(GN):
                gtok = grp * GN + tt
                with b.block() as block:
                    @block.sync
                    def _(sy):
                        sy.dma_start(gt[:, tt % 2, :], gsrc[:, gtok, :]).then_inc(s_d, 16)
                        sy.wait_ge(s_d, c["d"] + 16)
                c["d"] += 16
                with b.block() as block:
                    @block.scalar
                    def _(s):
                        s.activation(out=sg[:], in_=gt[:, tt % 2, :], func=AF.Silu)
                with b.block() as block:
                    @block.vector
                    def _(v):
                        v.tensor_tensor(out=yb[:, tt, :], in0=yb[:, tt, :], in1=sg[:], op=ALU.mult)
                with b.block() as block:
                    @block.scalar
                    def _(s):
                        s.activation(out=scrap[:], in_=yb[:, tt, :], func=AF.Square,
                                     accum_out=var[:, tt:tt + 1])
            with b.block() as block:
                @block.vector
                def _(v):
                    v.tensor_scalar(out=rst[:], in0=var[:], scalar1=1.0 / II, scalar2=None,
                                    op0=ALU.mult)
                    v.tensor_copy(sg[:, 0:512], sg[:, 0:512])
                    v.tensor_copy(sg[:, 0:512], sg[:, 0:512])
                    v.tensor_copy(sg[:, 0:512], sg[:, 0:512])
            with b.block() as block:
                @block.scalar
                def _(s):
                    s.activation(out=rst[:], in_=rst[:], func=AF.Sqrt, bias=epsb[:])
                    s.copy(sg[:, 0:512], sg[:, 0:512])
                    s.copy(sg[:, 0:512], sg[:, 0:512])
            with b.block() as block:
                @block.vector
                def _(v):
                    v.reciprocal(out=rst[:], in_=rst[:])
                    v.tensor_copy(sg[:, 0:512], sg[:, 0:512])
                    v.tensor_copy(sg[:, 0:512], sg[:, 0:512])
                    v.tensor_copy(sg[:, 0:512], sg[:, 0:512])
            with b.block() as block:
                @block.vector
                def _(v):
                    for tt in range(GN):
                        if tt >= 2:
                            v.wait_ge(s_o, c["o"] + 16 * (tt - 1))
                        v.scalar_tensor_tensor(out=stg[:, tt % 2, :], in0=yb[:, tt, :],
                                               scalar=rst[:, tt:tt + 1], in1=nwexp[:],
                                               op0=ALU.mult, op1=ALU.mult).then_inc(s_v, 1)
                @block.sync
                def _(sy):
                    for tt in range(GN):
                        sy.wait_ge(s_v, c["v"] + tt + 1)
                        sy.dma_start(gdst[:, grp * GN + tt, :], stg[:, tt % 2, :]).then_inc(s_o, 16)
                    sy.wait_ge(s_o, c["o"] + 16 * GN)
            c["v"] += GN
            c["o"] += 16 * GN




def emit_outproj(b, name, gn_dram, wout_dram, attn1_dram):
    nc = b.nc
    II = 4096
    ctx0 = ExitStack()
    gnfm = ctx0.enter_context(nc.sbuf_tensor(f"{name}_x", [P, II // P, TSEQ], BF16))
    s_d = b.sem(f"{name}_ld")
    dv = 0
    for seq in range(NSEQ):
        tok0 = seq * TSEQ
        with b.block() as block:
            @block.sync
            def _(sy):
                nonlocal dv
                for st in range(TSEQ // 512):
                    for k in range(II // P):
                        sy.dma_start(gnfm[:, k, st * 512:(st + 1) * 512],
                                     gn_dram[tok0 + st * 512:tok0 + (st + 1) * 512,
                                             k * P:(k + 1) * P],
                                     transpose=True).then_inc(s_d, 16)
                        dv += 16
                sy.wait_ge(s_d, dv)
        emit_mm_fm(b, f"{name}_m{seq}", wout_dram, gnfm, II, D, TSEQ,
                   out_dram=attn1_dram, out_tok0=tok0)
    ctx0.close()




def emit_ffn1(b, name, xn_dram, rt_dram, w1t_dram, b1_dram, w2t_dram, b2_dram,
              prevmask_dram, iota_dram, eye_dram, h_dram, out_dram, CAP=512):
    nc = b.nc
    HT = T // 2
    NTT = T // P
    ctx = ExitStack()
    with ctx:
        def csb(n, shape, dt):
            return ctx.enter_context(nc.sbuf_tensor(f"{name}_{n}", shape, dt))

        def cps(n, shape, dt=F32):
            return ctx.enter_context(nc.psum_tensor(f"{name}_{n}", shape, dt))

        xn = csb("xn", [P, DC, HT], BF16)
        rtw = csb("rt", [P, DC, 8], BF16)
        eyef = csb("ey", [P, P], F32)
        lg = csb("lg", [P, NTT, 8], F32)
        mx8 = csb("mx", [P, NTT, 8], F32)
        mi8 = csb("mi", [P, NTT, 8], U32)
        negm = csb("nm", [P, NTT, 1], F32)
        e12 = csb("e12", [P, NTT, 2], F32)
        oh1 = csb("oh1", [P, NTT, 8], F32)
        oh2 = csb("oh2", [P, NTT, 8], F32)
        p12 = csb("p12", [P, NTT, 2], F32)
        esum = csb("es", [P, NTT, 1], F32)
        stc = csb("stc", [P, NTT, 1], F32)
        poscol = csb("po", [P, NTT, 2], F32)
        iotab = csb("io", [P, 8], F32)
        ohT = csb("ohT", [8, T], F32)
        scnT = csb("scT", [8, T], F32)
        lincT = csb("lcT", [P, NTT, 2, 8], F32)
        G = csb("G", [8, 16], F32)
        pmsk = csb("pm", [8, 1], F32)
        offr = csb("of", [1, 2, 8], F32)
        offb = csb("ofb", [P, 2, 8], F32)
        b1sb = csb("b1", [P, 64], F32)
        b2sb = csb("b2", [1, D], BF16)
        cntr = csb("cn", [1, 16], F32)
        hti = [csb(f"ht{i}", [P, 64, P], BF16) for i in range(2)]
        w2c = [csb(f"w2{i}", [P, 64, 256], BF16) for i in range(1)]
        stg = csb("sg", [P, 3, 512], BF16)
        ps = [cps(f"p{i}", [P, 512]) for i in range(2)]

        s_d = b.sem(f"{name}_d")
        s_m = b.sem(f"{name}_m")
        s_a = b.sem(f"{name}_a")
        s_v = b.sem(f"{name}_v")
        s_o = b.sem(f"{name}_o")
        s_cc = b.sem(f"{name}_cc")
        c = dict(d=0, m=0, a=0, v=0, o=0)
        cc_in = b.dram(f"{name}_ccin", [1, 16], F32)
        cc_out = b.dram(f"{name}_ccout", [NCORE, 16], F32)

        with b.block() as block:
            @block.gpsimd
            def _(g):
                g.dma_start(rtw[:], rt_dram.rearrange("(c p) m -> p c m", p=P)).then_inc(s_d, 16)
                g.dma_start(iotab[0:1, :], iota_dram).then_inc(s_d, 16)
                g.dma_start(pmsk[:], prevmask_dram).then_inc(s_d, 16)
                g.dma_start(b1sb[:], b1_dram).then_inc(s_d, 16)
                g.dma_start(b2sb[:], b2_dram).then_inc(s_d, 16)
                g.dma_start(eyef[:], eye_dram).then_inc(s_d, 16)
                g.wait_ge(s_d, 96)
        c["d"] = 96
        with b.block() as block:
            @block.tensor
            def _(t):
                t.matmul(ps[0][:, 0:8], b.onesf_row[:], iotab[0:1, :], start=True, stop=True)
        with b.block() as block:
            @block.scalar
            def _(s):
                s.copy(iotab[:], ps[0][:, 0:8])

        for half in range(2):
            tok0 = half * HT
            tt0 = half * (HT // P)
            with b.block() as block:
                @block.sync
                def _(sy):
                    sy.dma_start(xn[:], fmv(xn_dram)[:, :, tok0:tok0 + HT]).then_inc(s_d, 16)
                    sy.wait_ge(s_d, c["d"] + 16)
            c["d"] += 16
            with b.block() as block:
                @block.tensor
                def _(t):
                    for tt in range(HT // P):
                        if tt >= 2:
                            t.wait_ge(s_a, c["a"] + tt - 1)
                        for k in range(DC):
                            ins = t.matmul(ps[tt % 2][0:P, 0:8],
                                           xn[:, k, tt * P:(tt + 1) * P],
                                           rtw[:, k, :], start=(k == 0), stop=(k == DC - 1))
                        ins.then_inc(s_m, 1)
                @block.scalar
                def _(s):
                    for tt in range(HT // P):
                        s.wait_ge(s_m, c["m"] + tt + 1)
                        s.copy(lg[:, tt0 + tt, :], ps[tt % 2][0:P, 0:8]).then_inc(s_a, 1)
            c["m"] += HT // P
            c["a"] += HT // P

        with b.block() as block:
            @block.vector
            def _(v):
                for tt in range(NTT):
                    v.max(out=mx8[:, tt, :], in_=lg[:, tt, :])
        with b.block() as block:
            @block.vector
            def _(v):
                for tt in range(NTT):
                    v.max_index(out=mi8[:, tt, :], in_max=mx8[:, tt, :], in_values=lg[:, tt, :])
                for tt in range(NTT):
                    v.tensor_scalar(out=negm[:, tt, :], in0=mx8[:, tt, 0:1], scalar1=-1.0,
                                    scalar2=None, op0=ALU.mult)
        with b.block() as block:
            @block.scalar
            def _(s):
                for tt in range(NTT):
                    s.activation(out=lg[:, tt, :], in_=lg[:, tt, :], func=AF.Exp,
                                 bias=negm[:, tt, 0:1], accum_out=esum[:, tt, 0:1])
        with b.block() as block:
            @block.vector
            def _(v):
                for tt in range(NTT):
                    v.tensor_copy(e12[:, tt, :], mi8[:, tt, 0:2])
                for tt in range(NTT):
                    v.tensor_scalar(out=oh1[:, tt, :], in0=iotab[:], scalar1=e12[:, tt, 0:1],
                                    scalar2=None, op0=ALU.is_equal)
                for tt in range(NTT):
                    v.tensor_scalar(out=oh2[:, tt, :], in0=iotab[:], scalar1=e12[:, tt, 1:2],
                                    scalar2=None, op0=ALU.is_equal)
                for tt in range(NTT):
                    v.reciprocal(out=p12[:, tt, 0:1], in_=esum[:, tt, 0:1])
                for tt in range(NTT):
                    v.tensor_tensor(out=p12[:, tt, 1:2], in0=mx8[:, tt, 1:2],
                                    in1=mx8[:, tt, 0:1], op=ALU.subtract)
        with b.block() as block:
            @block.scalar
            def _(s):
                for tt in range(NTT):
                    s.activation(out=p12[:, tt, 1:2], in_=p12[:, tt, 1:2], func=AF.Exp)
        with b.block() as block:
            @block.vector
            def _(v):
                for tt in range(NTT):
                    v.tensor_scalar(out=p12[:, tt, 1:2], in0=p12[:, tt, 1:2],
                                    scalar1=p12[:, tt, 0:1], scalar2=None, op0=ALU.mult)

        with b.block() as block:
            @block.tensor
            def _(t):
                for tt in range(NTT):
                    t.matmul(ps[0][0:1, 0:8], b.onesf_col[:], oh1[:, tt, :],
                             start=(tt == 0), stop=(tt == NTT - 1))
                for tt in range(NTT):
                    t.matmul(ps[0][0:1, 8:16], b.onesf_col[:], oh2[:, tt, :],
                             start=(tt == 0), stop=(tt == NTT - 1))
        with b.block() as block:
            @block.scalar
            def _(s):
                s.copy(cntr[:], ps[0][0:1, 0:16])
        with b.block() as block:
            @block.gpsimd
            def _(g):
                g.dma_start(cc_in, cntr[:]).then_inc(s_d, 16)
                g.wait_ge(s_d, c["d"] + 16)
                g.collective_compute("AllGather", ALU.bypass,
                                     replica_groups=[list(range(NCORE))],
                                     ins=[cc_in.opt()], outs=[cc_out.opt()]).then_inc(s_cc, 1)
                g.wait_ge(s_cc, 1)
                g.dma_start(G[:], cc_out).then_inc(s_d, 16)
                g.wait_ge(s_d, c["d"] + 32)
        c["d"] += 32
        with b.block() as block:
            @block.tensor
            def _(t):
                t.matmul(ps[0][0:1, 0:16], pmsk[:], G[:], start=True, stop=True)
                t.matmul(ps[0][0:1, 16:24], b.onesf_col[0:8, :], G[:, 0:8], start=True, stop=True)
        with b.block() as block:
            @block.scalar
            def _(s):
                s.copy(offr[0:1, 0, :], ps[0][0:1, 0:8])
                s.copy(offr[0:1, 1, :], ps[0][0:1, 8:16])
        with b.block() as block:
            @block.vector
            def _(v):
                v.tensor_tensor(out=offr[0:1, 1, :], in0=offr[0:1, 1, :],
                                in1=ps[0][0:1, 16:24], op=ALU.add)
                v.tensor_copy(ohT[0:8, 0:512], ohT[0:8, 0:512])
                v.tensor_copy(ohT[0:8, 0:512], ohT[0:8, 0:512])
                v.tensor_copy(ohT[0:8, 0:512], ohT[0:8, 0:512])
        with b.block() as block:
            @block.tensor
            def _(t):
                t.matmul(ps[1][:, 0:8], b.onesf_row[:], offr[0:1, 0, :], start=True, stop=True)
                t.matmul(ps[1][:, 8:16], b.onesf_row[:], offr[0:1, 1, :], start=True, stop=True)
        with b.block() as block:
            @block.scalar
            def _(s):
                s.copy(offb[:, 0, :], ps[1][:, 0:8])
                s.copy(offb[:, 1, :], ps[1][:, 8:16])
                s.copy(stg[:, 0, :], stg[:, 0, :])
                s.copy(stg[:, 0, :], stg[:, 0, :])

        for rank in range(2):
            oh = [oh1, oh2][rank]
            with b.block() as block:
                @block.tensor
                def _(t):
                    for tt in range(NTT):
                        if tt >= 1:
                            t.wait_ge(s_a, c["a"] + tt)
                        t.transpose(ps[0][0:8, 0:P], oh[:, tt, :], eyef[:]).then_inc(s_m, 1)
                @block.scalar
                def _(s):
                    for tt in range(NTT):
                        s.wait_ge(s_m, c["m"] + tt + 1)
                        s.copy(ohT[0:8, tt * P:(tt + 1) * P], ps[0][0:8, 0:P]).then_inc(s_a, 1)
            c["m"] += NTT
            c["a"] += NTT
            with b.block() as block:
                @block.vector
                def _(v):
                    v.tensor_tensor_scan(out=scnT[:], data0=ohT[:], data1=ohT[:],
                                         initial=0.0, op0=ALU.add, op1=ALU.bypass)
            with b.block() as block:
                @block.tensor
                def _(t):
                    for tt in range(NTT):
                        if tt >= 1:
                            t.wait_ge(s_a, c["a"] + tt)
                        t.transpose(ps[0][:, 0:8], scnT[0:8, tt * P:(tt + 1) * P],
                                    eyef[0:8, 0:8]).then_inc(s_m, 1)
                @block.scalar
                def _(s):
                    for tt in range(NTT):
                        s.wait_ge(s_m, c["m"] + tt + 1)
                        s.copy(lincT[:, tt, rank, :], ps[0][:, 0:8]).then_inc(s_a, 1)
            c["m"] += NTT
            c["a"] += NTT

        with b.block() as block:
            @block.vector
            def _(v):
                for rank in range(2):
                    oh = [oh1, oh2][rank]
                    for tt in range(NTT):
                        v.tensor_tensor(out=lincT[:, tt, rank, :], in0=lincT[:, tt, rank, :],
                                        in1=offb[:, rank, :], op=ALU.add)
                for rank in range(2):
                    oh = [oh1, oh2][rank]
                    for tt in range(NTT):
                        v.tensor_tensor(out=lincT[:, tt, rank, :], in0=lincT[:, tt, rank, :],
                                        in1=oh[:, tt, :], op=ALU.mult)
                for rank in range(2):
                    for tt in range(NTT):
                        v.reduce_sum(out=poscol[:, tt, rank:rank + 1],
                                     in_=lincT[:, tt, rank, :], axis=AX.X)
                for tt in range(NTT):
                    v.tensor_scalar(out=poscol[:, tt, :], in0=poscol[:, tt, :],
                                    scalar1=-1.0, scalar2=float(CAP), op0=ALU.add, op1=ALU.is_lt)
                for tt in range(NTT):
                    v.tensor_tensor(out=p12[:, tt, :], in0=p12[:, tt, :],
                                    in1=poscol[:, tt, :], op=ALU.mult)
                for tt in range(NTT):
                    v.reduce_sum(out=stc[:, tt, 0:1], in_=p12[:, tt, :], axis=AX.X)

        for half in range(2):
            tok0 = half * HT
            with b.block() as block:
                @block.sync
                def _(sy):
                    sy.dma_start(xn[:], fmv(xn_dram)[:, :, tok0:tok0 + HT]).then_inc(s_d, 16)
                    sy.wait_ge(s_d, c["d"] + 16)
            c["d"] += 16

            emit_mm_fm(b, f"{name}_h{half}", w1t_dram, xn, D, 8192, HT,
                       out_dram=h_dram, bias_sb=b1sb, act_fn=AF.Gelu, out_tok0=tok0)

            NTTh = HT // P
            NOC = 8
            with b.block() as block:
                @block.gpsimd
                def _(g):
                    for oc in range(NOC):
                        if oc >= 1:
                            g.wait_ge(s_m, c["m"] + oc * NTTh)
                        g.dma_start(w2c[0][:],
                                    w2t_dram.rearrange("(c p) m -> p c m", p=P)[:, :, oc * 256:(oc + 1) * 256]).then_inc(s_d, 16)

                @block.sync
                def _(sy):
                    n = 0
                    for oc in range(NOC):
                        for tt in range(NTTh):
                            n += 1
                            if n > 2:
                                sy.wait_ge(s_m, c["m"] + n - 2)
                            sy.dma_start(hti[(n - 1) % 2][:],
                                         h_dram.rearrange("(c p) t -> p c t", p=P)[:, :, tok0 + tt * P:tok0 + (tt + 1) * P]).then_inc(s_o, 16)

                @block.tensor
                def _(t):
                    n = 0
                    for oc in range(NOC):
                        t.wait_ge(s_d, c["d"] + 16 * (oc + 1))
                        for tt in range(NTTh):
                            n += 1
                            t.wait_ge(s_o, c["o"] + 16 * n)
                            if n > 2:
                                t.wait_ge(s_v, c["v"] + n - 2)
                            for fc in range(64):
                                t.matmul(ps[(n - 1) % 2][:, 0:256], hti[(n - 1) % 2][:, fc, :],
                                         w2c[0][:, fc, :], start=(fc == 0), stop=False)
                            t.matmul(ps[(n - 1) % 2][:, 0:256], b.ones_row[:],
                                     b2sb[0:1, oc * 256:(oc + 1) * 256],
                                     start=False, stop=True).then_inc(s_m, 1)

                @block.vector
                def _(v):
                    n = 0
                    for oc in range(NOC):
                        for tt in range(NTTh):
                            n += 1
                            v.wait_ge(s_m, c["m"] + n)
                            if n > 3:
                                v.wait_ge(s_a, c["a"] + 16 * (n - 3))
                            v.tensor_scalar(out=stg[:, (n - 1) % 3, 0:256], in0=ps[(n - 1) % 2][:, 0:256],
                                            scalar1=stc[:, half * NTTh + tt, 0:1],
                                            scalar2=None, op0=ALU.mult).then_inc(s_v, 1)

                @block.scalar
                def _(s):
                    n = 0
                    for oc in range(NOC):
                        for tt in range(NTTh):
                            n += 1
                            s.wait_ge(s_v, c["v"] + n)
                            s.dma_start(out_dram.rearrange("(n p) m -> p n m", p=P)[:, half * NTTh + tt, oc * 256:(oc + 1) * 256],
                                        stg[:, (n - 1) % 3, 0:256]).then_inc(s_a, 16)
                    s.wait_ge(s_a, c["a"] + 16 * NOC * NTTh)
            c["d"] += 16 * NOC
            c["m"] += NOC * NTTh
            c["v"] += NOC * NTTh
            c["o"] += 16 * NOC * NTTh
            c["a"] += 16 * NOC * NTTh




def emit_wc(b, name, yhat_dram, wc_dram, bc_dram, a0_dram):
    nc = b.nc
    with ExitStack() as ctx:
        ysb = ctx.enter_context(nc.sbuf_tensor(f"{name}_y", [P, DC, T], BF16))
        bcsb = ctx.enter_context(nc.sbuf_tensor(f"{name}_b", [P, DC], F32))
        s_d = b.sem(f"{name}_ld")
        with b.block() as block:
            @block.sync
            def _(sy):
                sy.dma_start(ysb[:], fmv(yhat_dram)).then_inc(s_d, 16)
                sy.dma_start(bcsb[:], bc_dram).then_inc(s_d, 16)
                sy.wait_ge(s_d, 32)
        emit_mm_fm(b, name + "m", wc_dram, ysb, D, D, T, out_dram=a0_dram, bias_sb=bcsb)




def emit_combine(b, name, x_dram, a_dram, ftok_dram, out_dram):
    nc = b.nc
    SW = 512
    nst = T // SW
    ctx = ExitStack()
    with ctx:
        def csb(n, shape, dt):
            return ctx.enter_context(nc.sbuf_tensor(f"{name}_{n}", shape, dt))
        xs = csb("x", [P, DC, SW], F32)
        aa = csb("a", [P, DC, SW], F32)
        ff = csb("f", [P, DC, SW], BF16)
        s_d = b.sem(f"{name}_d")
        dv = 0
        for i in range(nst):
            with b.block() as block:
                @block.sync
                def _(sy):
                    nonlocal dv
                    sy.dma_start(xs[:], fmv(x_dram)[:, :, i * SW:(i + 1) * SW]).then_inc(s_d, 16)
                    for k in range(DC):
                        sy.dma_start(ff[:, k, :],
                                     ftok_dram[i * SW:(i + 1) * SW, k * P:(k + 1) * P],
                                     transpose=True).then_inc(s_d, 16)
                    dv += 16 * (DC + 1)
                    sy.wait_ge(s_d, dv)
                @block.gpsimd
                def _(g):
                    nonlocal dv
                    g.dma_start(aa[:], fmv(a_dram)[:, :, i * SW:(i + 1) * SW]).then_inc(s_d, 16)
                    dv += 16
                    g.wait_ge(s_d, dv)
            with b.block() as block:
                @block.vector
                def _(v):
                    for k in range(0, DC, 2):
                        v.tensor_tensor(out=xs[:, k, :], in0=xs[:, k, :], in1=aa[:, k, :], op=ALU.add)
                        v.tensor_tensor(out=xs[:, k, :], in0=xs[:, k, :], in1=ff[:, k, :], op=ALU.add)
                @block.gpsimd
                def _(g):
                    for k in range(1, DC, 2):
                        g.tensor_tensor(out=xs[:, k, :], in0=xs[:, k, :], in1=aa[:, k, :], op=ALU.add)
                        g.tensor_tensor(out=xs[:, k, :], in0=xs[:, k, :], in1=ff[:, k, :], op=ALU.add)
            with b.block() as block:
                @block.sync
                def _(sy):
                    nonlocal dv
                    sy.dma_start(fmv(out_dram)[:, :, i * SW:(i + 1) * SW], xs[:]).then_inc(s_d, 16)
                    dv += 16
                    sy.wait_ge(s_d, dv)




def _make_masks():
    m = np.zeros((128, 4, 512), np.uint8)
    for mm in range(4):
        for s in range(128):
            m[s, mm, 128 * mm + s:] = 1
    return m


def _prep_weights(inputs):
    w = {}
    w["wq"] = np.ascontiguousarray(inputs["L0_attn_Wq"], np.float32)
    w["wk"] = np.ascontiguousarray(inputs["L0_attn_Wk"], np.float32)
    w["wv"] = np.ascontiguousarray(inputs["L0_attn_Wv"], np.float32)
    w["wg0"] = np.ascontiguousarray(inputs["L0_attn_Wg"], np.float32)
    w["wc"] = np.ascontiguousarray(inputs["L0_attn_Wc"], np.float32)
    w["bc"] = np.ascontiguousarray(np.asarray(inputs["L0_attn_bc"], np.float32).reshape(DC, P).T)
    w["fg"] = np.ascontiguousarray(np.asarray(inputs["L0_ffn_Wg"], np.float32).T)
    w["fbias"] = np.asarray(inputs["L0_ffn_bias"], np.float32).reshape(1, 8)
    w["w1t"] = np.ascontiguousarray(np.asarray(inputs["L0_ffn_W1"], np.float32).transpose(0, 2, 1))
    w["w3t"] = np.ascontiguousarray(np.asarray(inputs["L0_ffn_W3"], np.float32).transpose(0, 2, 1))
    w["w2t"] = np.ascontiguousarray(np.asarray(inputs["L0_ffn_W2"], np.float32).transpose(0, 2, 1))
    w["wint"] = np.ascontiguousarray(np.asarray(inputs["L1_attn_in_proj_w"], np.float32).T)
    w["cw"] = np.ascontiguousarray(np.asarray(inputs["L1_attn_conv_w"], np.float32).reshape(34, P, 4).transpose(1, 0, 2))
    w["cb"] = np.ascontiguousarray(np.asarray(inputs["L1_attn_conv_b"], np.float32).reshape(34, P).T)
    w["a1A"] = np.asarray(inputs["L1_attn_A"], np.float32).reshape(64, 1)
    w["dtb"] = np.asarray(inputs["L1_attn_dt_bias"], np.float32).reshape(64, 1)
    w["a1D"] = np.asarray(inputs["L1_attn_D"], np.float32).reshape(1, 64)
    w["nw"] = np.asarray(inputs["L1_attn_norm_w"], np.float32).reshape(1, 4096)
    w["wout"] = np.ascontiguousarray(np.asarray(inputs["L1_attn_out_proj_w"], np.float32).T)
    w["rt"] = np.ascontiguousarray(np.asarray(inputs["L1_ffn_router_w"], np.float32).T)
    w["f1w1t"] = np.ascontiguousarray(np.asarray(inputs["L1_ffn_w1"], np.float32).T)
    w["f1b1"] = np.ascontiguousarray(np.asarray(inputs["L1_ffn_b1"], np.float32).reshape(64, P).T)
    w["f1w2t"] = np.ascontiguousarray(np.asarray(inputs["L1_ffn_w2"], np.float32).T)
    w["f1b2"] = np.asarray(inputs["L1_ffn_b2"], np.float32).reshape(1, D)
    w["masks"] = _make_masks()
    w["eye"] = np.eye(P, dtype=np.float32)
    w["iota"] = np.arange(8, dtype=np.float32).reshape(1, 8)
    return w


def _launch(builder_fn, in_maps):
    b = B()
    builder_fn(b)
    nc = b.finish()
    res = run_bass_kernel_spmd(nc, in_maps, core_ids=list(range(NCORE)))
    return res.results


USE_FUSED = True


def kernel(**inputs):
    if USE_FUSED:
        return kernel_fused(inputs)
    return kernel_multi(inputs)


def kernel_multi(inputs):
    w = _prep_weights(inputs)
    x = np.asarray(inputs["L0_attn_hidden_states"], np.float32)
    xfm_c = [np.ascontiguousarray(
        np.concatenate([x[2 * cc].T, x[2 * cc + 1].T], axis=1)) for cc in range(NCORE)]

    st = {}

    def pc(key):
        return st[key]

    def b_ln0(b):
        xd = b.dram("x", [D, T], F32, kind="ExternalInput")
        xnd = b.dram("xn", [D, T], BF16, kind="ExternalOutput")
        b.init_consts()
        emit_ln(b, "ln0", xd, xnd)
    r = _launch(b_ln0, [{"x": xfm_c[cc]} for cc in range(NCORE)])
    st["xn0"] = [r[cc]["xn"] for cc in range(NCORE)]

    def b_at0(b):
        xn = b.dram("xn", [D, T], BF16, kind="ExternalInput")
        wq = b.dram("wq", [D, D], F32, kind="ExternalInput")
        wk = b.dram("wk", [D, 1024], F32, kind="ExternalInput")
        wv = b.dram("wv", [D, 1024], F32, kind="ExternalInput")
        wg = b.dram("wg", [D, 8], F32, kind="ExternalInput")
        masks = b.dram("masks", [P, 4, 512], mybir.dt.uint8, kind="ExternalInput")
        eye = b.dram("eye", [P, P], F32, kind="ExternalInput")
        yhat = b.dram("yhat", [D, T], BF16, kind="ExternalOutput")
        b.init_consts()
        emit_attn0(b, "at0", xn, wq, wk, wv, wg, masks, eye, yhat)
    r = _launch(b_at0, [{"xn": st["xn0"][cc], "wq": w["wq"], "wk": w["wk"],
                         "wv": w["wv"], "wg": w["wg0"], "masks": w["masks"],
                         "eye": w["eye"]} for cc in range(NCORE)])
    st["yhat"] = [r[cc]["yhat"] for cc in range(NCORE)]

    def b_wc(b):
        yhat = b.dram("yhat", [D, T], BF16, kind="ExternalInput")
        wc = b.dram("wc", [D, D], F32, kind="ExternalInput")
        bc = b.dram("bc", [P, DC], F32, kind="ExternalInput")
        a0 = b.dram("a0", [D, T], BF16, kind="ExternalOutput")
        b.init_consts()
        ysb = b.sb("ysb", [P, DC, T], BF16)
        bcsb = b.sb("bcsb", [P, DC], F32)
        s_d = b.sem("wcld")
        with b.block() as block:
            @block.sync
            def _(sy):
                sy.dma_start(ysb[:], fmv(yhat)).then_inc(s_d, 16)
                sy.dma_start(bcsb[:], bc).then_inc(s_d, 16)
                sy.wait_ge(s_d, 32)
        emit_mm_fm(b, "wc", wc, ysb, D, D, T, out_dram=a0, bias_sb=bcsb)
    r = _launch(b_wc, [{"yhat": st["yhat"][cc], "wc": w["wc"], "bc": w["bc"]}
                       for cc in range(NCORE)])
    st["a0"] = [r[cc]["a0"] for cc in range(NCORE)]

    def b_f0(b):
        xn = b.dram("xn", [D, T], BF16, kind="ExternalInput")
        wg = b.dram("wg", [D, 8], F32, kind="ExternalInput")
        bias = b.dram("bias", [1, 8], F32, kind="ExternalInput")
        iota = b.dram("iota", [1, 8], F32, kind="ExternalInput")
        w1t = b.dram("w1t", [8, D, 1024], F32, kind="ExternalInput")
        w3t = b.dram("w3t", [8, D, 1024], F32, kind="ExternalInput")
        w2t = b.dram("w2t", [8, 1024, D], F32, kind="ExternalInput")
        out = b.dram("out", [T, D], BF16, kind="ExternalOutput")
        b.init_consts()
        emit_ffn0(b, "f0", xn, wg, bias, iota, w1t, w3t, w2t, out)
    r = _launch(b_f0, [{"xn": st["xn0"][cc], "wg": w["fg"], "bias": w["fbias"],
                        "iota": w["iota"], "w1t": w["w1t"], "w3t": w["w3t"],
                        "w2t": w["w2t"]} for cc in range(NCORE)])
    st["f0"] = [r[cc]["out"] for cc in range(NCORE)]

    def b_c1(b):
        xd = b.dram("x", [D, T], F32, kind="ExternalInput")
        a0 = b.dram("a0", [D, T], BF16, kind="ExternalInput")
        f0 = b.dram("f0", [T, D], BF16, kind="ExternalInput")
        x1 = b.dram("x1", [D, T], F32, kind="ExternalOutput")
        xn1 = b.dram("xn1", [D, T], BF16, kind="ExternalOutput")
        b.init_consts()
        emit_ln(b, "c1", x1, xn1, x_store=(xd, a0, f0))
    r = _launch(b_c1, [{"x": xfm_c[cc], "a0": st["a0"][cc], "f0": st["f0"][cc]}
                       for cc in range(NCORE)])
    st["x1"] = [r[cc]["x1"] for cc in range(NCORE)]
    st["xn1"] = [r[cc]["xn1"] for cc in range(NCORE)]

    def b_ip(b):
        xn = b.dram("xn", [D, T], BF16, kind="ExternalInput")
        wint = b.dram("wint", [D, 8512], F32, kind="ExternalInput")
        gate = b.dram("gate", [T, 4096], BF16, kind="ExternalOutput")
        xbc = b.dram("xbc", [4352, T], BF16, kind="ExternalOutput")
        dtr = b.dram("dtr", [P, T], F32, kind="ExternalOutput")
        b.init_consts()
        emit_inproj(b, "ip", xn, wint, gate, xbc, dtr)
    r = _launch(b_ip, [{"xn": st["xn1"][cc], "wint": w["wint"]} for cc in range(NCORE)])
    st["gate"] = [r[cc]["gate"] for cc in range(NCORE)]
    st["xbc"] = [r[cc]["xbc"] for cc in range(NCORE)]
    st["dtr"] = [r[cc]["dtr"] for cc in range(NCORE)]

    def b_cv(b):
        xbc = b.dram("xbc", [4352, T], BF16, kind="ExternalInput")
        cw = b.dram("cw", [P, 34, 4], F32, kind="ExternalInput")
        cb = b.dram("cb", [P, 34], F32, kind="ExternalInput")
        eye = b.dram("eye", [P, P], F32, kind="ExternalInput")
        xtok = b.dram("xtok", [T, 4096], BF16, kind="ExternalOutput")
        bcf = b.dram("bcf", [256, T], BF16, kind="ExternalOutput")
        b.init_consts()
        emit_conv(b, "cv", xbc, cw, cb, eye, xtok, bcf)
    r = _launch(b_cv, [{"xbc": st["xbc"][cc], "cw": w["cw"], "cb": w["cb"],
                        "eye": w["eye"]} for cc in range(NCORE)])
    st["xtok"] = [r[cc]["xtok"] for cc in range(NCORE)]
    st["bcf"] = [r[cc]["bcf"] for cc in range(NCORE)]

    def b_ss(b):
        xtok = b.dram("xtok", [T, 4096], BF16, kind="ExternalInput")
        bcf = b.dram("bcf", [256, T], BF16, kind="ExternalInput")
        dtr = b.dram("dtr", [P, T], F32, kind="ExternalInput")
        a = b.dram("a", [64, 1], F32, kind="ExternalInput")
        dtb = b.dram("dtb", [64, 1], F32, kind="ExternalInput")
        dv = b.dram("dv", [1, 64], F32, kind="ExternalInput")
        masks = b.dram("masks", [P, 4, 512], mybir.dt.uint8, kind="ExternalInput")
        eye = b.dram("eye", [P, P], F32, kind="ExternalInput")
        ytok = b.dram("ytok", [T, 4096], BF16, kind="ExternalOutput")
        b.init_consts()
        emit_ssm2(b, "ss", xtok, bcf, dtr, a, dtb, dv, masks, eye, ytok)
    r = _launch(b_ss, [{"xtok": st["xtok"][cc], "bcf": st["bcf"][cc],
                        "dtr": st["dtr"][cc], "a": w["a1A"], "dtb": w["dtb"],
                        "dv": w["a1D"], "masks": w["masks"], "eye": w["eye"]}
                       for cc in range(NCORE)])
    st["ytok"] = [r[cc]["ytok"] for cc in range(NCORE)]

    def b_rm(b):
        ytok = b.dram("ytok", [T, 4096], BF16, kind="ExternalInput")
        gate = b.dram("gate", [T, 4096], BF16, kind="ExternalInput")
        nw = b.dram("nw", [1, 4096], F32, kind="ExternalInput")
        gn = b.dram("gn", [T, 4096], BF16, kind="ExternalOutput")
        b.init_consts()
        emit_rms(b, "rm", ytok, gate, nw, gn)
    r = _launch(b_rm, [{"ytok": st["ytok"][cc], "gate": st["gate"][cc],
                        "nw": w["nw"]} for cc in range(NCORE)])
    st["gn"] = [r[cc]["gn"] for cc in range(NCORE)]

    def b_op(b):
        gn = b.dram("gn", [T, 4096], BF16, kind="ExternalInput")
        wout = b.dram("wout", [4096, D], F32, kind="ExternalInput")
        a1 = b.dram("a1", [D, T], BF16, kind="ExternalOutput")
        b.init_consts()
        emit_outproj(b, "op", gn, wout, a1)
    r = _launch(b_op, [{"gn": st["gn"][cc], "wout": w["wout"]} for cc in range(NCORE)])
    st["a1"] = [r[cc]["a1"] for cc in range(NCORE)]

    def b_f1(b):
        xn = b.dram("xn", [D, T], BF16, kind="ExternalInput")
        rt = b.dram("rt", [D, 8], F32, kind="ExternalInput")
        w1t = b.dram("w1t", [D, 8192], F32, kind="ExternalInput")
        b1 = b.dram("b1", [P, 64], F32, kind="ExternalInput")
        w2t = b.dram("w2t", [8192, D], F32, kind="ExternalInput")
        b2 = b.dram("b2", [1, D], F32, kind="ExternalInput")
        pm = b.dram("pm", [8, 1], F32, kind="ExternalInput")
        iota = b.dram("iota", [1, 8], F32, kind="ExternalInput")
        eye = b.dram("eye", [P, P], F32, kind="ExternalInput")
        h = b.dram("hbuf", [8192, T], BF16)
        out = b.dram("out", [T, D], BF16, kind="ExternalOutput")
        b.init_consts()
        emit_ffn1(b, "f1", xn, rt, w1t, b1, w2t, b2, pm, iota, eye, h, out)
    ims = []
    for cc in range(NCORE):
        pmv = np.zeros((8, 1), np.float32)
        pmv[:cc] = 1.0
        ims.append({"xn": st["xn1"][cc], "rt": w["rt"], "w1t": w["f1w1t"],
                    "b1": w["f1b1"], "w2t": w["f1w2t"], "b2": w["f1b2"],
                    "pm": pmv, "iota": w["iota"], "eye": w["eye"]})
    r = _launch(b_f1, ims)
    st["f1"] = [r[cc]["out"] for cc in range(NCORE)]

    def b_fc(b):
        x1 = b.dram("x1", [D, T], F32, kind="ExternalInput")
        a1 = b.dram("a1", [D, T], BF16, kind="ExternalInput")
        f1 = b.dram("f1", [T, D], BF16, kind="ExternalInput")
        out = b.dram("out", [D, T], F32, kind="ExternalOutput")
        b.init_consts()
        emit_combine(b, "fc", x1, a1, f1, out)
    r = _launch(b_fc, [{"x1": st["x1"][cc], "a1": st["a1"][cc], "f1": st["f1"][cc]}
                       for cc in range(NCORE)])

    out = np.empty((16, TSEQ, D), np.float32)
    for cc in range(NCORE):
        o = np.asarray(r[cc]["out"])
        out[2 * cc] = o[:, 0:TSEQ].T
        out[2 * cc + 1] = o[:, TSEQ:].T
    return out




def build_fused(b, dbg=False):
    xd = b.dram("x", [D, T], F32, kind="ExternalInput")
    wq = b.dram("wq", [D, D], F32, kind="ExternalInput")
    wk = b.dram("wk", [D, 1024], F32, kind="ExternalInput")
    wv = b.dram("wv", [D, 1024], F32, kind="ExternalInput")
    wg0 = b.dram("wg0", [D, 8], F32, kind="ExternalInput")
    wc = b.dram("wc", [D, D], F32, kind="ExternalInput")
    bc = b.dram("bc", [P, DC], F32, kind="ExternalInput")
    fg = b.dram("fg", [D, 8], F32, kind="ExternalInput")
    fbias = b.dram("fbias", [1, 8], F32, kind="ExternalInput")
    w1t = b.dram("w1t", [8, D, 1024], F32, kind="ExternalInput")
    w3t = b.dram("w3t", [8, D, 1024], F32, kind="ExternalInput")
    w2t = b.dram("w2t", [8, 1024, D], F32, kind="ExternalInput")
    wint = b.dram("wint", [D, 8512], F32, kind="ExternalInput")
    cw = b.dram("cw", [P, 34, 4], F32, kind="ExternalInput")
    cb = b.dram("cb", [P, 34], F32, kind="ExternalInput")
    a1A = b.dram("a1A", [64, 1], F32, kind="ExternalInput")
    dtb = b.dram("dtb", [64, 1], F32, kind="ExternalInput")
    a1D = b.dram("a1D", [1, 64], F32, kind="ExternalInput")
    nw = b.dram("nw", [1, 4096], F32, kind="ExternalInput")
    wout = b.dram("wout", [4096, D], F32, kind="ExternalInput")
    rt = b.dram("rt", [D, 8], F32, kind="ExternalInput")
    f1w1t = b.dram("f1w1t", [D, 8192], F32, kind="ExternalInput")
    f1b1 = b.dram("f1b1", [P, 64], F32, kind="ExternalInput")
    f1w2t = b.dram("f1w2t", [8192, D], F32, kind="ExternalInput")
    f1b2 = b.dram("f1b2", [1, D], F32, kind="ExternalInput")
    pm = b.dram("pm", [8, 1], F32, kind="ExternalInput")
    masks = b.dram("masks", [P, 4, 512], mybir.dt.uint8, kind="ExternalInput")
    eye = b.dram("eye", [P, P], F32, kind="ExternalInput")
    iota = b.dram("iota", [1, 8], F32, kind="ExternalInput")
    out = b.dram("out", [D, T], F32, kind="ExternalOutput")

    kd = "ExternalOutput" if dbg else None
    xn0 = b.dram("i_xn0", [D, T], BF16, kind=kd)
    yhat = b.dram("i_yhat", [D, T], BF16, kind=kd)
    a0 = b.dram("i_a0", [D, T], BF16, kind=kd)
    f0 = b.dram("i_f0", [T, D], BF16, kind=kd)
    x1 = b.dram("i_x1", [D, T], F32, kind=kd)
    xn1 = b.dram("i_xn1", [D, T], BF16, kind=kd)
    gate = b.dram("i_gate", [T, 4096], BF16, kind=kd)
    xbc = b.dram("i_xbc", [4352, T], BF16, kind=kd)
    dtr = b.dram("i_dtr", [P, T], F32, kind=kd)
    xtok = b.dram("i_xtok", [T, 4096], BF16, kind=kd)
    bcf = b.dram("i_bcf", [256, T], BF16, kind=kd)
    ytok = b.dram("i_ytok", [T, 4096], BF16, kind=kd)
    gn = b.dram("i_gn", [T, 4096], BF16, kind=kd)
    a1 = b.dram("i_a1", [D, T], BF16, kind=kd)
    f1 = b.dram("i_f1", [T, D], BF16, kind=kd)
    hbuf = b.dram("i_hbuf", [8192, T], BF16)

    b.init_consts()
    emit_ln(b, "ln0", xd, xn0)
    emit_attn0(b, "at0", xn0, wq, wk, wv, wg0, masks, eye, yhat)
    emit_wc(b, "wc", yhat, wc, bc, a0)
    emit_ffn0(b, "f0", xn0, fg, fbias, iota, w1t, w3t, w2t, f0)
    emit_ln(b, "c1", x1, xn1, x_store=(xd, a0, f0))
    emit_inproj(b, "ip", xn1, wint, gate, xbc, dtr)
    emit_conv(b, "cv", xbc, cw, cb, eye, xtok, bcf)
    emit_ssm2(b, "ss", xtok, bcf, dtr, a1A, dtb, a1D, masks, eye, ytok)
    emit_rms(b, "rm", ytok, gate, nw, gn)
    emit_outproj(b, "op", gn, wout, a1)
    emit_ffn1(b, "f1", xn1, rt, f1w1t, f1b1, f1w2t, f1b2, pm, iota, eye, hbuf, f1)
    emit_combine(b, "fc", x1, a1, f1, out)


def kernel_fused(inputs):
    w = _prep_weights(inputs)
    x = np.asarray(inputs["L0_attn_hidden_states"], np.float32)
    b = B()
    build_fused(b)
    nc = b.finish()
    ims = []
    for cc in range(NCORE):
        pmv = np.zeros((8, 1), np.float32)
        pmv[:cc] = 1.0
        ims.append({
            "x": np.ascontiguousarray(np.concatenate([x[2 * cc].T, x[2 * cc + 1].T], axis=1)),
            "wq": w["wq"], "wk": w["wk"], "wv": w["wv"], "wg0": w["wg0"],
            "wc": w["wc"], "bc": w["bc"], "fg": w["fg"], "fbias": w["fbias"],
            "w1t": w["w1t"], "w3t": w["w3t"], "w2t": w["w2t"], "wint": w["wint"],
            "cw": w["cw"], "cb": w["cb"], "a1A": w["a1A"], "dtb": w["dtb"],
            "a1D": w["a1D"], "nw": w["nw"], "wout": w["wout"], "rt": w["rt"],
            "f1w1t": w["f1w1t"], "f1b1": w["f1b1"], "f1w2t": w["f1w2t"],
            "f1b2": w["f1b2"], "pm": pmv, "masks": w["masks"], "eye": w["eye"],
            "iota": w["iota"]})
    res = run_bass_kernel_spmd(nc, ims, core_ids=list(range(NCORE)))
    out = np.empty((16, TSEQ, D), np.float32)
    for cc in range(NCORE):
        o = np.asarray(res.results[cc]["out"])
        out[2 * cc] = o[:, 0:TSEQ].T
        out[2 * cc + 1] = o[:, TSEQ:].T
    return out
